# version 01
# baseline (speedup 1.0000x reference)
import numpy as np
import ml_dtypes
import concourse.bass as bass
import concourse.mybir as mybir
from concourse.bass_utils import run_bass_kernel_spmd

F32 = mybir.dt.float32
BF16 = mybir.dt.bfloat16
AF = mybir.ActivationFunctionType
ALU = mybir.AluOpType
AXL = mybir.AxisListType
BF_NP = ml_dtypes.bfloat16

B, T, D = 8, 2048, 1024
DBG_SINK = {}
NCHUNK = D // 128
NTSL = T // 512
EPS = 1e-5



class Prog:

    def __init__(self, nc):
        self.nc = nc
        self.ops = []
        self.last_write = {}
        self.readers = {}

    def op(self, engine, emit, reads=(), writes=(), dma=False, tag="", slow=False,
           width=0):
        idx = len(self.ops)
        deps = set()
        for r in reads:
            w = self.last_write.get(r)
            if w is not None:
                deps.add(w)
        for r in writes:
            w = self.last_write.get(r)
            if w is not None:
                deps.add(w)
            for rd in self.readers.get(r, ()):
                deps.add(rd)
        for r in writes:
            self.last_write[r] = idx
            self.readers[r] = []
        for r in reads:
            self.readers.setdefault(r, []).append(idx)
        self.ops.append(dict(engine=engine, emit=emit, deps=sorted(deps),
                             dma=dma, tag=tag, idx=idx, slow=slow, width=width))
        return idx

    def build(self):
        nc = self.nc
        ops = self.ops
        n = len(ops)

        def domain(o):
            return ("dma_" + o["engine"]) if o["dma"] else o["engine"]

        def skip_edge(po, o):
            if po["dma"] or o["dma"] or po["engine"] != o["engine"]:
                return False
            if po["engine"] == "tensor":
                return True
            return (not po["slow"]) and po["width"] >= 512 and o["width"] >= 512

        needed = [False] * n
        for o in ops:
            for d in o["deps"]:
                if skip_edge(ops[d], o):
                    continue
                needed[d] = True
        for o in ops:
            if o["dma"]:
                needed[o["idx"]] = True
        counters = {}
        val = [0] * n
        for o in ops:
            if needed[o["idx"]]:
                d = domain(o)
                counters[d] = counters.get(d, 0) + (16 if o["dma"] else 1)
                val[o["idx"]] = counters[d]
        sem_ctx = {}
        sems = []
        for d in sorted(counters.keys()):
            cm = nc.semaphore("sem_" + d)
            sem_ctx[d] = cm.__enter__()
            sems.append(cm)
        waited = {}
        plans = {}
        for o in ops:
            plans.setdefault(o["engine"], [])
        for o in ops:
            wl = {}
            for d in o["deps"]:
                po = ops[d]
                if skip_edge(po, o):
                    continue
                pd = domain(po)
                wl[pd] = max(wl.get(pd, 0), val[po["idx"]])
            eng = o["engine"]
            final = []
            for pd, v in sorted(wl.items()):
                kk = (eng, pd)
                if waited.get(kk, 0) >= v:
                    continue
                waited[kk] = v
                final.append((pd, v))
            plans[eng].append((o, final, val[o["idx"]] if needed[o["idx"]] else None))
        term = {}
        for d, c in counters.items():
            if d.startswith("dma_"):
                term[d[4:]] = (d, c)
        for ch in list(term):
            if ch not in plans:
                plans[ch] = []

        self.trace = []
        for eng in plans:
            for (o, waits, myval) in plans[eng]:
                self.trace.append((o["idx"], eng, o["tag"], waits,
                                   myval, ("dma" if o["dma"] else "")))
        self.trace.sort()
        block_cm = nc.Block()
        block = block_cm.__enter__()
        for eng in plans:
            def body(e, eng=eng):
                for (o, waits, myval) in plans[eng]:
                    for (pd, v) in waits:
                        e.wait_ge(sem_ctx[pd], v)
                    inst = o["emit"](e)
                    if myval is not None:
                        assert inst is not None, f"op {o['tag']} needs inc, emitted None"
                        inst.then_inc(sem_ctx[("dma_" + eng) if o["dma"] else eng],
                                      16 if o["dma"] else 1)
                if eng in term:
                    pd, c = term[eng]
                    e.wait_ge(sem_ctx[pd], c)
            getattr(block, eng)(body)
        block_cm.__exit__(None, None, None)
        for cm in sems:
            cm.__exit__(None, None, None)



class Host:
    def __init__(self):
        self.arrays = {}

    def add(self, name, arr):
        self.arrays[name] = np.ascontiguousarray(arr)
        return name


class ParamCols:

    def __init__(self):
        self.cols = []

    def add(self, vec):
        v = np.zeros(128, np.float32)
        v[: len(vec)] = np.asarray(vec, np.float32).reshape(-1)
        self.cols.append(v)
        return len(self.cols) - 1

    def add_vecD(self, vecD):
        base = len(self.cols)
        for c in range(NCHUNK):
            self.add(np.asarray(vecD, np.float32).reshape(-1)[c * 128:(c + 1) * 128])
        return base

    def array(self):
        if not self.cols:
            return np.zeros((128, 1), np.float32)
        return np.stack(self.cols, axis=1).astype(np.float32)


def pack_bf16_as_f32(bfarr):
    raw = np.ascontiguousarray(bfarr.astype(BF_NP)).view(np.uint16)
    lo = raw[:, 0::2].astype(np.uint32)
    hi = raw[:, 1::2].astype(np.uint32)
    return ((hi << 16) | lo).view(np.float32)


def pack_lhsT(w_in_out, ow=128):
    w = np.asarray(w_in_out, np.float32)
    n_in, n_out = w.shape
    kp = min(n_in, 128)
    assert n_in % kp == 0 and n_out % ow == 0
    n_ic, n_ob = n_in // kp, n_out // ow
    a = w.reshape(n_ic, kp, n_ob, ow)
    return np.ascontiguousarray(a.transpose(2, 1, 0, 3).reshape(n_ob, kp, n_ic * ow))


CONSTF = [1e-5, 1e-6, 0.5, -0.5, 1.0, -1.0, 20.0, -20.0, 2.0]


GAMMA = 0.96875


def make_consts():
    bf = np.zeros((128, 256), BF_NP)
    bf[:, 0:128] = BF_NP(1.0)
    bf[:, 128:256] = np.eye(128, dtype=BF_NP)
    out = np.zeros((128, 1536), np.float32)
    out[:, 0:128] = 1.0
    out[:, 128:256] = pack_bf16_as_f32(bf)
    for i, v in enumerate(CONSTF):
        out[:, 256 + i] = v
    ii = np.arange(128)
    out[:, 320:448] = np.where(ii[None, :] <= ii[:, None], 0.0, -2000.0).astype(np.float32)
    sw = np.zeros((128, 128), BF_NP)
    sw[ii, ii ^ 1] = BF_NP(1.0)
    out[:, 448:512] = pack_bf16_as_f32(sw)
    j = np.arange(128)
    out = np.concatenate([out, np.zeros((128, 128), np.float32)], axis=1)
    out[:, 512:640] = (GAMMA ** j)[None, :].astype(np.float32)
    out[:, 640:768] = (GAMMA ** (-j.astype(np.float64)))[None, :].astype(np.float32)
    out[:, 768:896] = (ii[None, :] <= ii[:, None]).astype(np.float32)
    return np.ascontiguousarray(out[:, 0:896])



ARENAS = [("AX", 16384), ("AD", 16384), ("AC", 8192), ("AB", 4096), ("AW", 6144)]
ARENA_REG = {"AX": 2048, "AD": 1024, "AC": 1024, "AB": 512, "AW": 512}


class V:
    __slots__ = ("ap", "regs")

    def __init__(self, ap, regs):
        self.ap = ap
        self.regs = regs


class Ctx:
    def __init__(self, nc, prog):
        self.nc = nc
        self.p = prog
        self.stack = []
        self.sb = {}
        for nm, cols in ARENAS:
            cm = nc.sbuf_tensor(nm, [128, cols], F32)
            self.stack.append(cm)
            self.sb[nm] = cm.__enter__()
        self.pb = []
        for i in range(8):
            cm = nc.psum_tensor(f"pb{i}", [128, 512], F32)
            self.stack.append(cm)
            self.pb.append(cm.__enter__())
        cm = nc.sbuf_tensor("consts", [128, 896], F32)
        self.stack.append(cm)
        self.consts = cm.__enter__()
        for i, v in enumerate(CONSTF):
            nc.const_aps.register(F32, v, self.consts.ap()[:, 256 + i:257 + i])
        cm = nc.sbuf_tensor("params", [128, 128], F32)
        self.stack.append(cm)
        self.params_t = cm.__enter__()
        cm = nc.sbuf_tensor("SC", [128, 64], F32)
        self.stack.append(cm)
        self.sc_t = cm.__enter__()
        cm = nc.sbuf_tensor("BP", [128, 512], F32)
        self.stack.append(cm)
        self.bp_t = cm.__enter__()
        self.rr = 0

    def bpcol(self, col):
        return self.bp_t.ap()[:, col:col + 1]

    def load_bp(self, d_bp, ncols, tag="ldbp"):
        self.p.op("sync", lambda e, d=d_bp, n=ncols: e.dma_start(
            out=self.bp_t.ap()[:, 0:n], in_=d), writes=[("BP",)], dma=True, tag=tag)

    def scol(self, col, n=1):
        return V(self.sc_t.ap()[:, col:col + n], [("SC", col)])

    def close(self):
        for cm in reversed(self.stack):
            cm.__exit__(None, None, None)

    def f32(self, name, col0, ncols):
        g = ARENA_REG[name]
        regs = [(name, i) for i in range(col0 // g, (col0 + ncols + g - 1) // g)]
        return V(self.sb[name].ap()[:, col0:col0 + ncols], regs)

    def bf(self, name, bcol0, nbcols):
        g = ARENA_REG[name]
        c0, c1 = bcol0 // 2, (bcol0 + nbcols + 1) // 2
        regs = [(name, i) for i in range(c0 // g, (c1 + g - 1) // g)]
        return V(self.sb[name].ap().bitcast(BF16)[:, bcol0:bcol0 + nbcols], regs)

    def psum(self, i, ncols=512):
        return V(self.pb[i].ap()[:, 0:ncols], [(f"pb{i}",)])

    def psum_bf(self, i, bcol0, nbcols):
        return V(self.pb[i].ap().bitcast(BF16)[:, bcol0:bcol0 + nbcols], [(f"pb{i}",)])

    def pcol(self, col):
        return self.params_t.ap()[:, col:col + 1]

    @property
    def ones_bf(self):
        return self.consts.ap().bitcast(BF16)[:, 256:384]

    @property
    def ident_bf(self):
        return self.consts.ap().bitcast(BF16)[:, 384:512]

    @property
    def causal_mask(self):
        return self.consts.ap()[:, 320:448]

    @property
    def swap_bf(self):
        return self.consts.ap().bitcast(BF16)[:, 896:1024]

    def gamma_q(self, w):
        assert w <= 128
        return self.consts.ap()[:, 512:512 + w]

    def gamma_k(self, w):
        assert w <= 128
        return self.consts.ap()[:, 640:640 + w]

    @property
    def caus01(self):
        return self.consts.ap()[:, 768:896]

    def alt(self, engines=("vector", "gpsimd")):
        self.rr += 1
        return engines[self.rr % len(engines)]

    def dma(self, dst, src, ch="sync", tag="dma"):
        dv, dr = (dst.ap, dst.regs) if isinstance(dst, V) else (dst, [("dram", dst.tensor.name)])
        sv, sr = (src.ap, src.regs) if isinstance(src, V) else (src, [("dram", src.tensor.name)])
        return self.p.op(ch, lambda e, a=dv, b=sv: e.dma_start(out=a, in_=b),
                         reads=sr, writes=dr, dma=True, tag=tag)

    def tt(self, eng, out, a, b, op, tag="tt"):
        return self.p.op(eng, lambda e, o=out.ap, x=a.ap, y=b.ap: e.tensor_tensor(o, x, y, op),
                         reads=a.regs + b.regs, writes=out.regs, tag=tag,
                         width=out.ap.shape[-1])

    @staticmethod
    def _sc(s):
        if s is None or np.isscalar(s):
            return s, []
        if isinstance(s, V):
            return s.ap, list(s.regs)
        return s, [("params",)]

    def ts(self, eng, out, a, s1, s2, op0, op1=ALU.bypass, accum=None, tag="ts"):
        s1v, r1 = self._sc(s1)
        s2v, r2 = self._sc(s2)
        wr = out.regs + (accum.regs if accum is not None else [])
        return self.p.op(eng, lambda e, o=out.ap, x=a.ap: e.tensor_scalar(
            o, x, s1v, s2v, op0, op1, accum_out=(accum.ap if accum is not None else None)),
            reads=a.regs + r1 + r2, writes=wr, tag=tag, width=out.ap.shape[-1])

    def stt(self, eng, out, a, s, b, op0, op1, tag="stt"):
        if eng == "gpsimd":
            eng = "vector"
        sv, r1 = self._sc(s)
        return self.p.op(eng, lambda e, o=out.ap, x=a.ap, y=b.ap: e.scalar_tensor_tensor(
            o, x, sv, y, op0, op1), reads=a.regs + b.regs + r1, writes=out.regs,
            tag=tag, width=out.ap.shape[-1])

    def act(self, out, a, func, bias=0.0, scale=1.0, accum=None, tag="act"):
        bv, r1 = self._sc(bias)
        sv, r2 = self._sc(scale)
        extra = [("consts",)] + r1 + r2
        wr = out.regs + (accum.regs if accum is not None else [])
        return self.p.op("scalar", lambda e, o=out.ap, x=a.ap: e.activation(
            o, x, func, bias=bv, scale=sv,
            accum_out=(accum.ap if accum is not None else None)),
            reads=a.regs + extra, writes=wr, tag=tag, width=out.ap.shape[-1])

    def copy(self, eng, out, a, tag="copy"):
        if eng == "scalar":
            return self.p.op(eng, lambda e, o=out.ap, x=a.ap: e.copy(o, x),
                             reads=a.regs, writes=out.regs, tag=tag,
                             width=out.ap.shape[-1])
        return self.p.op(eng, lambda e, o=out.ap, x=a.ap: e.tensor_copy(o, x),
                         reads=a.regs, writes=out.regs, tag=tag,
                         width=out.ap.shape[-1])

    def mm(self, psum, lhsT, rhs, start, stop, tag="mm"):
        return self.p.op("tensor", lambda e, o=psum.ap, l=lhsT.ap, r=rhs.ap:
                         e.matmul(o, l, r, start=start, stop=stop),
                         reads=lhsT.regs + rhs.regs, writes=psum.regs, tag=tag,
                         width=psum.ap.shape[-1])

    def transpose(self, psum, src, tag="tr"):
        cregs = [("consts",)]
        return self.p.op("tensor", lambda e, o=psum.ap, s=src.ap:
                         e.transpose(o, s, self.ident_bf),
                         reads=src.regs + cregs, writes=psum.regs, tag=tag)

    def reduce(self, eng, out, a, op, absval=False, tag="red"):
        return self.p.op(eng, lambda e, o=out.ap, x=a.ap: e.tensor_reduce(
            o, x, AXL.X, op, apply_absolute_value=absval),
            reads=a.regs, writes=out.regs, tag=tag, slow=True)

    def recip(self, out, a, tag="recip"):
        return self.p.op("vector", lambda e, o=out.ap, x=a.ap: e.reciprocal(o, x),
                         reads=a.regs, writes=out.regs, tag=tag, slow=True)

    def scan(self, out, d0, d1, init, op0, op1, tag="scan"):
        return self.p.op("vector", lambda e, o=out.ap, a=d0.ap, b=d1.ap:
                         e.tensor_tensor_scan(o, a, b, init, op0, op1),
                         reads=d0.regs + d1.regs, writes=out.regs, tag=tag, slow=True)

    def memset(self, eng, out, v, tag="memset"):
        return self.p.op(eng, lambda e, o=out.ap: e.memset(o, v),
                         reads=[], writes=out.regs, tag=tag, slow=True)

    def vconst(self, v):
        return V(v, [("params",)])


def dram_v(ap):
    return V(ap, [("dram", ap.tensor.name)])



def x_chunks(k):
    return [k.f32("AX", c * 2048, 2048) for c in range(NCHUNK)]


def emit_load_x(k, d_x):
    for c in range(NCHUNK):
        dst = k.f32("AX", c * 2048, 2048)
        k.dma(dst, dram_v(d_x[c * 128:(c + 1) * 128, :]), tag=f"ldx{c}")


def emit_store_x(k, d_y, ch="gpsimd"):
    for c in range(NCHUNK):
        src = k.f32("AX", c * 2048, 2048)
        k.dma(dram_v(d_y[c * 128:(c + 1) * 128, :]), src, ch=ch, tag=f"stx{c}")


def emit_ln(k, src, out_bf_base=None, out_f32_base=None, affine=None,
            xbf_base=("AC", 0), tag="ln"):
    xbf = []
    for c in range(NCHUNK):
        dst = k.bf(xbf_base[0], xbf_base[1] + c * 2048, 2048)
        k.copy(k.alt(), dst, src[c], tag=f"{tag}_xbf{c}")
        xbf.append(dst)
    for c in range(NCHUNK):
        sq = k.bf("AW", 8192 + (c % 2) * 2048, 2048)
        k.act(sq, src[c], AF.Square, tag=f"{tag}_sq{c}")
        for tsl in range(NTSL):
            pbv = k.psum(4 + tsl)
            k.p.op("tensor", lambda e, o=pbv.ap, s=sq.ap, t=tsl, st=(c == 0), sp=(c == NCHUNK - 1):
                   e.matmul(o, k.ones_bf, s[:, t * 512:(t + 1) * 512], start=st, stop=sp),
                   reads=sq.regs + [("consts",)], writes=pbv.regs, tag=f"{tag}_mmq{c}_{tsl}")
    for c in range(NCHUNK):
        for tsl in range(NTSL):
            pbv = k.psum(tsl)
            k.p.op("tensor", lambda e, o=pbv.ap, s=xbf[c].ap, t=tsl, st=(c == 0), sp=(c == NCHUNK - 1):
                   e.matmul(o, k.ones_bf, s[:, t * 512:(t + 1) * 512], start=st, stop=sp),
                   reads=xbf[c].regs + [("consts",)], writes=pbv.regs, tag=f"{tag}_mms{c}_{tsl}")
    mu = k.f32("AB", 0, 2048)
    rstd = k.f32("AB", 2048, 2048)
    for tsl in range(NTSL):
        musl = k.f32("AB", tsl * 512, 512)
        rsl = k.f32("AB", 2048 + tsl * 512, 512)
        k.ts("vector", musl, k.psum(tsl), 1.0 / D, None, ALU.mult, tag=f"{tag}_mu{tsl}")
        k.act(rsl, musl, AF.Square, tag=f"{tag}_mu2_{tsl}")
        k.stt("vector", rsl, k.psum(4 + tsl), 1.0 / D, rsl, ALU.mult, ALU.subtract,
              tag=f"{tag}_var{tsl}")
        k.act(rsl, rsl, AF.Ln, bias=EPS, tag=f"{tag}_lnv{tsl}")
        k.act(rsl, rsl, AF.Exp, scale=-0.5, tag=f"{tag}_rstd{tsl}")
    for c in range(NCHUNK):
        scr = k.f32("AW", 0, 2048)
        k.tt(k.alt(), scr, src[c], mu, ALU.subtract, tag=f"{tag}_sub{c}")
        if affine is None:
            if out_f32_base is not None:
                dstf = k.f32(out_f32_base[0], out_f32_base[1] + c * 2048, 2048)
                k.tt(k.alt(), dstf, scr, rstd, ALU.mult, tag=f"{tag}_mulf{c}")
                if out_bf_base is not None:
                    dstb = k.bf(out_bf_base[0], out_bf_base[1] + c * 2048, 2048)
                    k.copy(k.alt(), dstb, dstf, tag=f"{tag}_cpb{c}")
            elif out_bf_base is not None:
                dstb = k.bf(out_bf_base[0], out_bf_base[1] + c * 2048, 2048)
                k.tt(k.alt(), dstb, scr, rstd, ALU.mult, tag=f"{tag}_mulb{c}")
        else:
            wcol, bcol = affine
            scr2 = k.f32("AW", 4096, 2048)
            k.tt(k.alt(), scr2, scr, rstd, ALU.mult, tag=f"{tag}_mul{c}")
            dstb = k.bf(out_bf_base[0], out_bf_base[1] + c * 2048, 2048)
            k.ts(k.alt(), dstb, scr2, k.pcol(wcol + c), k.pcol(bcol + c),
                 ALU.mult, ALU.add, tag=f"{tag}_aff{c}")
    return mu, rstd


def emit_stream_w(k, d_wblocks, oc, g, gi, buf, ow=128, kp=128, neg=False, tag="w"):
    assert gi * ow <= 1024
    stage = k.f32("AW", 2048 + (buf % 2) * 1024, gi * ow)
    stage = V(stage.ap[0:kp, :], stage.regs)
    k.dma(stage, dram_v(d_wblocks[oc, :, g * ow:(g + gi) * ow]), tag=f"{tag}_ld{oc}_{g}")
    wbfv = k.bf("AW", 8192 + (buf % 2) * 1024, gi * ow)
    wbfv = V(wbfv.ap[0:kp, :], wbfv.regs)
    if neg:
        k.ts(k.alt(("vector", "gpsimd")), wbfv, stage, -1.0, None, ALU.mult,
             tag=f"{tag}_cv{oc}_{g}")
    else:
        k.copy(k.alt(("vector", "gpsimd")), wbfv, stage, tag=f"{tag}_cv{oc}_{g}")
    return [V(wbfv.ap[:, i * ow:(i + 1) * ow], wbfv.regs) for i in range(gi)]


def emit_proj(k, d_wblocks, n_ic, oc_list, rhs, evac,
              psum_sets=((0, 1, 2, 3), (4, 5, 6, 7)), ntsl=NTSL, ow=128, kp=128,
              tag="proj", buf0=0):
    bi = buf0
    gmax = 1024 // ow
    for j, oc in enumerate(oc_list):
        pset = psum_sets[j % len(psum_sets)]
        for g in range(0, n_ic, gmax):
            gi = min(gmax, n_ic - g)
            wt = emit_stream_w(k, d_wblocks, oc, g, gi, bi % 2, ow=ow, kp=kp, tag=tag)
            bi += 1
            for tsl in range(ntsl):
                pv = k.psum(pset[tsl])
                pv = V(pv.ap[0:ow, :], pv.regs)
                for i in range(gi):
                    ic = g + i
                    k.mm(pv, wt[i], rhs(ic, tsl), start=(ic == 0), stop=(ic == n_ic - 1),
                         tag=f"{tag}_mm{oc}_{tsl}_{ic}")
        for tsl in range(ntsl):
            pv = k.psum(pset[tsl])
            evac(oc, tsl, V(pv.ap[0:ow, :], pv.regs))
    return bi


def emit_l0_attn(k, meta):
    xs = x_chunks(k)
    emit_ln(k, xs, out_f32_base=("AD", 0), xbf_base=("AC", 0), tag="l0a_ln1")
    xn0 = [k.f32("AD", c * 2048, 2048) for c in range(NCHUNK)]
    emit_ln(k, xn0, out_bf_base=("AC", 0), affine=(meta["ln_w"], meta["ln_b"]),
            xbf_base=("AC", 0), tag="l0a_ln2")
    xn = [k.bf("AC", c * 2048, 2048) for c in range(NCHUNK)]

    def r64(v):
        return V(v.ap[0:64, :], v.regs)

    li = r64(k.f32("AB", 0, 2048))
    lf = r64(k.f32("AB", 2048, 2048))
    zt = r64(k.f32("AD", 4096, 2048))
    ot = r64(k.f32("AD", 6144, 2048))
    mt = r64(k.f32("AD", 8192, 2048))
    mprev = r64(k.f32("AD", 10240, 2048))
    fs = r64(k.f32("AD", 12288, 2048))
    isz = r64(k.f32("AD", 14336, 2048))
    dsts = [li, lf, zt, ot]
    bcols = [meta["b_i"], meta["b_f"], meta["b_z"], meta["b_o"]]

    def rhs(ic, tsl):
        return V(xn[ic].ap[:, tsl * 512:(tsl + 1) * 512], xn[ic].regs)

    def evac(oc, tsl, pv):
        dst = V(dsts[oc].ap[:, tsl * 512:(tsl + 1) * 512], dsts[oc].regs)
        bias = k.pcol(bcols[oc])[0:64, :]
        if oc == 0 or oc == 1:
            k.ts("vector", dst, pv, bias, None, ALU.add, tag=f"l0a_b{oc}_{tsl}")
            k.ts("vector", dst, dst, 20.0, -20.0, ALU.min, ALU.max,
                 tag=f"l0a_clip{oc}_{tsl}")
        elif oc == 2:
            k.act(dst, pv, AF.Tanh, bias=bias, tag=f"l0a_z{tsl}")
        else:
            k.act(dst, pv, AF.Sigmoid, bias=bias, tag=f"l0a_o{tsl}")

    emit_proj(k, meta["wifzo"], 8, [0, 1, 2, 3], rhs, evac, ow=64, tag="l0a_p")

    k.scan(mt, lf, li, 0.0, ALU.add, ALU.max, tag="l0a_mscan")
    k.memset("gpsimd", V(mprev.ap[:, 0:1], mprev.regs), 0.0, tag="l0a_mp0")
    k.copy("gpsimd", V(mprev.ap[:, 1:2048], mprev.regs), V(mt.ap[:, 0:2047], mt.regs),
           tag="l0a_mprev")
    k.tt("vector", fs, mprev, mt, ALU.subtract, tag="l0a_d1")
    k.tt("vector", fs, fs, lf, ALU.add, tag="l0a_d1b")
    k.act(fs, fs, AF.Exp, tag="l0a_fs")
    k.tt("vector", isz, li, mt, ALU.subtract, tag="l0a_d2")
    k.act(isz, isz, AF.Exp, tag="l0a_is")
    k.tt("vector", isz, isz, zt, ALU.mult, tag="l0a_isz")
    ct = li
    k.scan(ct, fs, isz, 0.0, ALU.mult, ALU.add, tag="l0a_cscan")
    th = zt
    k.act(th, ct, AF.Tanh, tag="l0a_tanh")
    hbf = V(k.sb["AD"].ap().bitcast(BF16)[0:64, 0:2048], [("AD", 0), ("AD", 1)])
    k.tt("vector", hbf, th, ot, ALU.mult, tag="l0a_h")

    def rhs2(ic, tsl):
        return V(hbf.ap[:, tsl * 512:(tsl + 1) * 512], hbf.regs)

    def evac2(oc, tsl, pv):
        xv = k.f32("AX", oc * 2048 + tsl * 512, 512)
        k.stt("vector", xv, pv, k.pcol(meta["bp"] + oc), xv, ALU.add, ALU.add,
              tag=f"l0a_res{oc}_{tsl}")

    emit_proj(k, meta["wp"], 1, list(range(NCHUNK)), rhs2, evac2, kp=64, tag="l0a_op")


def emit_l0_ffn(k, meta):
    xs = x_chunks(k)
    emit_ln(k, xs, out_bf_base=("AC", 0), xbf_base=("AC", 0), tag="l0f_ln")
    xn = [k.bf("AC", c * 2048, 2048) for c in range(NCHUNK)]
    for c in range(NCHUNK):
        k.dma(k.f32("AD", c * 512, 512), dram_v(meta["fr"][c]), tag=f"l0f_fr{c}")
        k.dma(k.f32("AD", 4096 + c * 512, 512), dram_v(meta["fi"][c]), tag=f"l0f_fi{c}")

    def fr_rhs(c, sl):
        v = k.bf("AD", c * 1024, 1024)
        return V(v.ap[:, sl * 512:(sl + 1) * 512], v.regs)

    def fi_rhs(c, sl):
        v = k.bf("AD", 8192 + c * 1024, 1024)
        return V(v.ap[:, sl * 512:(sl + 1) * 512], v.regs)

    def pt_pass(wa, wb, negb, dst_bfbase, tag):
        bi = 0
        for ob in range(NCHUNK):
            wta = emit_stream_w(k, meta[wa], ob, 0, 8, bi % 2, tag=tag + "a")
            bi += 1
            wtb = emit_stream_w(k, meta[wb], ob, 0, 8, bi % 2, neg=negb, tag=tag + "b")
            bi += 1
            for sl in range(2):
                pv = k.psum(4 * (ob % 2) + sl)
                for ic in range(8):
                    k.mm(pv, wta[ic], fr_rhs(ic, sl), start=(ic == 0), stop=False,
                         tag=f"{tag}_ma{ob}_{sl}_{ic}")
                for ic in range(8):
                    k.mm(pv, wtb[ic], fi_rhs(ic, sl), start=False, stop=(ic == 7),
                         tag=f"{tag}_mb{ob}_{sl}_{ic}")
                dst = k.bf("AB", dst_bfbase + ob * 1024 + sl * 512, 512)
                k.copy("vector", dst, pv, tag=f"{tag}_ev{ob}_{sl}")

    pt_pass("wr", "wi", True, 0, "l0f_prt")
    pt = [k.bf("AB", ob * 1024, 1024) for ob in range(NCHUNK)]

    def m_pass(f_rhs, first, tag):
        for db in range(NCHUNK):
            for sl in range(2):
                pv = k.psum(4 * (db % 2) + sl)
                for oc in range(NCHUNK):
                    lhs = V(pt[oc].ap[:, db * 128:(db + 1) * 128], pt[oc].regs)
                    k.mm(pv, lhs, f_rhs(oc, sl), start=(oc == 0), stop=(oc == 7),
                         tag=f"{tag}_{db}_{sl}_{oc}")
                dst = k.f32("AD", 8192 + db * 1024 + sl * 512, 512)
                if first:
                    k.ts("vector", dst, pv, 1.0 / 1024.0, None, ALU.mult,
                         tag=f"{tag}_ev{db}_{sl}")
                else:
                    k.stt("vector", dst, pv, 1.0 / 1024.0, dst, ALU.mult, ALU.add,
                          tag=f"{tag}_ev{db}_{sl}")

    m_pass(fr_rhs, True, "l0f_m1")
    pt_pass("wi", "wr", False, 0, "l0f_pit")
    m_pass(fi_rhs, False, "l0f_m2")
    m_bf = []
    for db in range(NCHUNK):
        src_v = k.f32("AD", 8192 + db * 1024, 1024)
        dst = k.bf("AB", db * 1024, 1024)
        k.copy(k.alt(), dst, src_v, tag=f"l0f_mcv{db}")
        m_bf.append(dst)
    for ob in range(NCHUNK):
        for tsl in range(NTSL):
            pv = k.psum(4 * (ob % 2) + tsl % 4)
            for dc in range(NCHUNK):
                lhs = V(m_bf[dc].ap[:, ob * 128:(ob + 1) * 128], m_bf[dc].regs)
                r = V(xn[dc].ap[:, tsl * 512:(tsl + 1) * 512], xn[dc].regs)
                k.mm(pv, lhs, r, start=(dc == 0), stop=(dc == 7),
                     tag=f"l0f_y{ob}_{tsl}_{dc}")
            xv = k.f32("AX", ob * 2048 + tsl * 512, 512)
            k.stt("vector", xv, pv, k.pcol(meta["bias"] + ob), xv, ALU.add, ALU.add,
                  tag=f"l0f_res{ob}_{tsl}")


def prep_l0_ffn(host, pc, inp):
    wr = np.asarray(inp["L0_ffn_w_freq_real"], np.float32)
    wi = np.asarray(inp["L0_ffn_w_freq_imag"], np.float32)
    host.add("l0f_wr", pack_lhsT(wr))
    host.add("l0f_wi", pack_lhsT(wi))
    N = 1024
    jk = np.outer(np.arange(N, dtype=np.float64), np.arange(N, dtype=np.float64))
    ang = 2.0 * np.pi * (jk % N) / N
    Fr = np.cos(ang).astype(np.float32)
    Fi = (-np.sin(ang)).astype(np.float32)
    frp = np.stack([pack_bf16_as_f32(Fr[c * 128:(c + 1) * 128, :]) for c in range(8)])
    fip = np.stack([pack_bf16_as_f32(Fi[c * 128:(c + 1) * 128, :]) for c in range(8)])
    host.add("l0f_frb", frp)
    host.add("l0f_fib", fip)
    return {
        "wr": "l0f_wr", "wi": "l0f_wi", "fr": "l0f_frb", "fi": "l0f_fib",
        "bias": pc.add_vecD(inp["L0_ffn_bias"]),
    }


def emit_spill_x(k, tag="spill"):
    for c in range(NCHUNK):
        k.dma(V(k.xsp[c * 128:(c + 1) * 128, :], [("dram", "xsp")]),
              k.f32("AX", c * 2048, 2048), ch="gpsimd", tag=f"{tag}{c}")


def emit_restore_x_to_AD(k, tag="rest"):
    for c in range(NCHUNK):
        k.dma(k.f32("AD", c * 2048, 2048),
              V(k.xsp[c * 128:(c + 1) * 128, :], [("dram", "xsp")]), tag=f"{tag}{c}")


def emit_copy_AD_to_AX(k, tag="adax"):
    for c in range(NCHUNK):
        k.copy(k.alt(), k.f32("AX", c * 2048, 2048), k.f32("AD", c * 2048, 2048),
               tag=f"{tag}{c}")


def emit_l1_attn(k, meta):
    xs = x_chunks(k)
    emit_ln(k, xs, out_bf_base=("AC", 0), xbf_base=("AC", 0), tag="l1a_ln")
    xn = [k.bf("AC", c * 2048, 2048) for c in range(NCHUNK)]
    emit_spill_x(k, tag="l1a_sp")

    def rhs_xn(ic, tsl):
        return V(xn[ic].ap[:, tsl * 512:(tsl + 1) * 512], xn[ic].regs)

    def evq(oc, tsl, pv):
        dst = k.bf("AD", oc * 2048 + tsl * 512, 512)
        k.copy("vector", dst, pv, tag=f"l1a_evq{oc}_{tsl}")

    def evk(oc, tsl, pv):
        dst = k.bf("AD", 16384 + oc * 2048 + tsl * 512, 512)
        k.copy("vector", dst, pv, tag=f"l1a_evk{oc}_{tsl}")

    emit_proj(k, meta["wq"], 8, list(range(8)), rhs_xn, evq, tag="l1a_q")
    emit_proj(k, meta["wk"], 8, list(range(8)), rhs_xn, evk, tag="l1a_k")
    wv_t = []
    for ic in range(8):
        stage = k.f32("AW", 2048 + (ic % 2) * 1024, 1024)
        k.dma(stage, dram_v(meta["wv"][0, :, ic * 1024:(ic + 1) * 1024]),
              tag=f"l1a_wvld{ic}")
        dst = k.bf("AB", ic * 1024, 1024)
        k.copy(k.alt(), dst, stage, tag=f"l1a_wvcv{ic}")
        wv_t.append(dst)
    for tt in range(16):
        for eh in range(2):
            pv = k.psum(4 + (tt % 2) * 2 + eh)
            for ic in range(8):
                lhs = V(xn[ic].ap[:, tt * 128:(tt + 1) * 128], xn[ic].regs)
                r = V(wv_t[ic].ap[:, eh * 512:(eh + 1) * 512], wv_t[ic].regs)
                k.mm(pv, lhs, r, start=(ic == 0), stop=(ic == 7),
                     tag=f"l1a_v{tt}_{eh}_{ic}")
            dst = k.bf("AX", tt * 1024 + eh * 512, 512)
            k.copy("vector", dst, pv, tag=f"l1a_evv{tt}_{eh}")
    v_tok = [k.bf("AX", tt * 1024, 1024) for tt in range(16)]

    for c in range(NCHUNK):
        cs = k.f32("AB", 0, 2048)
        ss = k.f32("AB", 2048, 2048)
        k.dma(cs, dram_v(meta["cs"][c * 128:(c + 1) * 128, :]), tag=f"l1a_cs{c}")
        k.dma(ss, dram_v(meta["ss"][c * 128:(c + 1) * 128, :]), tag=f"l1a_ss{c}")
        for base, nm in ((0, "q"), (16384, "k")):
            t = k.bf("AD", base + c * 2048, 2048)
            tmp = k.bf("AW", 8192, 2048)
            for sl in range(4):
                pv = k.psum(sl % 2)
                k.mm(pv, V(k.swap_bf, [("consts",)]),
                     V(t.ap[:, sl * 512:(sl + 1) * 512], t.regs),
                     start=True, stop=True, tag=f"l1a_sw{nm}{c}_{sl}")
                k.tt("vector", V(tmp.ap[:, sl * 512:(sl + 1) * 512], tmp.regs), pv,
                     V(ss.ap[:, sl * 512:(sl + 1) * 512], ss.regs), ALU.mult,
                     tag=f"l1a_swm{nm}{c}_{sl}")
            k.tt("vector", t, t, cs, ALU.mult, tag=f"l1a_rc{nm}{c}")
            k.tt("vector", t, t, tmp, ALU.add, tag=f"l1a_ra{nm}{c}")
    q_c = [k.bf("AD", c * 2048, 2048) for c in range(NCHUNK)]
    k_c = [k.bf("AD", 16384 + c * 2048, 2048) for c in range(NCHUNK)]

    for qt in range(16):
        L = (qt + 1) * 128
        nkb = qt // 4 + 1
        scores = k.f32("AB", 0, 2048)
        for kb in range(nkb):
            pv = k.psum(kb % 4)
            for ec in range(8):
                lhs = V(q_c[ec].ap[:, qt * 128:(qt + 1) * 128], q_c[ec].regs)
                r = V(k_c[ec].ap[:, kb * 512:(kb + 1) * 512], k_c[ec].regs)
                k.mm(pv, lhs, r, start=(ec == 0), stop=(ec == 7),
                     tag=f"l1a_s{qt}_{kb}_{ec}")
            full = (qt % 4) * 128 if kb == qt // 4 else 512
            if full > 0:
                k.copy("vector", V(scores.ap[:, kb * 512:kb * 512 + full], scores.regs),
                       V(pv.ap[:, 0:full], pv.regs), tag=f"l1a_se{qt}_{kb}")
            if kb == qt // 4:
                dcol = (qt % 4) * 128
                k.tt("vector", V(scores.ap[:, kb * 512 + dcol:kb * 512 + dcol + 128], scores.regs),
                     V(pv.ap[:, dcol:dcol + 128], pv.regs),
                     V(k.causal_mask, [("consts",)]), ALU.add, tag=f"l1a_sd{qt}")
        rmax = k.scol(0)
        nb = k.scol(1)
        rsum = k.scol(2)
        rrec = k.scol(3)
        k.reduce("vector", rmax, V(scores.ap[:, 0:L], scores.regs), ALU.max,
                 tag=f"l1a_rm{qt}")
        k.ts("vector", nb, rmax, -1.0 / 32.0, None, ALU.mult, tag=f"l1a_nb{qt}")
        ef = k.f32("AW", 0, 2048)
        efL = V(ef.ap[:, 0:L], ef.regs)
        k.p.op("scalar", lambda e, o=efL.ap, x=scores.ap[:, 0:L], b=nb.ap:
               e.activation(o, x, AF.Exp, bias=b, scale=1.0 / 32.0),
               reads=scores.regs + nb.regs + [("consts",)], writes=efL.regs,
               tag=f"l1a_exp{qt}")
        k.reduce("vector", rsum, efL, ALU.add, tag=f"l1a_rs{qt}")
        k.recip(rrec, rsum, tag=f"l1a_rc{qt}")
        pbf = k.bf("AW", 8192, 2048)
        pL = V(pbf.ap[:, 0:L], pbf.regs)
        k.ts("vector", pL, efL, rrec, None, ALU.mult, tag=f"l1a_pn{qt}")
        if DBG_SINK and qt == DBG_SINK.get("qt", -1):
            k.dma(dram_v(DBG_SINK["scores"]), scores, ch="gpsimd", tag="dbg_sc")
            k.dma(dram_v(DBG_SINK["ef"]), ef, ch="gpsimd", tag="dbg_ef")
            k.dma(dram_v(DBG_SINK["pbf"]), k.f32("AW", 4096, 1024), ch="gpsimd", tag="dbg_p")
            k.dma(dram_v(DBG_SINK["stats"]), k.scol(0, 8), ch="gpsimd", tag="dbg_st")
        ptq = k.bf("AW", 10240, 2048)
        for kt in range(qt + 1):
            pvt = k.psum_bf(6 + (kt % 2), 0, 128)
            k.transpose(pvt, V(pbf.ap[:, kt * 128:(kt + 1) * 128], pbf.regs),
                        tag=f"l1a_tr{qt}_{kt}")
            k.copy("vector", V(ptq.ap[:, kt * 128:(kt + 1) * 128], ptq.regs), pvt,
                   tag=f"l1a_tre{qt}_{kt}")
        for do in range(8):
            pv = k.psum(do % 6)
            pv = V(pv.ap[:, 0:128], pv.regs)
            for kt in range(qt + 1):
                lhs = V(v_tok[kt].ap[:, do * 128:(do + 1) * 128], v_tok[kt].regs)
                r = V(ptq.ap[:, kt * 128:(kt + 1) * 128], ptq.regs)
                k.mm(pv, lhs, r, start=(kt == 0), stop=(kt == qt),
                     tag=f"l1a_o{qt}_{do}_{kt}")
            ob = k.bf("AC", do * 2048 + qt * 128, 128)
            k.copy("vector", ob, pv, tag=f"l1a_oe{qt}_{do}")

    emit_restore_x_to_AD(k, tag="l1a_rs")
    for c in range(NCHUNK):
        k.tt(k.alt(), k.f32("AX", c * 2048, 2048), k.f32("AD", c * 2048, 2048),
             k.bf("AC", c * 2048, 2048), ALU.add, tag=f"l1a_fin{c}")


def prep_l1_attn(host, pc, inp):
    host.add("l1a_wq", pack_lhsT(np.asarray(inp["L1_attn_w_q"]).T))
    host.add("l1a_wk", pack_lhsT(np.asarray(inp["L1_attn_w_k"]).T))
    host.add("l1a_wv", pack_lhsT(np.asarray(inp["L1_attn_w_v"]).T, ow=1024))
    d = 1024
    i = np.arange(d // 2, dtype=np.float64)
    theta = np.power(10000.0, -2.0 * (i - 1.0) / d)
    pos = np.arange(T, dtype=np.float64)
    ang = pos[None, :] * theta[:, None]
    cos, sin = np.cos(ang), np.sin(ang)
    cs = np.zeros((d, T), np.float32)
    ss = np.zeros((d, T), np.float32)
    cs[0::2, :] = cos
    cs[1::2, :] = cos
    ss[0::2, :] = sin
    ss[1::2, :] = -sin
    host.add("l1a_cs", cs)
    host.add("l1a_ss", ss)
    return {"wq": "l1a_wq", "wk": "l1a_wk", "wv": "l1a_wv",
            "cs": "l1a_cs", "ss": "l1a_ss"}


def emit_l1_ffn(k, meta):
    xs = x_chunks(k)
    emit_ln(k, xs, out_bf_base=("AC", 0), xbf_base=("AC", 0), tag="l1f_ln")
    xn = [k.bf("AC", c * 2048, 2048) for c in range(NCHUNK)]
    k.load_bp(meta["bp"], 136, tag="l1f_bp")
    for c in range(NCHUNK):
        k.copy(k.alt(), k.f32("AD", c * 2048, 2048), k.f32("AX", c * 2048, 2048),
               tag=f"l1f_xcp{c}")
    sc = {"w1": 0, "w2": 32, "h1": 64, "h2": 96, "w3": 128}
    for tq in range(4):
        t0 = tq * 512
        hA = [k.bf("AX", oc * 512, 512) for oc in range(32)]
        hB = [k.bf("AX", 16384 + oc * 512, 512) for oc in range(32)]
        bi = 0
        for oc in range(32):
            wtA = emit_stream_w(k, meta["w1"], oc, 0, 8, bi % 2, tag="l1f_w1")
            bi += 1
            wtB = emit_stream_w(k, meta["w2"], oc, 0, 8, bi % 2, tag="l1f_w2")
            bi += 1
            pA = k.psum((oc % 2) * 2)
            pB = k.psum((oc % 2) * 2 + 1)
            for ic in range(8):
                r = V(xn[ic].ap[:, t0:t0 + 512], xn[ic].regs)
                k.mm(pA, wtA[ic], r, start=(ic == 0), stop=(ic == 7),
                     tag=f"l1f_mA{tq}_{oc}_{ic}")
            for ic in range(8):
                r = V(xn[ic].ap[:, t0:t0 + 512], xn[ic].regs)
                k.mm(pB, wtB[ic], r, start=(ic == 0), stop=(ic == 7),
                     tag=f"l1f_mB{tq}_{oc}_{ic}")
            a = k.f32("AB", (oc % 2) * 512, 512)
            k.act(a, pA, AF.Silu, scale=V(k.bpcol(sc["w1"] + oc), [("BP",)]),
                  tag=f"l1f_silu{tq}_{oc}")
            k.stt("vector", hA[oc], pB, V(k.bpcol(sc["w2"] + oc), [("BP",)]), a,
                  ALU.mult, ALU.mult, tag=f"l1f_h{tq}_{oc}")

        def mk_rhs(buf):
            def rhs(ic, tsl):
                return V(buf[ic].ap, buf[ic].regs)
            return rhs

        def mk_gelu_evac(dstbuf, colbase, tag):
            def evac(oc, tsl, pv):
                g = k.f32("AB", (oc % 2) * 512, 512)
                k.act(g, pv, AF.Gelu, scale=V(k.bpcol(colbase + oc), [("BP",)]),
                      tag=f"{tag}_g{oc}")
                k.copy(k.alt(), dstbuf[oc], g, tag=f"{tag}_c{oc}")
            return evac

        emit_proj(k, meta["h1"], 32, list(range(32)), mk_rhs(hA),
                  mk_gelu_evac(hB, sc["h1"], f"l1f_s2_{tq}"),
                  psum_sets=((4,), (5,), (6,), (7,)), ntsl=1, tag=f"l1f_h1_{tq}")
        emit_proj(k, meta["h2"], 32, list(range(32)), mk_rhs(hB),
                  mk_gelu_evac(hA, sc["h2"], f"l1f_s3_{tq}"),
                  psum_sets=((4,), (5,), (6,), (7,)), ntsl=1, tag=f"l1f_h2_{tq}")

        def evac_w3(oc, tsl, pv):
            acc = k.f32("AD", oc * 2048 + t0, 512)
            k.stt("vector", acc, pv, V(k.bpcol(sc["w3"] + oc), [("BP",)]), acc,
                  ALU.mult, ALU.add, tag=f"l1f_w3e{tq}_{oc}")

        emit_proj(k, meta["w3"], 32, list(range(8)), mk_rhs(hA), evac_w3,
                  psum_sets=((4,), (5,), (6,), (7,)), ntsl=1, tag=f"l1f_w3_{tq}")
    emit_copy_AD_to_AX(k, tag="l1f_fin")


def prep_l1_ffn(host, pc, inp):
    host.add("l1f_w1", pack_lhsT(np.asarray(inp["L1_ffn_w1_w"]).T))
    host.add("l1f_w2", pack_lhsT(np.asarray(inp["L1_ffn_w2_w"]).T))
    host.add("l1f_h1", pack_lhsT(np.asarray(inp["L1_ffn_h1_w"]).T))
    host.add("l1f_h2", pack_lhsT(np.asarray(inp["L1_ffn_h2_w"]).T))
    host.add("l1f_w3", pack_lhsT(np.asarray(inp["L1_ffn_w3_w"]).T))
    bp = np.zeros((128, 136), np.float32)
    for j, nm in enumerate(["w1_s", "w2_s", "h1_s", "h2_s"]):
        v = np.asarray(inp[f"L1_ffn_{nm}"], np.float32)
        for c in range(32):
            bp[:, j * 32 + c] = v[c * 128:(c + 1) * 128]
    v = np.asarray(inp["L1_ffn_w3_s"], np.float32)
    for c in range(8):
        bp[:, 128 + c] = v[c * 128:(c + 1) * 128]
    host.add("l1f_bpa", bp)
    return {"w1": "l1f_w1", "w2": "l1f_w2", "h1": "l1f_h1", "h2": "l1f_h2",
            "w3": "l1f_w3", "bp": "l1f_bpa"}


def emit_l2_attn(k, meta):
    xs = x_chunks(k)
    emit_ln(k, xs, out_f32_base=("AD", 0), xbf_base=("AC", 0), tag="l2a_ln")
    xn = [k.f32("AD", c * 2048, 2048) for c in range(NCHUNK)]
    emit_spill_x(k, tag="l2a_sp")
    mixdst = [("AC", 0, meta["tmk"]), ("AX", 0, meta["tmv"]), ("AX", 16384, meta["tmr"])]
    for c in range(NCHUNK):
        tmp = k.f32("AB", 0, 2047)
        k.tt("vector", tmp, V(xn[c].ap[:, 1:2048], xn[c].regs),
             V(xn[c].ap[:, 0:2047], xn[c].regs), ALU.subtract, tag=f"l2a_d{c}")
        for (ar, base, tmcol) in mixdst:
            dst = k.bf(ar, base + c * 2048, 2048)
            k.stt(k.alt(), V(dst.ap[:, 1:2048], dst.regs), tmp, k.pcol(tmcol + c),
                  V(xn[c].ap[:, 0:2047], xn[c].regs), ALU.mult, ALU.add,
                  tag=f"l2a_m{ar}{base}{c}")
            k.ts(k.alt(), V(dst.ap[:, 0:1], dst.regs), V(xn[c].ap[:, 0:1], xn[c].regs),
                 k.pcol(tmcol + c), None, ALU.mult, tag=f"l2a_m0{ar}{base}{c}")
    mix_k = [k.bf("AC", c * 2048, 2048) for c in range(NCHUNK)]
    mix_v = [k.bf("AX", c * 2048, 2048) for c in range(NCHUNK)]
    mix_r = [k.bf("AX", 16384 + c * 2048, 2048) for c in range(NCHUNK)]

    def mk_rhs(buf):
        def rhs(ic, tsl):
            return V(buf[ic].ap[:, tsl * 512:(tsl + 1) * 512], buf[ic].regs)
        return rhs

    def mk_ev(ar, base, tag):
        def evac(oc, tsl, pv):
            k.copy("vector", k.bf(ar, base + oc * 2048 + tsl * 512, 512), pv,
                   tag=f"{tag}{oc}_{tsl}")
        return evac

    emit_proj(k, meta["wk"], 8, list(range(8)), mk_rhs(mix_k), mk_ev("AD", 0, "l2a_evk"),
              tag="l2a_k")
    emit_proj(k, meta["wv"], 8, list(range(8)), mk_rhs(mix_v), mk_ev("AD", 16384, "l2a_evv"),
              tag="l2a_v")
    emit_proj(k, meta["wr"], 8, list(range(8)), mk_rhs(mix_r), mk_ev("AX", 0, "l2a_evr"),
              tag="l2a_r")
    k_c = [k.bf("AD", c * 2048, 2048) for c in range(NCHUNK)]
    v_c = [k.bf("AD", 16384 + c * 2048, 2048) for c in range(NCHUNK)]
    r_c = [k.bf("AX", c * 2048, 2048) for c in range(NCHUNK)]

    for c in range(NCHUNK):
        dec = k.scol(8 + c)
        eU = k.scol(16 + c)
        k.act(dec, V(k.pcol(meta["W"] + c), [("params",)]), AF.Exp, tag=f"l2a_w{c}")
        k.act(dec, dec, AF.Exp, scale=-1.0, tag=f"l2a_dec{c}")
        k.act(eU, V(k.pcol(meta["U"] + c), [("params",)]), AF.Exp, tag=f"l2a_eu{c}")
    ones_b = k.f32("AW", 0, 2048)
    k.memset("vector", ones_b, 1.0, tag="l2a_ones")

    for c in range(NCHUNK):
        ek = k.f32("AB", 0, 2048)
        ekv = k.f32("AB", 2048, 2048)
        dec_b = k.f32("AW", 2048, 2048)
        Abuf = k.f32("AW", 4096, 2048)
        k.act(ek, k_c[c], AF.Exp, tag=f"l2a_ek{c}")
        k.tt("vector", ekv, ek, v_c[c], ALU.mult, tag=f"l2a_ekv{c}")
        k.ts("vector", dec_b, ones_b, k.scol(8 + c), None, ALU.mult, tag=f"l2a_db{c}")
        k.scan(Abuf, dec_b, ekv, 0.0, ALU.mult, ALU.add, tag=f"l2a_A{c}")
        k.stt("vector", V(ekv.ap[:, 1:2048], ekv.regs), V(ekv.ap[:, 1:2048], ekv.regs),
              k.scol(16 + c), V(Abuf.ap[:, 0:2047], Abuf.regs), ALU.mult, ALU.add,
              tag=f"l2a_num{c}")
        k.ts("vector", V(ekv.ap[:, 0:1], ekv.regs), V(ekv.ap[:, 0:1], ekv.regs),
             k.scol(16 + c), None, ALU.mult, tag=f"l2a_num0{c}")
        k.scan(Abuf, dec_b, ek, 0.0, ALU.mult, ALU.add, tag=f"l2a_B{c}")
        k.stt("vector", V(ek.ap[:, 1:2048], ek.regs), V(ek.ap[:, 1:2048], ek.regs),
              k.scol(16 + c), V(Abuf.ap[:, 0:2047], Abuf.regs), ALU.mult, ALU.add,
              tag=f"l2a_den{c}")
        k.ts("vector", V(ek.ap[:, 0:1], ek.regs), V(ek.ap[:, 0:1], ek.regs),
             k.scol(16 + c), None, ALU.mult, tag=f"l2a_den0{c}")
        k.act(ek, ek, AF.Ln, tag=f"l2a_ln{c}")
        k.act(ek, ek, AF.Exp, scale=-1.0, tag=f"l2a_rden{c}")
        k.tt("vector", ekv, ekv, ek, ALU.mult, tag=f"l2a_y{c}")
        sig = k.f32("AW", 2048, 2048)
        k.act(sig, r_c[c], AF.Sigmoid, tag=f"l2a_sig{c}")
        k.tt("vector", k.bf("AC", c * 2048, 2048), ekv, sig, ALU.mult, tag=f"l2a_rw{c}")
    rw = [k.bf("AC", c * 2048, 2048) for c in range(NCHUNK)]

    emit_restore_x_to_AD(k, tag="l2a_rs")

    def evac_o(oc, tsl, pv):
        xv = k.f32("AD", oc * 2048 + tsl * 512, 512)
        k.tt("vector", xv, pv, xv, ALU.add, tag=f"l2a_oe{oc}_{tsl}")

    emit_proj(k, meta["wo"], 8, list(range(8)), mk_rhs(rw), evac_o, tag="l2a_o")
    emit_copy_AD_to_AX(k, tag="l2a_fin")


def prep_l2_attn(host, pc, inp):
    host.add("l2a_wk", pack_lhsT(np.asarray(inp["L2_attn_Wk"]).T))
    host.add("l2a_wv", pack_lhsT(np.asarray(inp["L2_attn_Wv"]).T))
    host.add("l2a_wr", pack_lhsT(np.asarray(inp["L2_attn_Wr"]).T))
    host.add("l2a_wo", pack_lhsT(np.asarray(inp["L2_attn_Wo"]).T))
    return {"wk": "l2a_wk", "wv": "l2a_wv", "wr": "l2a_wr", "wo": "l2a_wo",
            "W": pc.add_vecD(inp["L2_attn_W"]),
            "U": pc.add_vecD(inp["L2_attn_U"]),
            "tmk": pc.add_vecD(np.asarray(inp["L2_attn_time_mix_k"]).reshape(-1)),
            "tmv": pc.add_vecD(np.asarray(inp["L2_attn_time_mix_v"]).reshape(-1)),
            "tmr": pc.add_vecD(np.asarray(inp["L2_attn_time_mix_r"]).reshape(-1))}


def emit_l2_ffn(k, meta):
    xs = x_chunks(k)
    emit_ln(k, xs, out_f32_base=("AD", 0), out_bf_base=("AC", 0),
            xbf_base=("AC", 0), tag="l2f_ln")
    xn32 = [k.f32("AD", c * 2048, 2048) for c in range(NCHUNK)]
    xn = [k.bf("AC", c * 2048, 2048) for c in range(NCHUNK)]
    k.load_bp(meta["bp"], 256, tag="l2f_bp")
    wgs = k.f32("AW", 2048, 64)
    k.dma(wgs, dram_v(meta["wg"][0]), tag="l2f_wgld")
    glog = k.f32("AB", 0, 128)
    for tt in range(16):
        pv = k.psum(4 + tt % 2)
        pv = V(pv.ap[:, 0:8], pv.regs)
        for ic in range(NCHUNK):
            lhs = V(xn32[ic].ap[:, tt * 128:(tt + 1) * 128], xn32[ic].regs)
            r = V(wgs.ap[:, ic * 8:(ic + 1) * 8], wgs.regs)
            k.mm(pv, lhs, r, start=(ic == 0), stop=(ic == 7), tag=f"l2f_g{tt}_{ic}")
        k.copy("vector", V(glog.ap[:, tt * 8:(tt + 1) * 8], glog.regs), pv,
               tag=f"l2f_ge{tt}")
    for c in range(NCHUNK):
        k.copy(k.alt(), k.f32("AD", c * 2048, 2048), k.f32("AX", c * 2048, 2048),
               tag=f"l2f_xcp{c}")
    f_all = k.f32("AB", 128, 128)
    for tt in range(16):
        g = V(glog.ap[:, tt * 8:(tt + 1) * 8], glog.regs)
        m1, m2, q, rden, w2c = k.scol(24), k.scol(25), k.scol(26), k.scol(27), k.scol(28)
        eq1 = k.f32("AB", 256, 8)
        eq2 = k.f32("AB", 264, 8)
        k.reduce("vector", m1, g, ALU.max, tag=f"l2f_m1{tt}")
        k.ts("vector", eq1, g, m1, None, ALU.is_equal, tag=f"l2f_eq1{tt}")
        k.stt("vector", eq2, eq1, -60000.0, g, ALU.mult, ALU.add, tag=f"l2f_msk{tt}")
        k.reduce("vector", m2, eq2, ALU.max, tag=f"l2f_m2{tt}")
        k.ts("vector", eq2, eq2, m2, None, ALU.is_equal, tag=f"l2f_eq2{tt}")
        k.ts("vector", q, m2, m1, None, ALU.subtract, tag=f"l2f_q{tt}")
        k.act(q, q, AF.Exp, tag=f"l2f_eq{tt}")
        k.ts("vector", rden, q, 1.0, None, ALU.add, tag=f"l2f_d{tt}")
        k.recip(rden, rden, tag=f"l2f_r{tt}")
        k.ts("vector", w2c, q, rden, None, ALU.mult, tag=f"l2f_w2{tt}")
        fv = V(f_all.ap[:, tt * 8:(tt + 1) * 8], f_all.regs)
        k.ts("vector", fv, eq1, rden, None, ALU.mult, tag=f"l2f_f1{tt}")
        k.stt("vector", fv, eq2, w2c, fv, ALU.mult, ALU.add, tag=f"l2f_f2{tt}")
    fbf_all = V(k.sb["AB"].ap().bitcast(BF16)[:, 6656:6784], [("AB", 6)])
    k.copy("vector", fbf_all, f_all, tag="l2f_fcv")
    f_fm_bf = V(k.sb["AW"].ap().bitcast(BF16)[0:8, 10240:12288],
                [("AW", 10), ("AW", 11)])
    for tt in range(16):
        pvt = k.psum_bf(6, 0, 128)
        pvt = V(pvt.ap[0:8, :], pvt.regs)
        k.p.op("tensor", lambda e, o=pvt.ap, s=fbf_all.ap[:, tt * 8:(tt + 1) * 8]:
               e.transpose(o, s, k.ident_bf),
               reads=fbf_all.regs + [("consts",)], writes=pvt.regs, tag=f"l2f_tr{tt}")
        k.copy("vector", V(f_fm_bf.ap[:, tt * 128:(tt + 1) * 128], f_fm_bf.regs), pvt,
               tag=f"l2f_tre{tt}")

    if DBG_SINK.get("l2f"):
        k.dma(dram_v(DBG_SINK["glog"]), k.f32("AB", 0, 128), ch="gpsimd", tag="dbg_gl")
        k.dma(dram_v(DBG_SINK["fall"]), f_all, ch="gpsimd", tag="dbg_fa")
        k.dma(dram_v(DBG_SINK["ffm"]), V(k.sb["AW"].ap()[0:8, 5120:6144], f_fm_bf.regs),
              ch="gpsimd", tag="dbg_ffm")
    for e in range(8):
        frow = V(k.sb["AW"].ap().bitcast(BF16)[0:1, 2048:4096], [("AW", 2), ("AW", 3)])
        for tt in range(16):
            pvt = k.psum_bf(7, 0, 128)
            pvt = V(pvt.ap[0:1, :], pvt.regs)
            k.p.op("tensor", lambda ee, o=pvt.ap, s=fbf_all.ap[:, tt * 8 + e:tt * 8 + e + 1]:
                   ee.transpose(o, s, k.ident_bf),
                   reads=fbf_all.regs + [("consts",)], writes=pvt.regs,
                   tag=f"l2f_fr{e}_{tt}")
            k.copy("vector", V(frow.ap[:, tt * 128:(tt + 1) * 128], frow.regs), pvt,
                   tag=f"l2f_fre{e}_{tt}")
        for th in range(2):
            fbc = k.f32("AW", 0, 1024)
            for sl in range(2):
                pv = k.psum(4)
                lhs = V(k.ones_bf[0:1, :], [("consts",)])
                r = V(frow.ap[:, th * 1024 + sl * 512: th * 1024 + (sl + 1) * 512],
                      frow.regs)
                k.mm(pv, lhs, r, start=True, stop=True, tag=f"l2f_fb{th}_{e}_{sl}")
                k.copy("vector", V(fbc.ap[:, sl * 512:(sl + 1) * 512], fbc.regs), pv,
                       tag=f"l2f_fbe{th}_{e}_{sl}")
            bi = 0
            for oc in range(32):
                wt = emit_stream_w(k, meta["w1"][e], oc, 0, 8, bi % 2, tag="l2f_w1")
                bi += 1
                for sl in range(2):
                    pv = k.psum((oc % 2) * 2 + sl)
                    for ic in range(NCHUNK):
                        r = V(xn[ic].ap[:, th * 1024 + sl * 512:th * 1024 + (sl + 1) * 512],
                              xn[ic].regs)
                        k.mm(pv, wt[ic], r, start=(ic == 0), stop=(ic == 7),
                             tag=f"l2f_u{th}_{e}_{oc}_{sl}_{ic}")
                    a = k.f32("AB", 2304 + (oc % 2) * 512, 512)
                    k.act(a, pv, AF.Relu, bias=V(k.bpcol(e * 32 + oc), [("BP",)]),
                          tag=f"l2f_re{th}_{e}_{oc}_{sl}")
                    hb = k.bf("AX", oc * 1024 + sl * 512, 512)
                    k.tt("vector", hb, a, V(fbc.ap[:, sl * 512:(sl + 1) * 512], fbc.regs),
                         ALU.mult, tag=f"l2f_hf{th}_{e}_{oc}_{sl}")
            h_e = [k.bf("AX", oc * 1024, 1024) for oc in range(32)]
            bi2 = 0
            for ob in range(NCHUNK):
                for g in range(4):
                    wt = emit_stream_w(k, meta["w2"][e], ob, g * 8, 8, bi2 % 2,
                                       tag="l2f_w2")
                    bi2 += 1
                    for sl in range(2):
                        pv = k.psum(5 + sl)
                        for i in range(8):
                            ic = g * 8 + i
                            r = V(h_e[ic].ap[:, sl * 512:(sl + 1) * 512], h_e[ic].regs)
                            k.mm(pv, wt[i], r, start=(ic == 0), stop=(ic == 31),
                                 tag=f"l2f_d{th}_{e}_{ob}_{sl}_{ic}")
                for sl in range(2):
                    pv = k.psum(5 + sl)
                    acc = k.f32("AD", ob * 2048 + th * 1024 + sl * 512, 512)
                    k.tt("vector", acc, pv, acc, ALU.add, tag=f"l2f_de{th}_{e}_{ob}_{sl}")
            if DBG_SINK.get("l2f") and e == 0 and th == 1:
                for cdump in range(NCHUNK):
                    k.dma(dram_v(DBG_SINK["acc0"][cdump * 128:(cdump + 1) * 128, :]),
                          k.f32("AD", cdump * 2048, 2048), ch="gpsimd", tag=f"dbg_a{cdump}")
                for hdump in range(8):
                    k.dma(dram_v(DBG_SINK["h0"][hdump * 128:(hdump + 1) * 128, :]),
                          k.f32("AX", hdump * 512, 512), ch="gpsimd", tag=f"dbg_h{hdump}")
    for ob in range(NCHUNK):
        wt = emit_stream_w(k, meta["b2"], ob, 0, 1, ob % 2, ow=128, kp=8, tag="l2f_b2")
        for tsl in range(4):
            pv = k.psum(7)
            r = V(f_fm_bf.ap[:, tsl * 512:(tsl + 1) * 512], f_fm_bf.regs)
            k.mm(pv, wt[0], r, start=True, stop=True, tag=f"l2f_b2m{ob}_{tsl}")
            acc = k.f32("AD", ob * 2048 + tsl * 512, 512)
            k.tt("vector", acc, pv, acc, ALU.add, tag=f"l2f_b2e{ob}_{tsl}")
    emit_copy_AD_to_AX(k, tag="l2f_fin")


def prep_l2_ffn(host, pc, inp):
    W1 = np.asarray(inp["L2_ffn_W1"], np.float32)
    W2 = np.asarray(inp["L2_ffn_W2"], np.float32)
    host.add("l2f_w1", np.stack([pack_lhsT(W1[e]) for e in range(8)]))
    host.add("l2f_w2", np.stack([pack_lhsT(W2[e]) for e in range(8)]))
    host.add("l2f_wg", pack_lhsT(np.asarray(inp["L2_ffn_Wg"], np.float32), ow=8))
    host.add("l2f_b2p", pack_lhsT(np.asarray(inp["L2_ffn_b2"], np.float32)))
    b1 = np.asarray(inp["L2_ffn_b1"], np.float32)
    bp = np.zeros((128, 256), np.float32)
    for e in range(8):
        for c in range(32):
            bp[:, e * 32 + c] = b1[e, c * 128:(c + 1) * 128]
    host.add("l2f_bpa", bp)
    return {"w1": "l2f_w1", "w2": "l2f_w2", "wg": "l2f_wg", "b2": "l2f_b2p",
            "bp": "l2f_bpa"}


L3_BAND = 8


def emit_l3_attn(k, meta):
    xs = x_chunks(k)
    emit_ln(k, xs, out_bf_base=("AC", 0), xbf_base=("AC", 0), tag="l3a_ln")
    xn = [k.bf("AC", c * 2048, 2048) for c in range(NCHUNK)]
    emit_spill_x(k, tag="l3a_sp")

    def rhs_xn(ic, tsl):
        return V(xn[ic].ap[:, tsl * 512:(tsl + 1) * 512], xn[ic].regs)

    gq = V(k.gamma_q(128), [("consts",)])
    gk = V(k.gamma_k(128), [("consts",)])

    def mk_qk_ev(base, bcol, gtile, tag):
        def evac(oc, tsl, pv):
            for sub in range(4):
                dst = k.bf("AD", base + oc * 2048 + tsl * 512 + sub * 128, 128)
                k.stt("vector", dst, V(pv.ap[:, sub * 128:(sub + 1) * 128], pv.regs),
                      k.pcol(bcol + oc), gtile, ALU.add, ALU.mult,
                      tag=f"{tag}{oc}_{tsl}_{sub}")
        return evac

    emit_proj(k, meta["wq"], 8, list(range(8)), rhs_xn,
              mk_qk_ev(0, meta["bq"], gq, "l3a_evq"), tag="l3a_q")
    emit_proj(k, meta["wk"], 8, list(range(8)), rhs_xn,
              mk_qk_ev(16384, meta["bk"], gk, "l3a_evk"), tag="l3a_k")
    wv_t = []
    for ic in range(8):
        stage = k.f32("AW", 2048 + (ic % 2) * 1024, 1024)
        k.dma(stage, dram_v(meta["wv"][0, :, ic * 1024:(ic + 1) * 1024]),
              tag=f"l3a_wvld{ic}")
        dst = k.bf("AB", ic * 1024, 1024)
        k.copy(k.alt(), dst, stage, tag=f"l3a_wvcv{ic}")
        wv_t.append(dst)
    for tt in range(16):
        for eh in range(2):
            pv = k.psum(4 + (tt % 2) * 2 + eh)
            for ic in range(8):
                lhs = V(xn[ic].ap[:, tt * 128:(tt + 1) * 128], xn[ic].regs)
                r = V(wv_t[ic].ap[:, eh * 512:(eh + 1) * 512], wv_t[ic].regs)
                k.mm(pv, lhs, r, start=(ic == 0), stop=(ic == 7),
                     tag=f"l3a_v{tt}_{eh}_{ic}")
            dst = k.bf("AX", tt * 1024 + eh * 512, 512)
            k.copy("vector", dst, pv, tag=f"l3a_evv{tt}_{eh}")
    v_tok = [k.bf("AX", tt * 1024, 1024) for tt in range(16)]
    q_c = [k.bf("AD", c * 2048, 2048) for c in range(NCHUNK)]
    k_c = [k.bf("AD", 16384 + c * 2048, 2048) for c in range(NCHUNK)]

    emit_restore_x_to_AD_late = False
    for qt in range(16):
        kt_lo = max(0, qt - L3_BAND)
        pbf = k.bf("AW", 8192, 2048)
        kb_lo, kb_hi = kt_lo // 4, qt // 4
        for kb in range(kb_lo, kb_hi + 1):
            pv = k.psum(kb % 4)
            for ec in range(8):
                lhs = V(q_c[ec].ap[:, qt * 128:(qt + 1) * 128], q_c[ec].regs)
                r = V(k_c[ec].ap[:, kb * 512:(kb + 1) * 512], k_c[ec].regs)
                k.mm(pv, lhs, r, start=(ec == 0), stop=(ec == 7),
                     tag=f"l3a_s{qt}_{kb}_{ec}")
            for sub in range(4):
                kt = kb * 4 + sub
                if kt < kt_lo or kt > qt:
                    continue
                dlt = qt - kt
                cfac = float(GAMMA ** (128 * dlt))
                dst = V(pbf.ap[:, kt * 128:(kt + 1) * 128], pbf.regs)
                pslice = V(pv.ap[:, sub * 128:(sub + 1) * 128], pv.regs)
                if kt == qt:
                    k.tt("vector", dst, pslice, V(k.caus01, [("consts",)]), ALU.mult,
                         tag=f"l3a_sd{qt}")
                else:
                    k.ts("vector", dst, pslice, cfac, None, ALU.mult,
                         tag=f"l3a_se{qt}_{kt}")
        ptq = k.bf("AW", 10240, 2048)
        for kt in range(kt_lo, qt + 1):
            pvt = k.psum_bf(6 + (kt % 2), 0, 128)
            k.transpose(pvt, V(pbf.ap[:, kt * 128:(kt + 1) * 128], pbf.regs),
                        tag=f"l3a_tr{qt}_{kt}")
            k.copy("vector", V(ptq.ap[:, kt * 128:(kt + 1) * 128], ptq.regs), pvt,
                   tag=f"l3a_tre{qt}_{kt}")
        for do in range(8):
            pv = k.psum(do % 6)
            pv = V(pv.ap[:, 0:128], pv.regs)
            for j, kt in enumerate(range(kt_lo, qt + 1)):
                lhs = V(v_tok[kt].ap[:, do * 128:(do + 1) * 128], v_tok[kt].regs)
                r = V(ptq.ap[:, kt * 128:(kt + 1) * 128], ptq.regs)
                k.mm(pv, lhs, r, start=(j == 0), stop=(kt == qt),
                     tag=f"l3a_o{qt}_{do}_{kt}")
            ob = k.bf("AC", do * 2048 + qt * 128, 128)
            k.copy("vector", ob, pv, tag=f"l3a_oe{qt}_{do}")

    emit_restore_x_to_AD(k, tag="l3a_rs")
    for c in range(NCHUNK):
        k.tt(k.alt(), k.f32("AX", c * 2048, 2048), k.f32("AD", c * 2048, 2048),
             k.bf("AC", c * 2048, 2048), ALU.add, tag=f"l3a_fin{c}")


def prep_l3_attn(host, pc, inp):
    host.add("l3a_wq", pack_lhsT(np.asarray(inp["L3_attn_Wq"]).T))
    host.add("l3a_wk", pack_lhsT(np.asarray(inp["L3_attn_Wk"]).T))
    host.add("l3a_wv", pack_lhsT(np.asarray(inp["L3_attn_Wv"]).T, ow=1024))
    return {"wq": "l3a_wq", "wk": "l3a_wk", "wv": "l3a_wv",
            "bq": pc.add_vecD(inp["L3_attn_bq"]),
            "bk": pc.add_vecD(inp["L3_attn_bk"])}


def emit_l3_ffn(k, meta):
    xs = x_chunks(k)
    emit_ln(k, xs, out_f32_base=("AD", 0), out_bf_base=("AC", 0),
            xbf_base=("AC", 0), tag="l3f_ln")
    xn32 = [k.f32("AD", c * 2048, 2048) for c in range(NCHUNK)]
    xn = [k.bf("AC", c * 2048, 2048) for c in range(NCHUNK)]
    k.load_bp(meta["bp"], 296, tag="l3f_bp")
    gb = V(k.sc_t.ap()[0:1, 40:48], [("SC", 40)])
    k.dma(gb, dram_v(meta["gbias"]), tag="l3f_gb")
    wgs = k.f32("AW", 2048, 64)
    k.dma(wgs, dram_v(meta["wg"][0]), tag="l3f_wgld")
    glog = k.f32("AB", 0, 128)
    for tt in range(16):
        pv = k.psum(4 + tt % 2)
        pv = V(pv.ap[:, 0:8], pv.regs)
        for ic in range(NCHUNK):
            lhs = V(xn32[ic].ap[:, tt * 128:(tt + 1) * 128], xn32[ic].regs)
            r = V(wgs.ap[:, ic * 8:(ic + 1) * 8], wgs.regs)
            k.mm(pv, lhs, r, start=(ic == 0), stop=False, tag=f"l3f_g{tt}_{ic}")
        ones_row = V(k.consts.ap()[0:1, 0:128], [("consts",)])
        k.mm(pv, ones_row, gb, start=False, stop=True, tag=f"l3f_gb{tt}")
        k.act(V(glog.ap[:, tt * 8:(tt + 1) * 8], glog.regs), pv, AF.Sigmoid,
              tag=f"l3f_sig{tt}")
    f_all = k.f32("AB", 128, 128)
    for tt in range(16):
        g = V(glog.ap[:, tt * 8:(tt + 1) * 8], glog.regs)
        m1, m2, den, w1c, w2c = k.scol(24), k.scol(25), k.scol(26), k.scol(27), k.scol(28)
        eq1 = k.f32("AB", 256, 8)
        eq2 = k.f32("AB", 264, 8)
        k.reduce("vector", m1, g, ALU.max, tag=f"l3f_m1{tt}")
        k.ts("vector", eq1, g, m1, None, ALU.is_equal, tag=f"l3f_eq1{tt}")
        k.stt("vector", eq2, eq1, -60000.0, g, ALU.mult, ALU.add, tag=f"l3f_msk{tt}")
        k.reduce("vector", m2, eq2, ALU.max, tag=f"l3f_m2{tt}")
        k.ts("vector", eq2, eq2, m2, None, ALU.is_equal, tag=f"l3f_eq2{tt}")
        k.ts("vector", den, m1, m2, None, ALU.add, tag=f"l3f_d1{tt}")
        k.ts("vector", den, den, 1e-6, None, ALU.add, tag=f"l3f_d2{tt}")
        k.recip(den, den, tag=f"l3f_r{tt}")
        k.ts("vector", w1c, m1, den, None, ALU.mult, tag=f"l3f_w1{tt}")
        k.ts("vector", w2c, m2, den, None, ALU.mult, tag=f"l3f_w2{tt}")
        fv = V(f_all.ap[:, tt * 8:(tt + 1) * 8], f_all.regs)
        k.ts("vector", fv, eq1, w1c, None, ALU.mult, tag=f"l3f_f1{tt}")
        k.stt("vector", fv, eq2, w2c, fv, ALU.mult, ALU.add, tag=f"l3f_f2{tt}")
        k.ts("vector", fv, fv, 0.1, None, ALU.mult, tag=f"l3f_f01{tt}")
    fbf_all = V(k.sb["AB"].ap().bitcast(BF16)[:, 6656:6784], [("AB", 6)])
    k.copy("vector", fbf_all, f_all, tag="l3f_fcv")
    f_fm_bf = V(k.sb["AW"].ap().bitcast(BF16)[0:8, 10240:12288],
                [("AW", 10), ("AW", 11)])
    for tt in range(16):
        pvt = k.psum_bf(6, 0, 128)
        pvt = V(pvt.ap[0:8, :], pvt.regs)
        k.p.op("tensor", lambda e, o=pvt.ap, s=fbf_all.ap[:, tt * 8:(tt + 1) * 8]:
               e.transpose(o, s, k.ident_bf),
               reads=fbf_all.regs + [("consts",)], writes=pvt.regs, tag=f"l3f_tr{tt}")
        k.copy("vector", V(f_fm_bf.ap[:, tt * 128:(tt + 1) * 128], f_fm_bf.regs), pvt,
               tag=f"l3f_tre{tt}")
    emit_spill_x(k, tag="l3f_sp")
    for c in range(NCHUNK):
        k.memset(k.alt(), k.f32("AD", c * 2048, 2048), 0.0, tag=f"l3f_z{c}")
    for c in range(NCHUNK):
        k.ts("vector", k.scol(32 + c), V(k.bpcol(288 + c), [("BP",)]), 0.1, None,
             ALU.mult, tag=f"l3f_sbd{c}")

    def up_pass(wblocks, biasbase, dst_bf_base, rhs, tag):
        bi = 0
        for oc in range(32):
            wt = emit_stream_w(k, wblocks, oc, 0, 8, bi % 2, tag=tag)
            bi += 1
            for sl in range(2):
                pv = k.psum((oc % 2) * 2 + sl)
                for ic in range(NCHUNK):
                    k.mm(pv, wt[ic], rhs(ic, sl), start=(ic == 0), stop=(ic == 7),
                         tag=f"{tag}m{oc}_{sl}_{ic}")
                a = k.f32("AB", 2304 + (oc % 2) * 512, 512)
                k.act(a, pv, AF.Gelu, bias=V(k.bpcol(biasbase + oc), [("BP",)]),
                      tag=f"{tag}g{oc}_{sl}")
                hb = k.bf("AX", dst_bf_base + oc * 1024 + sl * 512, 512)
                k.copy(k.alt(), hb, a, tag=f"{tag}c{oc}_{sl}")

    def down_pass(wblocks, h_bf_base, th, evac, tag):
        h_e = [k.bf("AX", h_bf_base + ic * 1024, 1024) for ic in range(32)]
        bi2 = 0
        for ob in range(NCHUNK):
            for g in range(4):
                wt = emit_stream_w(k, wblocks, ob, g * 8, 8, bi2 % 2, tag=tag)
                bi2 += 1
                for sl in range(2):
                    pv = k.psum(5 + sl)
                    for i in range(8):
                        ic = g * 8 + i
                        r = V(h_e[ic].ap[:, sl * 512:(sl + 1) * 512], h_e[ic].regs)
                        k.mm(pv, wt[i], r, start=(ic == 0), stop=(ic == 31),
                             tag=f"{tag}m{ob}_{sl}_{ic}")
            for sl in range(2):
                evac(ob, sl, k.psum(5 + sl))

    for th in range(2):
        def rhs_th(ic, sl, th=th):
            return V(xn[ic].ap[:, th * 1024 + sl * 512:th * 1024 + (sl + 1) * 512],
                     xn[ic].regs)
        up_pass(meta["swu"], 256, 0, rhs_th, f"l3f_su{th}_")

        def ev_sh(ob, sl, pv, th=th):
            acc = k.f32("AD", ob * 2048 + th * 1024 + sl * 512, 512)
            k.stt("vector", acc, pv, 0.1, acc, ALU.mult, ALU.add,
                  tag=f"l3f_sde{th}_{ob}_{sl}")

        down_pass(meta["swd"], 0, th, ev_sh, f"l3f_sd{th}_")
        for e in range(8):
            frow = V(k.sb["AW"].ap().bitcast(BF16)[0:1, 2048:4096],
                     [("AW", 2), ("AW", 3)])
            if th == 0:
                pass
            for tt in range(16):
                pvt = k.psum_bf(7, 0, 128)
                pvt = V(pvt.ap[0:1, :], pvt.regs)
                k.p.op("tensor", lambda ee, o=pvt.ap,
                       s=fbf_all.ap[:, tt * 8 + e:tt * 8 + e + 1]:
                       ee.transpose(o, s, k.ident_bf),
                       reads=fbf_all.regs + [("consts",)], writes=pvt.regs,
                       tag=f"l3f_fr{th}_{e}_{tt}")
                k.copy("vector", V(frow.ap[:, tt * 128:(tt + 1) * 128], frow.regs), pvt,
                       tag=f"l3f_fre{th}_{e}_{tt}")
            fbc = k.f32("AW", 0, 1024)
            for sl in range(2):
                pv = k.psum(4)
                lhs = V(k.ones_bf[0:1, :], [("consts",)])
                r = V(frow.ap[:, th * 1024 + sl * 512:th * 1024 + (sl + 1) * 512],
                      frow.regs)
                k.mm(pv, lhs, r, start=True, stop=True, tag=f"l3f_fb{th}_{e}_{sl}")
                k.copy("vector", V(fbc.ap[:, sl * 512:(sl + 1) * 512], fbc.regs), pv,
                       tag=f"l3f_fbe{th}_{e}_{sl}")
            up_pass(meta["upw"][e], e * 32, 0, rhs_th, f"l3f_u{th}_{e}_")

            def ev_ex(ob, sl, pv, th=th):
                t1 = k.f32("AB", 2304 + sl * 512, 512)
                k.tt("vector", t1, pv, V(fbc.ap[:, sl * 512:(sl + 1) * 512], fbc.regs),
                     ALU.mult, tag=f"l3f_xf{th}_{e}_{ob}_{sl}")
                acc = k.f32("AD", ob * 2048 + th * 1024 + sl * 512, 512)
                k.tt("vector", acc, t1, acc, ALU.add, tag=f"l3f_xa{th}_{e}_{ob}_{sl}")

            down_pass(meta["dnw"][e], 0, th, ev_ex, f"l3f_d{th}_{e}_")
    for ob in range(NCHUNK):
        wt = emit_stream_w(k, meta["dbp"], ob, 0, 1, ob % 2, ow=128, kp=8, tag="l3f_db")
        for tsl in range(4):
            pv = k.psum(7)
            r = V(f_fm_bf.ap[:, tsl * 512:(tsl + 1) * 512], f_fm_bf.regs)
            k.mm(pv, wt[0], r, start=True, stop=True, tag=f"l3f_dbm{ob}_{tsl}")
            acc = k.f32("AD", ob * 2048 + tsl * 512, 512)
            k.tt("vector", acc, pv, acc, ALU.add, tag=f"l3f_dbe{ob}_{tsl}")
    for ob in range(NCHUNK):
        for tsl in range(4):
            acc = k.f32("AD", ob * 2048 + tsl * 512, 512)
            k.ts("vector", acc, acc, k.scol(32 + ob), None, ALU.add,
                 tag=f"l3f_sba{ob}_{tsl}")
    m_all = k.f32("AB", 0, 2048)
    k.act(m_all, k.f32("AD", 0, 2048), AF.Abs, tag="l3f_ab0")
    for c in range(1, NCHUNK):
        t_abs = k.f32("AW", 4096, 2048)
        k.act(t_abs, k.f32("AD", c * 2048, 2048), AF.Abs, tag=f"l3f_ab{c}")
        k.tt("vector", m_all, m_all, t_abs, ALU.max, tag=f"l3f_am{c}")
    mbf = V(k.sb["AW"].ap().bitcast(BF16)[:, 8192:10240], [("AW", 8), ("AW", 9)])
    k.copy("vector", mbf, m_all, tag="l3f_mbf")
    rrow_bf = V(k.sb["AW"].ap().bitcast(BF16)[0:1, 10240:12288],
                [("AW", 10), ("AW", 11)])
    rcol_bf = V(k.sb["AB"].ap().bitcast(BF16)[:, 6656:6657], [("AB", 6)])
    for tt in range(16):
        pvt = k.psum_bf(6, 0, 128)
        k.transpose(pvt, V(mbf.ap[:, tt * 128:(tt + 1) * 128], mbf.regs),
                    tag=f"l3f_mt{tt}")
        rmax = k.scol(48)
        k.reduce("vector", rmax, pvt, ALU.max, tag=f"l3f_rm{tt}")
        k.ts("vector", rmax, rmax, 1e-6, None, ALU.add, tag=f"l3f_re{tt}")
        k.recip(rmax, rmax, tag=f"l3f_rr{tt}")
        k.copy("vector", rcol_bf, rmax, tag=f"l3f_rc{tt}")
        pv2 = k.psum_bf(7, 0, 128)
        pv2 = V(pv2.ap[0:1, :], pv2.regs)
        k.p.op("tensor", lambda e, o=pv2.ap, s=rcol_bf.ap:
               e.transpose(o, s, k.ident_bf),
               reads=rcol_bf.regs + [("consts",)], writes=pv2.regs, tag=f"l3f_rt{tt}")
        k.copy("vector", V(rrow_bf.ap[:, tt * 128:(tt + 1) * 128], rrow_bf.regs), pv2,
               tag=f"l3f_rre{tt}")
    rb = k.f32("AW", 0, 2048)
    for tsl in range(4):
        pv = k.psum(4)
        k.mm(pv, V(k.ones_bf[0:1, :], [("consts",)]),
             V(rrow_bf.ap[:, tsl * 512:(tsl + 1) * 512], rrow_bf.regs),
             start=True, stop=True, tag=f"l3f_rb{tsl}")
        k.copy("vector", V(rb.ap[:, tsl * 512:(tsl + 1) * 512], rb.regs), pv,
               tag=f"l3f_rbe{tsl}")
    for c in range(NCHUNK):
        xsp = k.f32("AB", 0, 2048)
        k.dma(xsp, V(k.xsp[c * 128:(c + 1) * 128, :], [("dram", "xsp")]),
              tag=f"l3f_xr{c}")
        t = k.f32("AW", 4096, 2048)
        k.tt("vector", t, k.f32("AD", c * 2048, 2048), rb, ALU.mult, tag=f"l3f_n{c}")
        k.tt(k.alt(), k.f32("AX", c * 2048, 2048), t, xsp, ALU.add, tag=f"l3f_o{c}")


def prep_l3_ffn(host, pc, inp):
    upw = np.asarray(inp["L3_ffn_up_w"], np.float32)
    dnw = np.asarray(inp["L3_ffn_down_w"], np.float32)
    host.add("l3f_upw", np.stack([pack_lhsT(upw[e]) for e in range(8)]))
    host.add("l3f_dnw", np.stack([pack_lhsT(dnw[e]) for e in range(8)]))
    host.add("l3f_swu", pack_lhsT(np.asarray(inp["L3_ffn_sw_up"], np.float32)))
    host.add("l3f_swd", pack_lhsT(np.asarray(inp["L3_ffn_sw_down"], np.float32)))
    host.add("l3f_wg", pack_lhsT(np.asarray(inp["L3_ffn_gate_w"], np.float32), ow=8))
    host.add("l3f_dbp", pack_lhsT(np.asarray(inp["L3_ffn_down_b"], np.float32)))
    host.add("l3f_gbias", np.asarray(inp["L3_ffn_bias"], np.float32).reshape(1, 8))
    up_b = np.asarray(inp["L3_ffn_up_b"], np.float32)
    sb_up = np.asarray(inp["L3_ffn_sb_up"], np.float32)
    sb_down = np.asarray(inp["L3_ffn_sb_down"], np.float32)
    bp = np.zeros((128, 296), np.float32)
    for e in range(8):
        for c in range(32):
            bp[:, e * 32 + c] = up_b[e, c * 128:(c + 1) * 128]
    for c in range(32):
        bp[:, 256 + c] = sb_up[c * 128:(c + 1) * 128]
    for c in range(8):
        bp[:, 288 + c] = sb_down[c * 128:(c + 1) * 128]
    host.add("l3f_bpa", bp)
    return {"upw": "l3f_upw", "dnw": "l3f_dnw", "swu": "l3f_swu", "swd": "l3f_swd",
            "wg": "l3f_wg", "dbp": "l3f_dbp", "gbias": "l3f_gbias", "bp": "l3f_bpa"}



def build_slot_program(slots, host_arrays, param_array, want_trace=False):
    nc = bass.Bass("TRN2", target_bir_lowering=False)
    d = {}
    for name, arr in host_arrays.items():
        d[name] = nc.dram_tensor(name, list(arr.shape), F32, kind="ExternalInput").ap()
    d_y = nc.dram_tensor("y_fm", [D, T], F32, kind="ExternalOutput").ap()
    prog = Prog(nc)
    k = Ctx(nc, prog)
    k.xsp = nc.dram_tensor("xsp", [D, T], F32, kind="Internal").ap()
    load_basics_ops(k, d["consts_t"], d["params_t"], param_array.shape[1])
    emit_load_x(k, d["x_fm"])
    for emit_fn, meta in slots:
        emit_fn(k, {key: (d[v] if isinstance(v, str) and v in d else v)
                    for key, v in meta.items()})
    emit_store_x(k, d_y, ch="gpsimd")
    prog.build()
    k.close()
    return nc


def load_basics_ops(k, d_consts, d_params, n_param_cols):
    k.p.op("sync", lambda e: e.dma_start(out=k.consts.ap()[:, :], in_=d_consts),
           writes=[("consts",)], dma=True, tag="ld_consts")
    k.p.op("sync", lambda e, n=n_param_cols: e.dma_start(
        out=k.params_t.ap()[:, 0:n], in_=d_params[:, 0:n]),
        writes=[("params",)], dma=True, tag="ld_params")



def prep_l0_attn(host, pc, inp):
    wifzo = np.concatenate([inp["L0_attn_Wi"], inp["L0_attn_Wf"],
                            inp["L0_attn_Wz"], inp["L0_attn_Wo"]], axis=0)
    host.add("l0a_wifzo", pack_lhsT(wifzo.T, ow=64))
    host.add("l0a_wp", pack_lhsT(np.asarray(inp["L0_attn_Wp"]).T))
    meta = {
        "wifzo": "l0a_wifzo", "wp": "l0a_wp",
        "ln_w": pc.add_vecD(inp["L0_attn_ln_w"]),
        "ln_b": pc.add_vecD(inp["L0_attn_ln_b"]),
        "b_i": pc.add(inp["L0_attn_bi"]),
        "b_f": pc.add(inp["L0_attn_bf"]),
        "b_z": pc.add(inp["L0_attn_bz"]),
        "b_o": pc.add(inp["L0_attn_bo"]),
        "bp": pc.add_vecD(inp["L0_attn_bp"]),
    }
    return meta


SLOT_PREP = {"l0_attn": (prep_l0_attn, emit_l0_attn),
             "l0_ffn": (prep_l0_ffn, emit_l0_ffn),
             "l1_attn": (prep_l1_attn, emit_l1_attn),
             "l1_ffn": (prep_l1_ffn, emit_l1_ffn),
             "l2_attn": (prep_l2_attn, emit_l2_attn),
             "l2_ffn": (prep_l2_ffn, emit_l2_ffn),
             "l3_attn": (prep_l3_attn, emit_l3_attn),
             "l3_ffn": (prep_l3_ffn, emit_l3_ffn)}


def run_slots(slot_names, inputs, x_fm_percore):
    host = Host()
    pc = ParamCols()
    slots = []
    for s in slot_names:
        prep, emit = SLOT_PREP[s]
        meta = prep(host, pc, inputs)
        slots.append((emit, meta))
    param_array = pc.array()
    host.add("consts_t", make_consts())
    host.add("params_t", param_array)
    host.add("x_fm", x_fm_percore[0])
    nc = build_slot_program(slots, host.arrays, param_array)
    in_maps = []
    for core in range(8):
        m = dict(host.arrays)
        m["x_fm"] = x_fm_percore[core]
        in_maps.append(m)
    res = run_bass_kernel_spmd(nc, in_maps, core_ids=list(range(8)))
    return [r["y_fm"] for r in res.results]


ALL_SLOTS = ["l0_attn", "l0_ffn", "l1_attn", "l1_ffn", "l2_attn", "l2_ffn",
             "l3_attn", "l3_ffn"]


def kernel(**inputs):
    x = np.asarray(inputs["L0_attn_x"], np.float32)
    x_fm = [np.ascontiguousarray(x[b].T) for b in range(8)]
    outs = run_slots(ALL_SLOTS, inputs, x_fm)
    return np.stack([np.asarray(o).T for o in outs]).astype(np.float32)


# version 03
# speedup vs baseline: 1.1411x; 1.1411x over previous
import numpy as np
import ml_dtypes
import concourse.bass as bass
import concourse.mybir as mybir
from concourse.bass_utils import run_bass_kernel_spmd

F32 = mybir.dt.float32
BF16 = mybir.dt.bfloat16
AF = mybir.ActivationFunctionType
ALU = mybir.AluOpType
AXL = mybir.AxisListType
BF_NP = ml_dtypes.bfloat16

B, T, D = 8, 2048, 1024
DBG_SINK = {}
NCHUNK = D // 128
NTSL = T // 512
EPS = 1e-5



class Prog:

    def __init__(self, nc):
        self.nc = nc
        self.ops = []
        self.last_write = {}
        self.readers = {}

    def op(self, engine, emit, reads=(), writes=(), dma=False, tag="", slow=False,
           width=0):
        idx = len(self.ops)
        deps = set()
        for r in reads:
            w = self.last_write.get(r)
            if w is not None:
                deps.add(w)
        for r in writes:
            w = self.last_write.get(r)
            if w is not None:
                deps.add(w)
            for rd in self.readers.get(r, ()):
                deps.add(rd)
        for r in writes:
            self.last_write[r] = idx
            self.readers[r] = []
        for r in reads:
            self.readers.setdefault(r, []).append(idx)
        self.ops.append(dict(engine=engine, emit=emit, deps=sorted(deps),
                             dma=dma, tag=tag, idx=idx, slow=slow, width=width))
        return idx

    def build(self):
        nc = self.nc
        ops = self.ops
        n = len(ops)

        def domain(o):
            return ("dma_" + o["engine"]) if o["dma"] else o["engine"]

        def skip_edge(po, o):
            if po["dma"] or o["dma"] or po["engine"] != o["engine"]:
                return False
            if po["engine"] == "tensor":
                return True
            return (not po["slow"]) and po["width"] >= 512 and o["width"] >= 512

        needed = [False] * n
        for o in ops:
            for d in o["deps"]:
                if skip_edge(ops[d], o):
                    continue
                needed[d] = True
        for o in ops:
            if o["dma"]:
                needed[o["idx"]] = True
        counters = {}
        val = [0] * n
        for o in ops:
            if needed[o["idx"]]:
                d = domain(o)
                counters[d] = counters.get(d, 0) + (16 if o["dma"] else 1)
                val[o["idx"]] = counters[d]
        sem_ctx = {}
        sems = []
        for d in sorted(counters.keys()):
            cm = nc.semaphore("sem_" + d)
            sem_ctx[d] = cm.__enter__()
            sems.append(cm)
        waited = {}
        plans = {}
        for o in ops:
            plans.setdefault(o["engine"], [])
        for o in ops:
            wl = {}
            for d in o["deps"]:
                po = ops[d]
                if skip_edge(po, o):
                    continue
                pd = domain(po)
                wl[pd] = max(wl.get(pd, 0), val[po["idx"]])
            eng = o["engine"]
            final = []
            for pd, v in sorted(wl.items()):
                kk = (eng, pd)
                if waited.get(kk, 0) >= v:
                    continue
                waited[kk] = v
                final.append((pd, v))
            plans[eng].append((o, final, val[o["idx"]] if needed[o["idx"]] else None))
        term = {}
        for d, c in counters.items():
            if d.startswith("dma_"):
                term[d[4:]] = (d, c)
        for ch in list(term):
            if ch not in plans:
                plans[ch] = []

        self.trace = []
        for eng in plans:
            for (o, waits, myval) in plans[eng]:
                self.trace.append((o["idx"], eng, o["tag"], waits,
                                   myval, ("dma" if o["dma"] else "")))
        self.trace.sort()
        block_cm = nc.Block()
        block = block_cm.__enter__()
        for eng in plans:
            def body(e, eng=eng):
                for (o, waits, myval) in plans[eng]:
                    for (pd, v) in waits:
                        e.wait_ge(sem_ctx[pd], v)
                    inst = o["emit"](e)
                    if myval is not None:
                        assert inst is not None, f"op {o['tag']} needs inc, emitted None"
                        inst.then_inc(sem_ctx[("dma_" + eng) if o["dma"] else eng],
                                      16 if o["dma"] else 1)
                if eng in term:
                    pd, c = term[eng]
                    e.wait_ge(sem_ctx[pd], c)
            getattr(block, eng)(body)
        block_cm.__exit__(None, None, None)
        for cm in sems:
            cm.__exit__(None, None, None)



class Host:
    def __init__(self):
        self.arrays = {}

    def add(self, name, arr):
        self.arrays[name] = np.ascontiguousarray(arr)
        return name


class ParamCols:

    def __init__(self):
        self.cols = []

    def add(self, vec):
        v = np.zeros(128, np.float32)
        v[: len(vec)] = np.asarray(vec, np.float32).reshape(-1)
        self.cols.append(v)
        return len(self.cols) - 1

    def add_vecD(self, vecD):
        base = len(self.cols)
        for c in range(NCHUNK):
            self.add(np.asarray(vecD, np.float32).reshape(-1)[c * 128:(c + 1) * 128])
        return base

    def array(self):
        if not self.cols:
            return np.zeros((128, 1), np.float32)
        return np.stack(self.cols, axis=1).astype(np.float32)


def pack_bf16_as_f32(bfarr):
    raw = np.ascontiguousarray(bfarr.astype(BF_NP)).view(np.uint16)
    lo = raw[:, 0::2].astype(np.uint32)
    hi = raw[:, 1::2].astype(np.uint32)
    return ((hi << 16) | lo).view(np.float32)


def pack_lhsT(w_in_out, ow=128):
    w = np.asarray(w_in_out, np.float32)
    n_in, n_out = w.shape
    kp = min(n_in, 128)
    assert n_in % kp == 0 and n_out % ow == 0
    n_ic, n_ob = n_in // kp, n_out // ow
    a = w.reshape(n_ic, kp, n_ob, ow)
    return np.ascontiguousarray(a.transpose(2, 1, 0, 3).reshape(n_ob, kp, n_ic * ow))


CONSTF = [1e-5, 1e-6, 0.5, -0.5, 1.0, -1.0, 20.0, -20.0, 2.0]


GAMMA = 0.96875


def make_consts():
    bf = np.zeros((128, 256), BF_NP)
    bf[:, 0:128] = BF_NP(1.0)
    bf[:, 128:256] = np.eye(128, dtype=BF_NP)
    out = np.zeros((128, 1536), np.float32)
    out[:, 0:128] = 1.0
    out[:, 128:256] = pack_bf16_as_f32(bf)
    for i, v in enumerate(CONSTF):
        out[:, 256 + i] = v
    ii = np.arange(128)
    out[:, 320:448] = np.where(ii[None, :] <= ii[:, None], 0.0, -2000.0).astype(np.float32)
    sw = np.zeros((128, 128), BF_NP)
    sw[ii, ii ^ 1] = BF_NP(1.0)
    out[:, 448:512] = pack_bf16_as_f32(sw)
    j = np.arange(128)
    out = np.concatenate([out, np.zeros((128, 128), np.float32)], axis=1)
    out[:, 512:640] = (GAMMA ** j)[None, :].astype(np.float32)
    out[:, 640:768] = (GAMMA ** (-j.astype(np.float64)))[None, :].astype(np.float32)
    out[:, 768:896] = (ii[None, :] <= ii[:, None]).astype(np.float32)
    return np.ascontiguousarray(out[:, 0:896])



ARENAS = [("AX", 16384), ("AD", 16384), ("AC", 8192), ("AB", 4096), ("AW", 6144)]
ARENA_REG = {"AX": 2048, "AD": 1024, "AC": 1024, "AB": 512, "AW": 512}


class V:
    __slots__ = ("ap", "regs")

    def __init__(self, ap, regs):
        self.ap = ap
        self.regs = regs


class Ctx:
    def __init__(self, nc, prog):
        self.nc = nc
        self.p = prog
        self.stack = []
        self.sb = {}
        for nm, cols in ARENAS:
            cm = nc.sbuf_tensor(nm, [128, cols], F32)
            self.stack.append(cm)
            self.sb[nm] = cm.__enter__()
        self.pb = []
        for i in range(8):
            cm = nc.psum_tensor(f"pb{i}", [128, 512], F32)
            self.stack.append(cm)
            self.pb.append(cm.__enter__())
        cm = nc.sbuf_tensor("consts", [128, 896], F32)
        self.stack.append(cm)
        self.consts = cm.__enter__()
        for i, v in enumerate(CONSTF):
            nc.const_aps.register(F32, v, self.consts.ap()[:, 256 + i:257 + i])
        cm = nc.sbuf_tensor("params", [128, 128], F32)
        self.stack.append(cm)
        self.params_t = cm.__enter__()
        cm = nc.sbuf_tensor("SC", [128, 64], F32)
        self.stack.append(cm)
        self.sc_t = cm.__enter__()
        cm = nc.sbuf_tensor("BP", [128, 512], F32)
        self.stack.append(cm)
        self.bp_t = cm.__enter__()
        self.rr = 0

    def bpcol(self, col):
        return self.bp_t.ap()[:, col:col + 1]

    def load_bp(self, d_bp, ncols, tag="ldbp"):
        self.p.op("sync", lambda e, d=d_bp, n=ncols: e.dma_start(
            out=self.bp_t.ap()[:, 0:n], in_=d), writes=[("BP",)], dma=True, tag=tag)

    def scol(self, col, n=1):
        return V(self.sc_t.ap()[:, col:col + n], [("SC", col)])

    def close(self):
        for cm in reversed(self.stack):
            cm.__exit__(None, None, None)

    def f32(self, name, col0, ncols):
        g = ARENA_REG[name]
        regs = [(name, i) for i in range(col0 // g, (col0 + ncols + g - 1) // g)]
        return V(self.sb[name].ap()[:, col0:col0 + ncols], regs)

    def bf(self, name, bcol0, nbcols):
        g = ARENA_REG[name]
        c0, c1 = bcol0 // 2, (bcol0 + nbcols + 1) // 2
        regs = [(name, i) for i in range(c0 // g, (c1 + g - 1) // g)]
        return V(self.sb[name].ap().bitcast(BF16)[:, bcol0:bcol0 + nbcols], regs)

    def psum(self, i, ncols=512):
        return V(self.pb[i].ap()[:, 0:ncols], [(f"pb{i}",)])

    def psum_bf(self, i, bcol0, nbcols):
        return V(self.pb[i].ap().bitcast(BF16)[:, bcol0:bcol0 + nbcols], [(f"pb{i}",)])

    def pcol(self, col):
        return self.params_t.ap()[:, col:col + 1]

    @property
    def ones_bf(self):
        return self.consts.ap().bitcast(BF16)[:, 256:384]

    @property
    def ident_bf(self):
        return self.consts.ap().bitcast(BF16)[:, 384:512]

    @property
    def causal_mask(self):
        return self.consts.ap()[:, 320:448]

    @property
    def swap_bf(self):
        return self.consts.ap().bitcast(BF16)[:, 896:1024]

    def gamma_q(self, w):
        assert w <= 128
        return self.consts.ap()[:, 512:512 + w]

    def gamma_k(self, w):
        assert w <= 128
        return self.consts.ap()[:, 640:640 + w]

    @property
    def caus01(self):
        return self.consts.ap()[:, 768:896]

    def alt(self, engines=("vector", "gpsimd")):
        self.rr += 1
        return engines[self.rr % len(engines)]

    def dma(self, dst, src, ch="sync", tag="dma"):
        dv, dr = (dst.ap, dst.regs) if isinstance(dst, V) else (dst, [("dram", dst.tensor.name)])
        sv, sr = (src.ap, src.regs) if isinstance(src, V) else (src, [("dram", src.tensor.name)])
        return self.p.op(ch, lambda e, a=dv, b=sv: e.dma_start(out=a, in_=b),
                         reads=sr, writes=dr, dma=True, tag=tag)

    def tt(self, eng, out, a, b, op, tag="tt"):
        return self.p.op(eng, lambda e, o=out.ap, x=a.ap, y=b.ap: e.tensor_tensor(o, x, y, op),
                         reads=a.regs + b.regs, writes=out.regs, tag=tag,
                         width=out.ap.shape[-1])

    @staticmethod
    def _sc(s):
        if s is None or np.isscalar(s):
            return s, []
        if isinstance(s, V):
            return s.ap, list(s.regs)
        return s, [("params",)]

    def ts(self, eng, out, a, s1, s2, op0, op1=ALU.bypass, accum=None, tag="ts"):
        s1v, r1 = self._sc(s1)
        s2v, r2 = self._sc(s2)
        wr = out.regs + (accum.regs if accum is not None else [])
        return self.p.op(eng, lambda e, o=out.ap, x=a.ap: e.tensor_scalar(
            o, x, s1v, s2v, op0, op1, accum_out=(accum.ap if accum is not None else None)),
            reads=a.regs + r1 + r2, writes=wr, tag=tag, width=out.ap.shape[-1])

    def stt(self, eng, out, a, s, b, op0, op1, tag="stt"):
        if eng == "gpsimd":
            eng = "vector"
        sv, r1 = self._sc(s)
        return self.p.op(eng, lambda e, o=out.ap, x=a.ap, y=b.ap: e.scalar_tensor_tensor(
            o, x, sv, y, op0, op1), reads=a.regs + b.regs + r1, writes=out.regs,
            tag=tag, width=out.ap.shape[-1])

    def act(self, out, a, func, bias=0.0, scale=1.0, accum=None, tag="act"):
        bv, r1 = self._sc(bias)
        sv, r2 = self._sc(scale)
        extra = [("consts",)] + r1 + r2
        wr = out.regs + (accum.regs if accum is not None else [])
        return self.p.op("scalar", lambda e, o=out.ap, x=a.ap: e.activation(
            o, x, func, bias=bv, scale=sv,
            accum_out=(accum.ap if accum is not None else None)),
            reads=a.regs + extra, writes=wr, tag=tag, width=out.ap.shape[-1])

    def copy(self, eng, out, a, tag="copy"):
        if eng == "scalar":
            return self.p.op(eng, lambda e, o=out.ap, x=a.ap: e.copy(o, x),
                             reads=a.regs, writes=out.regs, tag=tag,
                             width=out.ap.shape[-1])
        return self.p.op(eng, lambda e, o=out.ap, x=a.ap: e.tensor_copy(o, x),
                         reads=a.regs, writes=out.regs, tag=tag,
                         width=out.ap.shape[-1])

    def mm(self, psum, lhsT, rhs, start, stop, tag="mm"):
        return self.p.op("tensor", lambda e, o=psum.ap, l=lhsT.ap, r=rhs.ap:
                         e.matmul(o, l, r, start=start, stop=stop),
                         reads=lhsT.regs + rhs.regs, writes=psum.regs, tag=tag,
                         width=psum.ap.shape[-1])

    def transpose(self, psum, src, tag="tr"):
        cregs = [("consts",)]
        return self.p.op("tensor", lambda e, o=psum.ap, s=src.ap:
                         e.transpose(o, s, self.ident_bf),
                         reads=src.regs + cregs, writes=psum.regs, tag=tag)

    def reduce(self, eng, out, a, op, absval=False, tag="red"):
        return self.p.op(eng, lambda e, o=out.ap, x=a.ap: e.tensor_reduce(
            o, x, AXL.X, op, apply_absolute_value=absval),
            reads=a.regs, writes=out.regs, tag=tag, slow=True)

    def recip(self, out, a, tag="recip"):
        return self.p.op("vector", lambda e, o=out.ap, x=a.ap: e.reciprocal(o, x),
                         reads=a.regs, writes=out.regs, tag=tag, slow=True)

    def scan(self, out, d0, d1, init, op0, op1, tag="scan"):
        return self.p.op("vector", lambda e, o=out.ap, a=d0.ap, b=d1.ap:
                         e.tensor_tensor_scan(o, a, b, init, op0, op1),
                         reads=d0.regs + d1.regs, writes=out.regs, tag=tag, slow=True)

    def memset(self, eng, out, v, tag="memset"):
        return self.p.op(eng, lambda e, o=out.ap: e.memset(o, v),
                         reads=[], writes=out.regs, tag=tag, slow=True)

    def vconst(self, v):
        return V(v, [("params",)])


def dram_v(ap):
    return V(ap, [("dram", ap.tensor.name)])



def x_chunks(k):
    return [k.f32("AX", c * 2048, 2048) for c in range(NCHUNK)]


def emit_load_x(k, d_x):
    for c in range(NCHUNK):
        dst = k.f32("AX", c * 2048, 2048)
        k.dma(dst, dram_v(d_x[c * 128:(c + 1) * 128, :]), tag=f"ldx{c}")


def emit_store_x(k, d_y, ch="gpsimd"):
    for c in range(NCHUNK):
        src = k.f32("AX", c * 2048, 2048)
        k.dma(dram_v(d_y[c * 128:(c + 1) * 128, :]), src, ch=ch, tag=f"stx{c}")


def emit_ln(k, src, out_bf_base=None, out_f32_base=None, affine=None,
            xbf_base=("AC", 0), tag="ln"):
    xbf = []
    for c in range(NCHUNK):
        dst = k.bf(xbf_base[0], xbf_base[1] + c * 2048, 2048)
        k.copy(k.alt(), dst, src[c], tag=f"{tag}_xbf{c}")
        xbf.append(dst)
    for c in range(NCHUNK):
        sq = k.bf("AW", 8192 + (c % 2) * 2048, 2048)
        k.act(sq, src[c], AF.Square, tag=f"{tag}_sq{c}")
        for tsl in range(NTSL):
            pbv = k.psum(4 + tsl)
            k.p.op("tensor", lambda e, o=pbv.ap, s=sq.ap, t=tsl, st=(c == 0), sp=(c == NCHUNK - 1):
                   e.matmul(o, k.ones_bf, s[:, t * 512:(t + 1) * 512], start=st, stop=sp),
                   reads=sq.regs + [("consts",)], writes=pbv.regs, tag=f"{tag}_mmq{c}_{tsl}")
    for c in range(NCHUNK):
        for tsl in range(NTSL):
            pbv = k.psum(tsl)
            k.p.op("tensor", lambda e, o=pbv.ap, s=xbf[c].ap, t=tsl, st=(c == 0), sp=(c == NCHUNK - 1):
                   e.matmul(o, k.ones_bf, s[:, t * 512:(t + 1) * 512], start=st, stop=sp),
                   reads=xbf[c].regs + [("consts",)], writes=pbv.regs, tag=f"{tag}_mms{c}_{tsl}")
    mu = k.f32("AB", 0, 2048)
    rstd = k.f32("AB", 2048, 2048)
    for tsl in range(NTSL):
        musl = k.f32("AB", tsl * 512, 512)
        rsl = k.f32("AB", 2048 + tsl * 512, 512)
        k.ts("vector", musl, k.psum(tsl), 1.0 / D, None, ALU.mult, tag=f"{tag}_mu{tsl}")
        k.act(rsl, musl, AF.Square, tag=f"{tag}_mu2_{tsl}")
        k.stt("vector", rsl, k.psum(4 + tsl), 1.0 / D, rsl, ALU.mult, ALU.subtract,
              tag=f"{tag}_var{tsl}")
        k.act(rsl, rsl, AF.Ln, bias=EPS, tag=f"{tag}_lnv{tsl}")
        k.act(rsl, rsl, AF.Exp, scale=-0.5, tag=f"{tag}_rstd{tsl}")
    for c in range(NCHUNK):
        scr = k.f32("AW", (c % 2) * 4096, 2048)
        k.tt("vector", scr, src[c], mu, ALU.subtract, tag=f"{tag}_sub{c}")
        if affine is None:
            if out_f32_base is not None:
                dstf = k.f32(out_f32_base[0], out_f32_base[1] + c * 2048, 2048)
                k.tt("vector", dstf, scr, rstd, ALU.mult, tag=f"{tag}_mulf{c}")
                if out_bf_base is not None:
                    dstb = k.bf(out_bf_base[0], out_bf_base[1] + c * 2048, 2048)
                    k.copy(k.alt(("gpsimd", "scalar")), dstb, dstf, tag=f"{tag}_cpb{c}")
            elif out_bf_base is not None:
                dstb = k.bf(out_bf_base[0], out_bf_base[1] + c * 2048, 2048)
                k.tt("vector", dstb, scr, rstd, ALU.mult, tag=f"{tag}_mulb{c}")
        else:
            wcol, bcol = affine
            scr2 = k.f32("AW", ((c + 1) % 2) * 4096, 2048)
            k.tt("vector", scr2, scr, rstd, ALU.mult, tag=f"{tag}_mul{c}")
            dstb = k.bf(out_bf_base[0], out_bf_base[1] + c * 2048, 2048)
            k.ts("vector", dstb, scr2, k.pcol(wcol + c), k.pcol(bcol + c),
                 ALU.mult, ALU.add, tag=f"{tag}_aff{c}")
    return mu, rstd


def emit_stream_w(k, d_wblocks, oc, g, gi, buf, ow=128, kp=128, neg=False, tag="w"):
    assert gi * ow <= 1024
    stage = k.f32("AW", 2048 + (buf % 2) * 1024, gi * ow)
    stage = V(stage.ap[0:kp, :], stage.regs)
    k.dma(stage, dram_v(d_wblocks[oc, :, g * ow:(g + gi) * ow]), tag=f"{tag}_ld{oc}_{g}")
    wbfv = k.bf("AW", 8192 + (buf % 2) * 1024, gi * ow)
    wbfv = V(wbfv.ap[0:kp, :], wbfv.regs)
    if neg:
        k.ts("vector", wbfv, stage, -1.0, None, ALU.mult, tag=f"{tag}_cv{oc}_{g}")
    else:
        k.copy(k.alt(("vector", "scalar")), wbfv, stage, tag=f"{tag}_cv{oc}_{g}")
    return [V(wbfv.ap[:, i * ow:(i + 1) * ow], wbfv.regs) for i in range(gi)]


def emit_proj(k, d_wblocks, n_ic, oc_list, rhs, evac,
              psum_sets=((0, 1, 2, 3), (4, 5, 6, 7)), ntsl=NTSL, ow=128, kp=128,
              tag="proj", buf0=0):
    bi = buf0
    gmax = 1024 // ow
    for j, oc in enumerate(oc_list):
        pset = psum_sets[j % len(psum_sets)]
        for g in range(0, n_ic, gmax):
            gi = min(gmax, n_ic - g)
            wt = emit_stream_w(k, d_wblocks, oc, g, gi, bi % 2, ow=ow, kp=kp, tag=tag)
            bi += 1
            for tsl in range(ntsl):
                pv = k.psum(pset[tsl])
                pv = V(pv.ap[0:ow, :], pv.regs)
                for i in range(gi):
                    ic = g + i
                    k.mm(pv, wt[i], rhs(ic, tsl), start=(ic == 0), stop=(ic == n_ic - 1),
                         tag=f"{tag}_mm{oc}_{tsl}_{ic}")
        for tsl in range(ntsl):
            pv = k.psum(pset[tsl])
            evac(oc, tsl, V(pv.ap[0:ow, :], pv.regs))
    return bi


def emit_l0_attn(k, meta):
    xs = x_chunks(k)
    emit_ln(k, xs, out_f32_base=("AD", 0), xbf_base=("AC", 0), tag="l0a_ln1")
    xn0 = [k.f32("AD", c * 2048, 2048) for c in range(NCHUNK)]
    emit_ln(k, xn0, out_bf_base=("AC", 0), affine=(meta["ln_w"], meta["ln_b"]),
            xbf_base=("AC", 0), tag="l0a_ln2")
    xn = [k.bf("AC", c * 2048, 2048) for c in range(NCHUNK)]

    def r64(v):
        return V(v.ap[0:64, :], v.regs)

    li = r64(k.f32("AB", 0, 2048))
    lf = r64(k.f32("AB", 2048, 2048))
    zt = r64(k.f32("AD", 4096, 2048))
    ot = r64(k.f32("AD", 6144, 2048))
    mt = r64(k.f32("AD", 8192, 2048))
    mprev = r64(k.f32("AD", 10240, 2048))
    fs = r64(k.f32("AD", 12288, 2048))
    isz = r64(k.f32("AD", 14336, 2048))
    dsts = [li, lf, zt, ot]
    bcols = [meta["b_i"], meta["b_f"], meta["b_z"], meta["b_o"]]

    def rhs(ic, tsl):
        return V(xn[ic].ap[:, tsl * 512:(tsl + 1) * 512], xn[ic].regs)

    def evac(oc, tsl, pv):
        dst = V(dsts[oc].ap[:, tsl * 512:(tsl + 1) * 512], dsts[oc].regs)
        bias = k.pcol(bcols[oc])[0:64, :]
        if oc == 0 or oc == 1:
            k.ts("vector", dst, pv, bias, None, ALU.add, tag=f"l0a_b{oc}_{tsl}")
            k.ts("vector", dst, dst, 20.0, -20.0, ALU.min, ALU.max,
                 tag=f"l0a_clip{oc}_{tsl}")
        elif oc == 2:
            k.act(dst, pv, AF.Tanh, bias=bias, tag=f"l0a_z{tsl}")
        else:
            k.act(dst, pv, AF.Sigmoid, bias=bias, tag=f"l0a_o{tsl}")

    emit_proj(k, meta["wifzo"], 8, [0, 1, 2, 3], rhs, evac, ow=64, tag="l0a_p")

    k.scan(mt, lf, li, 0.0, ALU.add, ALU.max, tag="l0a_mscan")
    k.memset("gpsimd", V(mprev.ap[:, 0:1], mprev.regs), 0.0, tag="l0a_mp0")
    k.copy("gpsimd", V(mprev.ap[:, 1:2048], mprev.regs), V(mt.ap[:, 0:2047], mt.regs),
           tag="l0a_mprev")
    k.tt("vector", fs, mprev, mt, ALU.subtract, tag="l0a_d1")
    k.tt("vector", fs, fs, lf, ALU.add, tag="l0a_d1b")
    k.act(fs, fs, AF.Exp, tag="l0a_fs")
    k.tt("vector", isz, li, mt, ALU.subtract, tag="l0a_d2")
    k.act(isz, isz, AF.Exp, tag="l0a_is")
    k.tt("vector", isz, isz, zt, ALU.mult, tag="l0a_isz")
    ct = li
    k.scan(ct, fs, isz, 0.0, ALU.mult, ALU.add, tag="l0a_cscan")
    th = zt
    k.act(th, ct, AF.Tanh, tag="l0a_tanh")
    hbf = V(k.sb["AD"].ap().bitcast(BF16)[0:64, 0:2048], [("AD", 0), ("AD", 1)])
    k.tt("vector", hbf, th, ot, ALU.mult, tag="l0a_h")

    def rhs2(ic, tsl):
        return V(hbf.ap[:, tsl * 512:(tsl + 1) * 512], hbf.regs)

    def evac2(oc, tsl, pv):
        xv = k.f32("AX", oc * 2048 + tsl * 512, 512)
        k.stt("vector", xv, pv, k.pcol(meta["bp"] + oc), xv, ALU.add, ALU.add,
              tag=f"l0a_res{oc}_{tsl}")

    emit_proj(k, meta["wp"], 1, list(range(NCHUNK)), rhs2, evac2, kp=64, tag="l0a_op")


def emit_l0_ffn(k, meta):
    xs = x_chunks(k)
    emit_ln(k, xs, out_bf_base=("AC", 0), xbf_base=("AC", 0), tag="l0f_ln")
    xn = [k.bf("AC", c * 2048, 2048) for c in range(NCHUNK)]
    for c in range(NCHUNK):
        k.dma(k.f32("AD", c * 512, 512), dram_v(meta["fr"][c]), tag=f"l0f_fr{c}")
        k.dma(k.f32("AD", 4096 + c * 512, 512), dram_v(meta["fi"][c]), tag=f"l0f_fi{c}")

    def fr_rhs(c, sl):
        v = k.bf("AD", c * 1024, 1024)
        return V(v.ap[:, sl * 512:(sl + 1) * 512], v.regs)

    def fi_rhs(c, sl):
        v = k.bf("AD", 8192 + c * 1024, 1024)
        return V(v.ap[:, sl * 512:(sl + 1) * 512], v.regs)

    def pt_pass(wa, wb, negb, dst_bfbase, tag):
        bi = 0
        for ob in range(NCHUNK):
            wta = emit_stream_w(k, meta[wa], ob, 0, 8, bi % 2, tag=tag + "a")
            bi += 1
            wtb = emit_stream_w(k, meta[wb], ob, 0, 8, bi % 2, neg=negb, tag=tag + "b")
            bi += 1
            for sl in range(2):
                pv = k.psum(4 * (ob % 2) + sl)
                for ic in range(8):
                    k.mm(pv, wta[ic], fr_rhs(ic, sl), start=(ic == 0), stop=False,
                         tag=f"{tag}_ma{ob}_{sl}_{ic}")
                for ic in range(8):
                    k.mm(pv, wtb[ic], fi_rhs(ic, sl), start=False, stop=(ic == 7),
                         tag=f"{tag}_mb{ob}_{sl}_{ic}")
                dst = k.bf("AB", dst_bfbase + ob * 1024 + sl * 512, 512)
                k.copy("vector", dst, pv, tag=f"{tag}_ev{ob}_{sl}")

    pt_pass("wr", "wi", True, 0, "l0f_prt")
    pt = [k.bf("AB", ob * 1024, 1024) for ob in range(NCHUNK)]

    def m_pass(f_rhs, first, tag):
        for db in range(NCHUNK):
            for sl in range(2):
                pv = k.psum(4 * (db % 2) + sl)
                for oc in range(NCHUNK):
                    lhs = V(pt[oc].ap[:, db * 128:(db + 1) * 128], pt[oc].regs)
                    k.mm(pv, lhs, f_rhs(oc, sl), start=(oc == 0), stop=(oc == 7),
                         tag=f"{tag}_{db}_{sl}_{oc}")
                dst = k.f32("AD", 8192 + db * 1024 + sl * 512, 512)
                if first:
                    k.ts("vector", dst, pv, 1.0 / 1024.0, None, ALU.mult,
                         tag=f"{tag}_ev{db}_{sl}")
                else:
                    k.stt("vector", dst, pv, 1.0 / 1024.0, dst, ALU.mult, ALU.add,
                          tag=f"{tag}_ev{db}_{sl}")

    m_pass(fr_rhs, True, "l0f_m1")
    pt_pass("wi", "wr", False, 0, "l0f_pit")
    m_pass(fi_rhs, False, "l0f_m2")
    m_bf = []
    for db in range(NCHUNK):
        src_v = k.f32("AD", 8192 + db * 1024, 1024)
        dst = k.bf("AB", db * 1024, 1024)
        k.copy(k.alt(), dst, src_v, tag=f"l0f_mcv{db}")
        m_bf.append(dst)
    for ob in range(NCHUNK):
        for tsl in range(NTSL):
            pv = k.psum(4 * (ob % 2) + tsl % 4)
            for dc in range(NCHUNK):
                lhs = V(m_bf[dc].ap[:, ob * 128:(ob + 1) * 128], m_bf[dc].regs)
                r = V(xn[dc].ap[:, tsl * 512:(tsl + 1) * 512], xn[dc].regs)
                k.mm(pv, lhs, r, start=(dc == 0), stop=(dc == 7),
                     tag=f"l0f_y{ob}_{tsl}_{dc}")
            xv = k.f32("AX", ob * 2048 + tsl * 512, 512)
            k.stt("vector", xv, pv, k.pcol(meta["bias"] + ob), xv, ALU.add, ALU.add,
                  tag=f"l0f_res{ob}_{tsl}")


def prep_l0_ffn(host, pc, inp):
    wr = np.asarray(inp["L0_ffn_w_freq_real"], np.float32)
    wi = np.asarray(inp["L0_ffn_w_freq_imag"], np.float32)
    host.add("l0f_wr", pack_lhsT(wr))
    host.add("l0f_wi", pack_lhsT(wi))
    N = 1024
    jk = np.outer(np.arange(N, dtype=np.float64), np.arange(N, dtype=np.float64))
    ang = 2.0 * np.pi * (jk % N) / N
    Fr = np.cos(ang).astype(np.float32)
    Fi = (-np.sin(ang)).astype(np.float32)
    frp = np.stack([pack_bf16_as_f32(Fr[c * 128:(c + 1) * 128, :]) for c in range(8)])
    fip = np.stack([pack_bf16_as_f32(Fi[c * 128:(c + 1) * 128, :]) for c in range(8)])
    host.add("l0f_frb", frp)
    host.add("l0f_fib", fip)
    return {
        "wr": "l0f_wr", "wi": "l0f_wi", "fr": "l0f_frb", "fi": "l0f_fib",
        "bias": pc.add_vecD(inp["L0_ffn_bias"]),
    }


def emit_spill_x(k, tag="spill"):
    for c in range(NCHUNK):
        k.dma(V(k.xsp[c * 128:(c + 1) * 128, :], [("dram", "xsp")]),
              k.f32("AX", c * 2048, 2048), ch="gpsimd", tag=f"{tag}{c}")


def emit_restore_x_to_AD(k, tag="rest"):
    for c in range(NCHUNK):
        k.dma(k.f32("AD", c * 2048, 2048),
              V(k.xsp[c * 128:(c + 1) * 128, :], [("dram", "xsp")]), tag=f"{tag}{c}")


def emit_copy_AD_to_AX(k, tag="adax"):
    for c in range(NCHUNK):
        k.copy(k.alt(), k.f32("AX", c * 2048, 2048), k.f32("AD", c * 2048, 2048),
               tag=f"{tag}{c}")


def emit_l1_attn(k, meta):
    xs = x_chunks(k)
    emit_ln(k, xs, out_bf_base=("AC", 0), xbf_base=("AC", 0), tag="l1a_ln")
    xn = [k.bf("AC", c * 2048, 2048) for c in range(NCHUNK)]
    emit_spill_x(k, tag="l1a_sp")

    def rhs_xn(ic, tsl):
        return V(xn[ic].ap[:, tsl * 512:(tsl + 1) * 512], xn[ic].regs)

    def evq(oc, tsl, pv):
        dst = k.bf("AD", oc * 2048 + tsl * 512, 512)
        k.copy("vector", dst, pv, tag=f"l1a_evq{oc}_{tsl}")

    def evk(oc, tsl, pv):
        dst = k.bf("AD", 16384 + oc * 2048 + tsl * 512, 512)
        k.copy("vector", dst, pv, tag=f"l1a_evk{oc}_{tsl}")

    emit_proj(k, meta["wq"], 8, list(range(8)), rhs_xn, evq, tag="l1a_q")
    emit_proj(k, meta["wk"], 8, list(range(8)), rhs_xn, evk, tag="l1a_k")
    wv_t = []
    for ic in range(8):
        stage = k.f32("AW", 2048 + (ic % 2) * 1024, 1024)
        k.dma(stage, dram_v(meta["wv"][0, :, ic * 1024:(ic + 1) * 1024]),
              tag=f"l1a_wvld{ic}")
        dst = k.bf("AB", ic * 1024, 1024)
        k.copy(k.alt(), dst, stage, tag=f"l1a_wvcv{ic}")
        wv_t.append(dst)
    for tt in range(16):
        for eh in range(2):
            pv = k.psum(4 + (tt % 2) * 2 + eh)
            for ic in range(8):
                lhs = V(xn[ic].ap[:, tt * 128:(tt + 1) * 128], xn[ic].regs)
                r = V(wv_t[ic].ap[:, eh * 512:(eh + 1) * 512], wv_t[ic].regs)
                k.mm(pv, lhs, r, start=(ic == 0), stop=(ic == 7),
                     tag=f"l1a_v{tt}_{eh}_{ic}")
            dst = k.bf("AX", tt * 1024 + eh * 512, 512)
            k.copy("vector", dst, pv, tag=f"l1a_evv{tt}_{eh}")
    v_tok = [k.bf("AX", tt * 1024, 1024) for tt in range(16)]

    for c in range(NCHUNK):
        cs = k.f32("AB", 0, 2048)
        ss = k.f32("AB", 2048, 2048)
        k.dma(cs, dram_v(meta["cs"][c * 128:(c + 1) * 128, :]), tag=f"l1a_cs{c}")
        k.dma(ss, dram_v(meta["ss"][c * 128:(c + 1) * 128, :]), tag=f"l1a_ss{c}")
        for base, nm in ((0, "q"), (16384, "k")):
            t = k.bf("AD", base + c * 2048, 2048)
            tmp = k.bf("AW", 8192, 2048)
            for sl in range(4):
                pv = k.psum(sl % 2)
                k.mm(pv, V(k.swap_bf, [("consts",)]),
                     V(t.ap[:, sl * 512:(sl + 1) * 512], t.regs),
                     start=True, stop=True, tag=f"l1a_sw{nm}{c}_{sl}")
                k.tt("vector", V(tmp.ap[:, sl * 512:(sl + 1) * 512], tmp.regs), pv,
                     V(ss.ap[:, sl * 512:(sl + 1) * 512], ss.regs), ALU.mult,
                     tag=f"l1a_swm{nm}{c}_{sl}")
            k.tt("vector", t, t, cs, ALU.mult, tag=f"l1a_rc{nm}{c}")
            k.tt("vector", t, t, tmp, ALU.add, tag=f"l1a_ra{nm}{c}")
    q_c = [k.bf("AD", c * 2048, 2048) for c in range(NCHUNK)]
    k_c = [k.bf("AD", 16384 + c * 2048, 2048) for c in range(NCHUNK)]

    for qt in range(16):
        L = (qt + 1) * 128
        nkb = qt // 4 + 1
        scores = k.f32("AB", 0, 2048)
        for kb in range(nkb):
            pv = k.psum(kb % 4)
            for ec in range(8):
                lhs = V(q_c[ec].ap[:, qt * 128:(qt + 1) * 128], q_c[ec].regs)
                r = V(k_c[ec].ap[:, kb * 512:(kb + 1) * 512], k_c[ec].regs)
                k.mm(pv, lhs, r, start=(ec == 0), stop=(ec == 7),
                     tag=f"l1a_s{qt}_{kb}_{ec}")
            full = (qt % 4) * 128 if kb == qt // 4 else 512
            if full > 0:
                k.copy("vector", V(scores.ap[:, kb * 512:kb * 512 + full], scores.regs),
                       V(pv.ap[:, 0:full], pv.regs), tag=f"l1a_se{qt}_{kb}")
            if kb == qt // 4:
                dcol = (qt % 4) * 128
                k.tt("vector", V(scores.ap[:, kb * 512 + dcol:kb * 512 + dcol + 128], scores.regs),
                     V(pv.ap[:, dcol:dcol + 128], pv.regs),
                     V(k.causal_mask, [("consts",)]), ALU.add, tag=f"l1a_sd{qt}")
        rmax = k.scol(0)
        nb = k.scol(1)
        rsum = k.scol(2)
        rrec = k.scol(3)
        k.reduce("vector", rmax, V(scores.ap[:, 0:L], scores.regs), ALU.max,
                 tag=f"l1a_rm{qt}")
        k.ts("vector", nb, rmax, -1.0 / 32.0, None, ALU.mult, tag=f"l1a_nb{qt}")
        ef = k.f32("AW", 0, 2048)
        efL = V(ef.ap[:, 0:L], ef.regs)
        k.p.op("scalar", lambda e, o=efL.ap, x=scores.ap[:, 0:L], b=nb.ap:
               e.activation(o, x, AF.Exp, bias=b, scale=1.0 / 32.0),
               reads=scores.regs + nb.regs + [("consts",)], writes=efL.regs,
               tag=f"l1a_exp{qt}")
        k.reduce("vector", rsum, efL, ALU.add, tag=f"l1a_rs{qt}")
        k.recip(rrec, rsum, tag=f"l1a_rc{qt}")
        pbf = k.bf("AW", 8192, 2048)
        pL = V(pbf.ap[:, 0:L], pbf.regs)
        k.ts("vector", pL, efL, rrec, None, ALU.mult, tag=f"l1a_pn{qt}")
        if DBG_SINK and qt == DBG_SINK.get("qt", -1):
            k.dma(dram_v(DBG_SINK["scores"]), scores, ch="gpsimd", tag="dbg_sc")
            k.dma(dram_v(DBG_SINK["ef"]), ef, ch="gpsimd", tag="dbg_ef")
            k.dma(dram_v(DBG_SINK["pbf"]), k.f32("AW", 4096, 1024), ch="gpsimd", tag="dbg_p")
            k.dma(dram_v(DBG_SINK["stats"]), k.scol(0, 8), ch="gpsimd", tag="dbg_st")
        ptq = k.bf("AW", 10240, 2048)
        for kt in range(qt + 1):
            pvt = k.psum_bf(6 + (kt % 2), 0, 128)
            k.transpose(pvt, V(pbf.ap[:, kt * 128:(kt + 1) * 128], pbf.regs),
                        tag=f"l1a_tr{qt}_{kt}")
            k.copy("vector", V(ptq.ap[:, kt * 128:(kt + 1) * 128], ptq.regs), pvt,
                   tag=f"l1a_tre{qt}_{kt}")
        for do in range(8):
            pv = k.psum(do % 6)
            pv = V(pv.ap[:, 0:128], pv.regs)
            for kt in range(qt + 1):
                lhs = V(v_tok[kt].ap[:, do * 128:(do + 1) * 128], v_tok[kt].regs)
                r = V(ptq.ap[:, kt * 128:(kt + 1) * 128], ptq.regs)
                k.mm(pv, lhs, r, start=(kt == 0), stop=(kt == qt),
                     tag=f"l1a_o{qt}_{do}_{kt}")
            ob = k.bf("AC", do * 2048 + qt * 128, 128)
            k.copy("vector", ob, pv, tag=f"l1a_oe{qt}_{do}")

    emit_restore_x_to_AD(k, tag="l1a_rs")
    for c in range(NCHUNK):
        k.tt(k.alt(), k.f32("AX", c * 2048, 2048), k.f32("AD", c * 2048, 2048),
             k.bf("AC", c * 2048, 2048), ALU.add, tag=f"l1a_fin{c}")


def prep_l1_attn(host, pc, inp):
    host.add("l1a_wq", pack_lhsT(np.asarray(inp["L1_attn_w_q"]).T))
    host.add("l1a_wk", pack_lhsT(np.asarray(inp["L1_attn_w_k"]).T))
    host.add("l1a_wv", pack_lhsT(np.asarray(inp["L1_attn_w_v"]).T, ow=1024))
    d = 1024
    i = np.arange(d // 2, dtype=np.float64)
    theta = np.power(10000.0, -2.0 * (i - 1.0) / d)
    pos = np.arange(T, dtype=np.float64)
    ang = pos[None, :] * theta[:, None]
    cos, sin = np.cos(ang), np.sin(ang)
    cs = np.zeros((d, T), np.float32)
    ss = np.zeros((d, T), np.float32)
    cs[0::2, :] = cos
    cs[1::2, :] = cos
    ss[0::2, :] = sin
    ss[1::2, :] = -sin
    host.add("l1a_cs", cs)
    host.add("l1a_ss", ss)
    return {"wq": "l1a_wq", "wk": "l1a_wk", "wv": "l1a_wv",
            "cs": "l1a_cs", "ss": "l1a_ss"}


def emit_l1_ffn(k, meta):
    xs = x_chunks(k)
    emit_ln(k, xs, out_bf_base=("AC", 0), xbf_base=("AC", 0), tag="l1f_ln")
    xn = [k.bf("AC", c * 2048, 2048) for c in range(NCHUNK)]
    k.load_bp(meta["bp"], 136, tag="l1f_bp")
    for c in range(NCHUNK):
        k.copy(k.alt(), k.f32("AD", c * 2048, 2048), k.f32("AX", c * 2048, 2048),
               tag=f"l1f_xcp{c}")
    sc = {"w1": 0, "w2": 32, "h1": 64, "h2": 96, "w3": 128}
    for tq in range(4):
        t0 = tq * 512
        hA = [k.bf("AX", oc * 512, 512) for oc in range(32)]
        hB = [k.bf("AX", 16384 + oc * 512, 512) for oc in range(32)]
        bi = 0
        for oc in range(32):
            wtA = emit_stream_w(k, meta["w1"], oc, 0, 8, bi % 2, tag="l1f_w1")
            bi += 1
            wtB = emit_stream_w(k, meta["w2"], oc, 0, 8, bi % 2, tag="l1f_w2")
            bi += 1
            pA = k.psum((oc % 2) * 2)
            pB = k.psum((oc % 2) * 2 + 1)
            for ic in range(8):
                r = V(xn[ic].ap[:, t0:t0 + 512], xn[ic].regs)
                k.mm(pA, wtA[ic], r, start=(ic == 0), stop=(ic == 7),
                     tag=f"l1f_mA{tq}_{oc}_{ic}")
            for ic in range(8):
                r = V(xn[ic].ap[:, t0:t0 + 512], xn[ic].regs)
                k.mm(pB, wtB[ic], r, start=(ic == 0), stop=(ic == 7),
                     tag=f"l1f_mB{tq}_{oc}_{ic}")
            a = k.f32("AB", (oc % 2) * 512, 512)
            k.act(a, pA, AF.Silu, scale=V(k.bpcol(sc["w1"] + oc), [("BP",)]),
                  tag=f"l1f_silu{tq}_{oc}")
            k.stt("vector", hA[oc], pB, V(k.bpcol(sc["w2"] + oc), [("BP",)]), a,
                  ALU.mult, ALU.mult, tag=f"l1f_h{tq}_{oc}")

        def mk_rhs(buf):
            def rhs(ic, tsl):
                return V(buf[ic].ap, buf[ic].regs)
            return rhs

        def mk_gelu_evac(dstbuf, colbase, tag):
            def evac(oc, tsl, pv):
                g = k.f32("AB", (oc % 2) * 512, 512)
                k.act(g, pv, AF.Gelu, scale=V(k.bpcol(colbase + oc), [("BP",)]),
                      tag=f"{tag}_g{oc}")
                k.copy(k.alt(), dstbuf[oc], g, tag=f"{tag}_c{oc}")
            return evac

        emit_proj(k, meta["h1"], 32, list(range(32)), mk_rhs(hA),
                  mk_gelu_evac(hB, sc["h1"], f"l1f_s2_{tq}"),
                  psum_sets=((4,), (5,), (6,), (7,)), ntsl=1, tag=f"l1f_h1_{tq}")
        emit_proj(k, meta["h2"], 32, list(range(32)), mk_rhs(hB),
                  mk_gelu_evac(hA, sc["h2"], f"l1f_s3_{tq}"),
                  psum_sets=((4,), (5,), (6,), (7,)), ntsl=1, tag=f"l1f_h2_{tq}")

        def evac_w3(oc, tsl, pv):
            acc = k.f32("AD", oc * 2048 + t0, 512)
            k.stt("vector", acc, pv, V(k.bpcol(sc["w3"] + oc), [("BP",)]), acc,
                  ALU.mult, ALU.add, tag=f"l1f_w3e{tq}_{oc}")

        emit_proj(k, meta["w3"], 32, list(range(8)), mk_rhs(hA), evac_w3,
                  psum_sets=((4,), (5,), (6,), (7,)), ntsl=1, tag=f"l1f_w3_{tq}")
    emit_copy_AD_to_AX(k, tag="l1f_fin")


def prep_l1_ffn(host, pc, inp):
    host.add("l1f_w1", pack_lhsT(np.asarray(inp["L1_ffn_w1_w"]).T))
    host.add("l1f_w2", pack_lhsT(np.asarray(inp["L1_ffn_w2_w"]).T))
    host.add("l1f_h1", pack_lhsT(np.asarray(inp["L1_ffn_h1_w"]).T))
    host.add("l1f_h2", pack_lhsT(np.asarray(inp["L1_ffn_h2_w"]).T))
    host.add("l1f_w3", pack_lhsT(np.asarray(inp["L1_ffn_w3_w"]).T))
    bp = np.zeros((128, 136), np.float32)
    for j, nm in enumerate(["w1_s", "w2_s", "h1_s", "h2_s"]):
        v = np.asarray(inp[f"L1_ffn_{nm}"], np.float32)
        for c in range(32):
            bp[:, j * 32 + c] = v[c * 128:(c + 1) * 128]
    v = np.asarray(inp["L1_ffn_w3_s"], np.float32)
    for c in range(8):
        bp[:, 128 + c] = v[c * 128:(c + 1) * 128]
    host.add("l1f_bpa", bp)
    return {"w1": "l1f_w1", "w2": "l1f_w2", "h1": "l1f_h1", "h2": "l1f_h2",
            "w3": "l1f_w3", "bp": "l1f_bpa"}


def emit_l2_attn(k, meta):
    xs = x_chunks(k)
    emit_ln(k, xs, out_f32_base=("AD", 0), xbf_base=("AC", 0), tag="l2a_ln")
    xn = [k.f32("AD", c * 2048, 2048) for c in range(NCHUNK)]
    emit_spill_x(k, tag="l2a_sp")
    mixdst = [("AC", 0, meta["tmk"]), ("AX", 0, meta["tmv"]), ("AX", 16384, meta["tmr"])]
    for c in range(NCHUNK):
        tmp = k.f32("AB", 0, 2047)
        k.tt("vector", tmp, V(xn[c].ap[:, 1:2048], xn[c].regs),
             V(xn[c].ap[:, 0:2047], xn[c].regs), ALU.subtract, tag=f"l2a_d{c}")
        for (ar, base, tmcol) in mixdst:
            dst = k.bf(ar, base + c * 2048, 2048)
            k.stt(k.alt(), V(dst.ap[:, 1:2048], dst.regs), tmp, k.pcol(tmcol + c),
                  V(xn[c].ap[:, 0:2047], xn[c].regs), ALU.mult, ALU.add,
                  tag=f"l2a_m{ar}{base}{c}")
            k.ts(k.alt(), V(dst.ap[:, 0:1], dst.regs), V(xn[c].ap[:, 0:1], xn[c].regs),
                 k.pcol(tmcol + c), None, ALU.mult, tag=f"l2a_m0{ar}{base}{c}")
    mix_k = [k.bf("AC", c * 2048, 2048) for c in range(NCHUNK)]
    mix_v = [k.bf("AX", c * 2048, 2048) for c in range(NCHUNK)]
    mix_r = [k.bf("AX", 16384 + c * 2048, 2048) for c in range(NCHUNK)]

    def mk_rhs(buf):
        def rhs(ic, tsl):
            return V(buf[ic].ap[:, tsl * 512:(tsl + 1) * 512], buf[ic].regs)
        return rhs

    def mk_ev(ar, base, tag):
        def evac(oc, tsl, pv):
            k.copy("vector", k.bf(ar, base + oc * 2048 + tsl * 512, 512), pv,
                   tag=f"{tag}{oc}_{tsl}")
        return evac

    emit_proj(k, meta["wk"], 8, list(range(8)), mk_rhs(mix_k), mk_ev("AD", 0, "l2a_evk"),
              tag="l2a_k")
    emit_proj(k, meta["wv"], 8, list(range(8)), mk_rhs(mix_v), mk_ev("AD", 16384, "l2a_evv"),
              tag="l2a_v")
    emit_proj(k, meta["wr"], 8, list(range(8)), mk_rhs(mix_r), mk_ev("AX", 0, "l2a_evr"),
              tag="l2a_r")
    k_c = [k.bf("AD", c * 2048, 2048) for c in range(NCHUNK)]
    v_c = [k.bf("AD", 16384 + c * 2048, 2048) for c in range(NCHUNK)]
    r_c = [k.bf("AX", c * 2048, 2048) for c in range(NCHUNK)]

    for c in range(NCHUNK):
        dec = k.scol(8 + c)
        eU = k.scol(16 + c)
        k.act(dec, V(k.pcol(meta["W"] + c), [("params",)]), AF.Exp, tag=f"l2a_w{c}")
        k.act(dec, dec, AF.Exp, scale=-1.0, tag=f"l2a_dec{c}")
        k.act(eU, V(k.pcol(meta["U"] + c), [("params",)]), AF.Exp, tag=f"l2a_eu{c}")
    ones_b = k.f32("AW", 0, 2048)
    k.memset("vector", ones_b, 1.0, tag="l2a_ones")

    for c in range(NCHUNK):
        ek = k.f32("AB", 0, 2048)
        ekv = k.f32("AB", 2048, 2048)
        dec_b = k.f32("AW", 2048, 2048)
        Abuf = k.f32("AW", 4096, 2048)
        k.act(ek, k_c[c], AF.Exp, tag=f"l2a_ek{c}")
        k.tt("vector", ekv, ek, v_c[c], ALU.mult, tag=f"l2a_ekv{c}")
        k.ts("vector", dec_b, ones_b, k.scol(8 + c), None, ALU.mult, tag=f"l2a_db{c}")
        k.scan(Abuf, dec_b, ekv, 0.0, ALU.mult, ALU.add, tag=f"l2a_A{c}")
        k.stt("vector", V(ekv.ap[:, 1:2048], ekv.regs), V(ekv.ap[:, 1:2048], ekv.regs),
              k.scol(16 + c), V(Abuf.ap[:, 0:2047], Abuf.regs), ALU.mult, ALU.add,
              tag=f"l2a_num{c}")
        k.ts("vector", V(ekv.ap[:, 0:1], ekv.regs), V(ekv.ap[:, 0:1], ekv.regs),
             k.scol(16 + c), None, ALU.mult, tag=f"l2a_num0{c}")
        k.scan(Abuf, dec_b, ek, 0.0, ALU.mult, ALU.add, tag=f"l2a_B{c}")
        k.stt("vector", V(ek.ap[:, 1:2048], ek.regs), V(ek.ap[:, 1:2048], ek.regs),
              k.scol(16 + c), V(Abuf.ap[:, 0:2047], Abuf.regs), ALU.mult, ALU.add,
              tag=f"l2a_den{c}")
        k.ts("vector", V(ek.ap[:, 0:1], ek.regs), V(ek.ap[:, 0:1], ek.regs),
             k.scol(16 + c), None, ALU.mult, tag=f"l2a_den0{c}")
        k.act(ek, ek, AF.Ln, tag=f"l2a_ln{c}")
        k.act(ek, ek, AF.Exp, scale=-1.0, tag=f"l2a_rden{c}")
        k.tt("vector", ekv, ekv, ek, ALU.mult, tag=f"l2a_y{c}")
        sig = k.f32("AW", 2048, 2048)
        k.act(sig, r_c[c], AF.Sigmoid, tag=f"l2a_sig{c}")
        k.tt("vector", k.bf("AC", c * 2048, 2048), ekv, sig, ALU.mult, tag=f"l2a_rw{c}")
    rw = [k.bf("AC", c * 2048, 2048) for c in range(NCHUNK)]

    emit_restore_x_to_AD(k, tag="l2a_rs")

    def evac_o(oc, tsl, pv):
        xv = k.f32("AD", oc * 2048 + tsl * 512, 512)
        k.tt("vector", xv, pv, xv, ALU.add, tag=f"l2a_oe{oc}_{tsl}")

    emit_proj(k, meta["wo"], 8, list(range(8)), mk_rhs(rw), evac_o, tag="l2a_o")
    emit_copy_AD_to_AX(k, tag="l2a_fin")


def prep_l2_attn(host, pc, inp):
    host.add("l2a_wk", pack_lhsT(np.asarray(inp["L2_attn_Wk"]).T))
    host.add("l2a_wv", pack_lhsT(np.asarray(inp["L2_attn_Wv"]).T))
    host.add("l2a_wr", pack_lhsT(np.asarray(inp["L2_attn_Wr"]).T))
    host.add("l2a_wo", pack_lhsT(np.asarray(inp["L2_attn_Wo"]).T))
    return {"wk": "l2a_wk", "wv": "l2a_wv", "wr": "l2a_wr", "wo": "l2a_wo",
            "W": pc.add_vecD(inp["L2_attn_W"]),
            "U": pc.add_vecD(inp["L2_attn_U"]),
            "tmk": pc.add_vecD(np.asarray(inp["L2_attn_time_mix_k"]).reshape(-1)),
            "tmv": pc.add_vecD(np.asarray(inp["L2_attn_time_mix_v"]).reshape(-1)),
            "tmr": pc.add_vecD(np.asarray(inp["L2_attn_time_mix_r"]).reshape(-1))}


def emit_l2_ffn(k, meta):
    xs = x_chunks(k)
    emit_ln(k, xs, out_f32_base=("AD", 0), out_bf_base=("AC", 0),
            xbf_base=("AC", 0), tag="l2f_ln")
    xn32 = [k.f32("AD", c * 2048, 2048) for c in range(NCHUNK)]
    xn = [k.bf("AC", c * 2048, 2048) for c in range(NCHUNK)]
    k.load_bp(meta["bp"], 256, tag="l2f_bp")
    wgs = k.f32("AW", 2048, 64)
    k.dma(wgs, dram_v(meta["wg"][0]), tag="l2f_wgld")
    glog = k.f32("AB", 0, 128)
    for tt in range(16):
        pv = k.psum(4 + tt % 2)
        pv = V(pv.ap[:, 0:8], pv.regs)
        for ic in range(NCHUNK):
            lhs = V(xn32[ic].ap[:, tt * 128:(tt + 1) * 128], xn32[ic].regs)
            r = V(wgs.ap[:, ic * 8:(ic + 1) * 8], wgs.regs)
            k.mm(pv, lhs, r, start=(ic == 0), stop=(ic == 7), tag=f"l2f_g{tt}_{ic}")
        k.copy("vector", V(glog.ap[:, tt * 8:(tt + 1) * 8], glog.regs), pv,
               tag=f"l2f_ge{tt}")
    for c in range(NCHUNK):
        k.copy(k.alt(), k.f32("AD", c * 2048, 2048), k.f32("AX", c * 2048, 2048),
               tag=f"l2f_xcp{c}")
    f_all = k.f32("AB", 128, 128)
    for tt in range(16):
        g = V(glog.ap[:, tt * 8:(tt + 1) * 8], glog.regs)
        m1, m2, q, rden, w2c = k.scol(24), k.scol(25), k.scol(26), k.scol(27), k.scol(28)
        eq1 = k.f32("AB", 256, 8)
        eq2 = k.f32("AB", 264, 8)
        k.reduce("vector", m1, g, ALU.max, tag=f"l2f_m1{tt}")
        k.ts("vector", eq1, g, m1, None, ALU.is_equal, tag=f"l2f_eq1{tt}")
        k.stt("vector", eq2, eq1, -60000.0, g, ALU.mult, ALU.add, tag=f"l2f_msk{tt}")
        k.reduce("vector", m2, eq2, ALU.max, tag=f"l2f_m2{tt}")
        k.ts("vector", eq2, eq2, m2, None, ALU.is_equal, tag=f"l2f_eq2{tt}")
        k.ts("vector", q, m2, m1, None, ALU.subtract, tag=f"l2f_q{tt}")
        k.act(q, q, AF.Exp, tag=f"l2f_eq{tt}")
        k.ts("vector", rden, q, 1.0, None, ALU.add, tag=f"l2f_d{tt}")
        k.recip(rden, rden, tag=f"l2f_r{tt}")
        k.ts("vector", w2c, q, rden, None, ALU.mult, tag=f"l2f_w2{tt}")
        fv = V(f_all.ap[:, tt * 8:(tt + 1) * 8], f_all.regs)
        k.ts("vector", fv, eq1, rden, None, ALU.mult, tag=f"l2f_f1{tt}")
        k.stt("vector", fv, eq2, w2c, fv, ALU.mult, ALU.add, tag=f"l2f_f2{tt}")
    fbf_all = V(k.sb["AB"].ap().bitcast(BF16)[:, 6656:6784], [("AB", 6)])
    k.copy("vector", fbf_all, f_all, tag="l2f_fcv")
    f_fm_bf = V(k.sb["AW"].ap().bitcast(BF16)[0:8, 10240:12288],
                [("AW", 10), ("AW", 11)])
    for tt in range(16):
        pvt = k.psum_bf(6, 0, 128)
        pvt = V(pvt.ap[0:8, :], pvt.regs)
        k.p.op("tensor", lambda e, o=pvt.ap, s=fbf_all.ap[:, tt * 8:(tt + 1) * 8]:
               e.transpose(o, s, k.ident_bf),
               reads=fbf_all.regs + [("consts",)], writes=pvt.regs, tag=f"l2f_tr{tt}")
        k.copy("vector", V(f_fm_bf.ap[:, tt * 128:(tt + 1) * 128], f_fm_bf.regs), pvt,
               tag=f"l2f_tre{tt}")

    if DBG_SINK.get("l2f"):
        k.dma(dram_v(DBG_SINK["glog"]), k.f32("AB", 0, 128), ch="gpsimd", tag="dbg_gl")
        k.dma(dram_v(DBG_SINK["fall"]), f_all, ch="gpsimd", tag="dbg_fa")
        k.dma(dram_v(DBG_SINK["ffm"]), V(k.sb["AW"].ap()[0:8, 5120:6144], f_fm_bf.regs),
              ch="gpsimd", tag="dbg_ffm")
    for e in range(8):
        frow = V(k.sb["AW"].ap().bitcast(BF16)[0:1, 2048:4096], [("AW", 2), ("AW", 3)])
        for tt in range(16):
            pvt = k.psum_bf(7, 0, 128)
            pvt = V(pvt.ap[0:1, :], pvt.regs)
            k.p.op("tensor", lambda ee, o=pvt.ap, s=fbf_all.ap[:, tt * 8 + e:tt * 8 + e + 1]:
                   ee.transpose(o, s, k.ident_bf),
                   reads=fbf_all.regs + [("consts",)], writes=pvt.regs,
                   tag=f"l2f_fr{e}_{tt}")
            k.copy("vector", V(frow.ap[:, tt * 128:(tt + 1) * 128], frow.regs), pvt,
                   tag=f"l2f_fre{e}_{tt}")
        for th in range(2):
            fbc = k.f32("AW", 0, 1024)
            for sl in range(2):
                pv = k.psum(4)
                lhs = V(k.ones_bf[0:1, :], [("consts",)])
                r = V(frow.ap[:, th * 1024 + sl * 512: th * 1024 + (sl + 1) * 512],
                      frow.regs)
                k.mm(pv, lhs, r, start=True, stop=True, tag=f"l2f_fb{th}_{e}_{sl}")
                k.copy("vector", V(fbc.ap[:, sl * 512:(sl + 1) * 512], fbc.regs), pv,
                       tag=f"l2f_fbe{th}_{e}_{sl}")
            bi = 0
            for oc in range(32):
                wt = emit_stream_w(k, meta["w1"][e], oc, 0, 8, bi % 2, tag="l2f_w1")
                bi += 1
                for sl in range(2):
                    pv = k.psum((oc % 2) * 2 + sl)
                    for ic in range(NCHUNK):
                        r = V(xn[ic].ap[:, th * 1024 + sl * 512:th * 1024 + (sl + 1) * 512],
                              xn[ic].regs)
                        k.mm(pv, wt[ic], r, start=(ic == 0), stop=(ic == 7),
                             tag=f"l2f_u{th}_{e}_{oc}_{sl}_{ic}")
                    a = k.f32("AB", 2304 + (oc % 2) * 512, 512)
                    k.act(a, pv, AF.Relu, bias=V(k.bpcol(e * 32 + oc), [("BP",)]),
                          tag=f"l2f_re{th}_{e}_{oc}_{sl}")
                    hb = k.bf("AX", oc * 1024 + sl * 512, 512)
                    k.tt("vector", hb, a, V(fbc.ap[:, sl * 512:(sl + 1) * 512], fbc.regs),
                         ALU.mult, tag=f"l2f_hf{th}_{e}_{oc}_{sl}")
            h_e = [k.bf("AX", oc * 1024, 1024) for oc in range(32)]
            bi2 = 0
            for ob in range(NCHUNK):
                for g in range(4):
                    wt = emit_stream_w(k, meta["w2"][e], ob, g * 8, 8, bi2 % 2,
                                       tag="l2f_w2")
                    bi2 += 1
                    for sl in range(2):
                        pv = k.psum(5 + sl)
                        for i in range(8):
                            ic = g * 8 + i
                            r = V(h_e[ic].ap[:, sl * 512:(sl + 1) * 512], h_e[ic].regs)
                            k.mm(pv, wt[i], r, start=(ic == 0), stop=(ic == 31),
                                 tag=f"l2f_d{th}_{e}_{ob}_{sl}_{ic}")
                for sl in range(2):
                    pv = k.psum(5 + sl)
                    acc = k.f32("AD", ob * 2048 + th * 1024 + sl * 512, 512)
                    k.tt("vector", acc, pv, acc, ALU.add, tag=f"l2f_de{th}_{e}_{ob}_{sl}")
            if DBG_SINK.get("l2f") and e == 0 and th == 1:
                for cdump in range(NCHUNK):
                    k.dma(dram_v(DBG_SINK["acc0"][cdump * 128:(cdump + 1) * 128, :]),
                          k.f32("AD", cdump * 2048, 2048), ch="gpsimd", tag=f"dbg_a{cdump}")
                for hdump in range(8):
                    k.dma(dram_v(DBG_SINK["h0"][hdump * 128:(hdump + 1) * 128, :]),
                          k.f32("AX", hdump * 512, 512), ch="gpsimd", tag=f"dbg_h{hdump}")
    for ob in range(NCHUNK):
        wt = emit_stream_w(k, meta["b2"], ob, 0, 1, ob % 2, ow=128, kp=8, tag="l2f_b2")
        for tsl in range(4):
            pv = k.psum(7)
            r = V(f_fm_bf.ap[:, tsl * 512:(tsl + 1) * 512], f_fm_bf.regs)
            k.mm(pv, wt[0], r, start=True, stop=True, tag=f"l2f_b2m{ob}_{tsl}")
            acc = k.f32("AD", ob * 2048 + tsl * 512, 512)
            k.tt("vector", acc, pv, acc, ALU.add, tag=f"l2f_b2e{ob}_{tsl}")
    emit_copy_AD_to_AX(k, tag="l2f_fin")


def prep_l2_ffn(host, pc, inp):
    W1 = np.asarray(inp["L2_ffn_W1"], np.float32)
    W2 = np.asarray(inp["L2_ffn_W2"], np.float32)
    host.add("l2f_w1", np.stack([pack_lhsT(W1[e]) for e in range(8)]))
    host.add("l2f_w2", np.stack([pack_lhsT(W2[e]) for e in range(8)]))
    host.add("l2f_wg", pack_lhsT(np.asarray(inp["L2_ffn_Wg"], np.float32), ow=8))
    host.add("l2f_b2p", pack_lhsT(np.asarray(inp["L2_ffn_b2"], np.float32)))
    b1 = np.asarray(inp["L2_ffn_b1"], np.float32)
    bp = np.zeros((128, 256), np.float32)
    for e in range(8):
        for c in range(32):
            bp[:, e * 32 + c] = b1[e, c * 128:(c + 1) * 128]
    host.add("l2f_bpa", bp)
    return {"w1": "l2f_w1", "w2": "l2f_w2", "wg": "l2f_wg", "b2": "l2f_b2p",
            "bp": "l2f_bpa"}


L3_BAND = 8


def emit_l3_attn(k, meta):
    xs = x_chunks(k)
    emit_ln(k, xs, out_bf_base=("AC", 0), xbf_base=("AC", 0), tag="l3a_ln")
    xn = [k.bf("AC", c * 2048, 2048) for c in range(NCHUNK)]
    emit_spill_x(k, tag="l3a_sp")

    def rhs_xn(ic, tsl):
        return V(xn[ic].ap[:, tsl * 512:(tsl + 1) * 512], xn[ic].regs)

    gq = V(k.gamma_q(128), [("consts",)])
    gk = V(k.gamma_k(128), [("consts",)])

    def mk_qk_ev(base, bcol, gtile, tag):
        def evac(oc, tsl, pv):
            for sub in range(4):
                dst = k.bf("AD", base + oc * 2048 + tsl * 512 + sub * 128, 128)
                k.stt("vector", dst, V(pv.ap[:, sub * 128:(sub + 1) * 128], pv.regs),
                      k.pcol(bcol + oc), gtile, ALU.add, ALU.mult,
                      tag=f"{tag}{oc}_{tsl}_{sub}")
        return evac

    emit_proj(k, meta["wq"], 8, list(range(8)), rhs_xn,
              mk_qk_ev(0, meta["bq"], gq, "l3a_evq"), tag="l3a_q")
    emit_proj(k, meta["wk"], 8, list(range(8)), rhs_xn,
              mk_qk_ev(16384, meta["bk"], gk, "l3a_evk"), tag="l3a_k")
    wv_t = []
    for ic in range(8):
        stage = k.f32("AW", 2048 + (ic % 2) * 1024, 1024)
        k.dma(stage, dram_v(meta["wv"][0, :, ic * 1024:(ic + 1) * 1024]),
              tag=f"l3a_wvld{ic}")
        dst = k.bf("AB", ic * 1024, 1024)
        k.copy(k.alt(), dst, stage, tag=f"l3a_wvcv{ic}")
        wv_t.append(dst)
    for tt in range(16):
        for eh in range(2):
            pv = k.psum(4 + (tt % 2) * 2 + eh)
            for ic in range(8):
                lhs = V(xn[ic].ap[:, tt * 128:(tt + 1) * 128], xn[ic].regs)
                r = V(wv_t[ic].ap[:, eh * 512:(eh + 1) * 512], wv_t[ic].regs)
                k.mm(pv, lhs, r, start=(ic == 0), stop=(ic == 7),
                     tag=f"l3a_v{tt}_{eh}_{ic}")
            dst = k.bf("AX", tt * 1024 + eh * 512, 512)
            k.copy("vector", dst, pv, tag=f"l3a_evv{tt}_{eh}")
    v_tok = [k.bf("AX", tt * 1024, 1024) for tt in range(16)]
    q_c = [k.bf("AD", c * 2048, 2048) for c in range(NCHUNK)]
    k_c = [k.bf("AD", 16384 + c * 2048, 2048) for c in range(NCHUNK)]

    emit_restore_x_to_AD_late = False
    for qt in range(16):
        kt_lo = max(0, qt - L3_BAND)
        pbf = k.bf("AW", 8192, 2048)
        kb_lo, kb_hi = kt_lo // 4, qt // 4
        for kb in range(kb_lo, kb_hi + 1):
            pv = k.psum(kb % 4)
            for ec in range(8):
                lhs = V(q_c[ec].ap[:, qt * 128:(qt + 1) * 128], q_c[ec].regs)
                r = V(k_c[ec].ap[:, kb * 512:(kb + 1) * 512], k_c[ec].regs)
                k.mm(pv, lhs, r, start=(ec == 0), stop=(ec == 7),
                     tag=f"l3a_s{qt}_{kb}_{ec}")
            for sub in range(4):
                kt = kb * 4 + sub
                if kt < kt_lo or kt > qt:
                    continue
                dlt = qt - kt
                cfac = float(GAMMA ** (128 * dlt))
                dst = V(pbf.ap[:, kt * 128:(kt + 1) * 128], pbf.regs)
                pslice = V(pv.ap[:, sub * 128:(sub + 1) * 128], pv.regs)
                if kt == qt:
                    k.tt("vector", dst, pslice, V(k.caus01, [("consts",)]), ALU.mult,
                         tag=f"l3a_sd{qt}")
                else:
                    k.ts("vector", dst, pslice, cfac, None, ALU.mult,
                         tag=f"l3a_se{qt}_{kt}")
        ptq = k.bf("AW", 10240, 2048)
        for kt in range(kt_lo, qt + 1):
            pvt = k.psum_bf(6 + (kt % 2), 0, 128)
            k.transpose(pvt, V(pbf.ap[:, kt * 128:(kt + 1) * 128], pbf.regs),
                        tag=f"l3a_tr{qt}_{kt}")
            k.copy("vector", V(ptq.ap[:, kt * 128:(kt + 1) * 128], ptq.regs), pvt,
                   tag=f"l3a_tre{qt}_{kt}")
        for do in range(8):
            pv = k.psum(do % 6)
            pv = V(pv.ap[:, 0:128], pv.regs)
            for j, kt in enumerate(range(kt_lo, qt + 1)):
                lhs = V(v_tok[kt].ap[:, do * 128:(do + 1) * 128], v_tok[kt].regs)
                r = V(ptq.ap[:, kt * 128:(kt + 1) * 128], ptq.regs)
                k.mm(pv, lhs, r, start=(j == 0), stop=(kt == qt),
                     tag=f"l3a_o{qt}_{do}_{kt}")
            ob = k.bf("AC", do * 2048 + qt * 128, 128)
            k.copy("vector", ob, pv, tag=f"l3a_oe{qt}_{do}")

    emit_restore_x_to_AD(k, tag="l3a_rs")
    for c in range(NCHUNK):
        k.tt(k.alt(), k.f32("AX", c * 2048, 2048), k.f32("AD", c * 2048, 2048),
             k.bf("AC", c * 2048, 2048), ALU.add, tag=f"l3a_fin{c}")


def prep_l3_attn(host, pc, inp):
    host.add("l3a_wq", pack_lhsT(np.asarray(inp["L3_attn_Wq"]).T))
    host.add("l3a_wk", pack_lhsT(np.asarray(inp["L3_attn_Wk"]).T))
    host.add("l3a_wv", pack_lhsT(np.asarray(inp["L3_attn_Wv"]).T, ow=1024))
    return {"wq": "l3a_wq", "wk": "l3a_wk", "wv": "l3a_wv",
            "bq": pc.add_vecD(inp["L3_attn_bq"]),
            "bk": pc.add_vecD(inp["L3_attn_bk"])}


def emit_l3_ffn(k, meta):
    xs = x_chunks(k)
    emit_ln(k, xs, out_f32_base=("AD", 0), out_bf_base=("AC", 0),
            xbf_base=("AC", 0), tag="l3f_ln")
    xn32 = [k.f32("AD", c * 2048, 2048) for c in range(NCHUNK)]
    xn = [k.bf("AC", c * 2048, 2048) for c in range(NCHUNK)]
    k.load_bp(meta["bp"], 296, tag="l3f_bp")
    gb = V(k.sc_t.ap()[0:1, 40:48], [("SC", 40)])
    k.dma(gb, dram_v(meta["gbias"]), tag="l3f_gb")
    wgs = k.f32("AW", 2048, 64)
    k.dma(wgs, dram_v(meta["wg"][0]), tag="l3f_wgld")
    glog = k.f32("AB", 0, 128)
    for tt in range(16):
        pv = k.psum(4 + tt % 2)
        pv = V(pv.ap[:, 0:8], pv.regs)
        for ic in range(NCHUNK):
            lhs = V(xn32[ic].ap[:, tt * 128:(tt + 1) * 128], xn32[ic].regs)
            r = V(wgs.ap[:, ic * 8:(ic + 1) * 8], wgs.regs)
            k.mm(pv, lhs, r, start=(ic == 0), stop=False, tag=f"l3f_g{tt}_{ic}")
        ones_row = V(k.consts.ap()[0:1, 0:128], [("consts",)])
        k.mm(pv, ones_row, gb, start=False, stop=True, tag=f"l3f_gb{tt}")
        k.act(V(glog.ap[:, tt * 8:(tt + 1) * 8], glog.regs), pv, AF.Sigmoid,
              tag=f"l3f_sig{tt}")
    f_all = k.f32("AB", 128, 128)
    for tt in range(16):
        g = V(glog.ap[:, tt * 8:(tt + 1) * 8], glog.regs)
        m1, m2, den, w1c, w2c = k.scol(24), k.scol(25), k.scol(26), k.scol(27), k.scol(28)
        eq1 = k.f32("AB", 256, 8)
        eq2 = k.f32("AB", 264, 8)
        k.reduce("vector", m1, g, ALU.max, tag=f"l3f_m1{tt}")
        k.ts("vector", eq1, g, m1, None, ALU.is_equal, tag=f"l3f_eq1{tt}")
        k.stt("vector", eq2, eq1, -60000.0, g, ALU.mult, ALU.add, tag=f"l3f_msk{tt}")
        k.reduce("vector", m2, eq2, ALU.max, tag=f"l3f_m2{tt}")
        k.ts("vector", eq2, eq2, m2, None, ALU.is_equal, tag=f"l3f_eq2{tt}")
        k.ts("vector", den, m1, m2, None, ALU.add, tag=f"l3f_d1{tt}")
        k.ts("vector", den, den, 1e-6, None, ALU.add, tag=f"l3f_d2{tt}")
        k.recip(den, den, tag=f"l3f_r{tt}")
        k.ts("vector", w1c, m1, den, None, ALU.mult, tag=f"l3f_w1{tt}")
        k.ts("vector", w2c, m2, den, None, ALU.mult, tag=f"l3f_w2{tt}")
        fv = V(f_all.ap[:, tt * 8:(tt + 1) * 8], f_all.regs)
        k.ts("vector", fv, eq1, w1c, None, ALU.mult, tag=f"l3f_f1{tt}")
        k.stt("vector", fv, eq2, w2c, fv, ALU.mult, ALU.add, tag=f"l3f_f2{tt}")
        k.ts("vector", fv, fv, 0.1, None, ALU.mult, tag=f"l3f_f01{tt}")
    fbf_all = V(k.sb["AB"].ap().bitcast(BF16)[:, 6656:6784], [("AB", 6)])
    k.copy("vector", fbf_all, f_all, tag="l3f_fcv")
    f_fm_bf = V(k.sb["AW"].ap().bitcast(BF16)[0:8, 10240:12288],
                [("AW", 10), ("AW", 11)])
    for tt in range(16):
        pvt = k.psum_bf(6, 0, 128)
        pvt = V(pvt.ap[0:8, :], pvt.regs)
        k.p.op("tensor", lambda e, o=pvt.ap, s=fbf_all.ap[:, tt * 8:(tt + 1) * 8]:
               e.transpose(o, s, k.ident_bf),
               reads=fbf_all.regs + [("consts",)], writes=pvt.regs, tag=f"l3f_tr{tt}")
        k.copy("vector", V(f_fm_bf.ap[:, tt * 128:(tt + 1) * 128], f_fm_bf.regs), pvt,
               tag=f"l3f_tre{tt}")
    emit_spill_x(k, tag="l3f_sp")
    for c in range(NCHUNK):
        k.memset(k.alt(), k.f32("AD", c * 2048, 2048), 0.0, tag=f"l3f_z{c}")
    for c in range(NCHUNK):
        k.ts("vector", k.scol(32 + c), V(k.bpcol(288 + c), [("BP",)]), 0.1, None,
             ALU.mult, tag=f"l3f_sbd{c}")

    def up_pass(wblocks, biasbase, dst_bf_base, rhs, tag):
        bi = 0
        for oc in range(32):
            wt = emit_stream_w(k, wblocks, oc, 0, 8, bi % 2, tag=tag)
            bi += 1
            for sl in range(2):
                pv = k.psum((oc % 2) * 2 + sl)
                for ic in range(NCHUNK):
                    k.mm(pv, wt[ic], rhs(ic, sl), start=(ic == 0), stop=(ic == 7),
                         tag=f"{tag}m{oc}_{sl}_{ic}")
                a = k.f32("AB", 2304 + (oc % 2) * 512, 512)
                k.act(a, pv, AF.Gelu, bias=V(k.bpcol(biasbase + oc), [("BP",)]),
                      tag=f"{tag}g{oc}_{sl}")
                hb = k.bf("AX", dst_bf_base + oc * 1024 + sl * 512, 512)
                k.copy(k.alt(), hb, a, tag=f"{tag}c{oc}_{sl}")

    def down_pass(wblocks, h_bf_base, th, evac, tag):
        h_e = [k.bf("AX", h_bf_base + ic * 1024, 1024) for ic in range(32)]
        bi2 = 0
        for ob in range(NCHUNK):
            for g in range(4):
                wt = emit_stream_w(k, wblocks, ob, g * 8, 8, bi2 % 2, tag=tag)
                bi2 += 1
                for sl in range(2):
                    pv = k.psum(5 + sl)
                    for i in range(8):
                        ic = g * 8 + i
                        r = V(h_e[ic].ap[:, sl * 512:(sl + 1) * 512], h_e[ic].regs)
                        k.mm(pv, wt[i], r, start=(ic == 0), stop=(ic == 31),
                             tag=f"{tag}m{ob}_{sl}_{ic}")
            for sl in range(2):
                evac(ob, sl, k.psum(5 + sl))

    for th in range(2):
        def rhs_th(ic, sl, th=th):
            return V(xn[ic].ap[:, th * 1024 + sl * 512:th * 1024 + (sl + 1) * 512],
                     xn[ic].regs)
        up_pass(meta["swu"], 256, 0, rhs_th, f"l3f_su{th}_")

        def ev_sh(ob, sl, pv, th=th):
            acc = k.f32("AD", ob * 2048 + th * 1024 + sl * 512, 512)
            k.stt("vector", acc, pv, 0.1, acc, ALU.mult, ALU.add,
                  tag=f"l3f_sde{th}_{ob}_{sl}")

        down_pass(meta["swd"], 0, th, ev_sh, f"l3f_sd{th}_")
        for e in range(8):
            frow = V(k.sb["AW"].ap().bitcast(BF16)[0:1, 2048:4096],
                     [("AW", 2), ("AW", 3)])
            if th == 0:
                pass
            for tt in range(16):
                pvt = k.psum_bf(7, 0, 128)
                pvt = V(pvt.ap[0:1, :], pvt.regs)
                k.p.op("tensor", lambda ee, o=pvt.ap,
                       s=fbf_all.ap[:, tt * 8 + e:tt * 8 + e + 1]:
                       ee.transpose(o, s, k.ident_bf),
                       reads=fbf_all.regs + [("consts",)], writes=pvt.regs,
                       tag=f"l3f_fr{th}_{e}_{tt}")
                k.copy("vector", V(frow.ap[:, tt * 128:(tt + 1) * 128], frow.regs), pvt,
                       tag=f"l3f_fre{th}_{e}_{tt}")
            fbc = k.f32("AW", 0, 1024)
            for sl in range(2):
                pv = k.psum(4)
                lhs = V(k.ones_bf[0:1, :], [("consts",)])
                r = V(frow.ap[:, th * 1024 + sl * 512:th * 1024 + (sl + 1) * 512],
                      frow.regs)
                k.mm(pv, lhs, r, start=True, stop=True, tag=f"l3f_fb{th}_{e}_{sl}")
                k.copy("vector", V(fbc.ap[:, sl * 512:(sl + 1) * 512], fbc.regs), pv,
                       tag=f"l3f_fbe{th}_{e}_{sl}")
            up_pass(meta["upw"][e], e * 32, 0, rhs_th, f"l3f_u{th}_{e}_")

            def ev_ex(ob, sl, pv, th=th):
                t1 = k.f32("AB", 2304 + sl * 512, 512)
                k.tt("vector", t1, pv, V(fbc.ap[:, sl * 512:(sl + 1) * 512], fbc.regs),
                     ALU.mult, tag=f"l3f_xf{th}_{e}_{ob}_{sl}")
                acc = k.f32("AD", ob * 2048 + th * 1024 + sl * 512, 512)
                k.tt("vector", acc, t1, acc, ALU.add, tag=f"l3f_xa{th}_{e}_{ob}_{sl}")

            down_pass(meta["dnw"][e], 0, th, ev_ex, f"l3f_d{th}_{e}_")
    for ob in range(NCHUNK):
        wt = emit_stream_w(k, meta["dbp"], ob, 0, 1, ob % 2, ow=128, kp=8, tag="l3f_db")
        for tsl in range(4):
            pv = k.psum(7)
            r = V(f_fm_bf.ap[:, tsl * 512:(tsl + 1) * 512], f_fm_bf.regs)
            k.mm(pv, wt[0], r, start=True, stop=True, tag=f"l3f_dbm{ob}_{tsl}")
            acc = k.f32("AD", ob * 2048 + tsl * 512, 512)
            k.tt("vector", acc, pv, acc, ALU.add, tag=f"l3f_dbe{ob}_{tsl}")
    for ob in range(NCHUNK):
        for tsl in range(4):
            acc = k.f32("AD", ob * 2048 + tsl * 512, 512)
            k.ts("vector", acc, acc, k.scol(32 + ob), None, ALU.add,
                 tag=f"l3f_sba{ob}_{tsl}")
    m_all = k.f32("AB", 0, 2048)
    k.act(m_all, k.f32("AD", 0, 2048), AF.Abs, tag="l3f_ab0")
    for c in range(1, NCHUNK):
        t_abs = k.f32("AW", 4096, 2048)
        k.act(t_abs, k.f32("AD", c * 2048, 2048), AF.Abs, tag=f"l3f_ab{c}")
        k.tt("vector", m_all, m_all, t_abs, ALU.max, tag=f"l3f_am{c}")
    mbf = V(k.sb["AW"].ap().bitcast(BF16)[:, 8192:10240], [("AW", 8), ("AW", 9)])
    k.copy("vector", mbf, m_all, tag="l3f_mbf")
    rrow_bf = V(k.sb["AW"].ap().bitcast(BF16)[0:1, 10240:12288],
                [("AW", 10), ("AW", 11)])
    rcol_bf = V(k.sb["AB"].ap().bitcast(BF16)[:, 6656:6657], [("AB", 6)])
    for tt in range(16):
        pvt = k.psum_bf(6, 0, 128)
        k.transpose(pvt, V(mbf.ap[:, tt * 128:(tt + 1) * 128], mbf.regs),
                    tag=f"l3f_mt{tt}")
        rmax = k.scol(48)
        k.reduce("vector", rmax, pvt, ALU.max, tag=f"l3f_rm{tt}")
        k.ts("vector", rmax, rmax, 1e-6, None, ALU.add, tag=f"l3f_re{tt}")
        k.recip(rmax, rmax, tag=f"l3f_rr{tt}")
        k.copy("vector", rcol_bf, rmax, tag=f"l3f_rc{tt}")
        pv2 = k.psum_bf(7, 0, 128)
        pv2 = V(pv2.ap[0:1, :], pv2.regs)
        k.p.op("tensor", lambda e, o=pv2.ap, s=rcol_bf.ap:
               e.transpose(o, s, k.ident_bf),
               reads=rcol_bf.regs + [("consts",)], writes=pv2.regs, tag=f"l3f_rt{tt}")
        k.copy("vector", V(rrow_bf.ap[:, tt * 128:(tt + 1) * 128], rrow_bf.regs), pv2,
               tag=f"l3f_rre{tt}")
    rb = k.f32("AW", 0, 2048)
    for tsl in range(4):
        pv = k.psum(4)
        k.mm(pv, V(k.ones_bf[0:1, :], [("consts",)]),
             V(rrow_bf.ap[:, tsl * 512:(tsl + 1) * 512], rrow_bf.regs),
             start=True, stop=True, tag=f"l3f_rb{tsl}")
        k.copy("vector", V(rb.ap[:, tsl * 512:(tsl + 1) * 512], rb.regs), pv,
               tag=f"l3f_rbe{tsl}")
    for c in range(NCHUNK):
        xsp = k.f32("AB", 0, 2048)
        k.dma(xsp, V(k.xsp[c * 128:(c + 1) * 128, :], [("dram", "xsp")]),
              tag=f"l3f_xr{c}")
        t = k.f32("AW", 4096, 2048)
        k.tt("vector", t, k.f32("AD", c * 2048, 2048), rb, ALU.mult, tag=f"l3f_n{c}")
        k.tt(k.alt(), k.f32("AX", c * 2048, 2048), t, xsp, ALU.add, tag=f"l3f_o{c}")


def prep_l3_ffn(host, pc, inp):
    upw = np.asarray(inp["L3_ffn_up_w"], np.float32)
    dnw = np.asarray(inp["L3_ffn_down_w"], np.float32)
    host.add("l3f_upw", np.stack([pack_lhsT(upw[e]) for e in range(8)]))
    host.add("l3f_dnw", np.stack([pack_lhsT(dnw[e]) for e in range(8)]))
    host.add("l3f_swu", pack_lhsT(np.asarray(inp["L3_ffn_sw_up"], np.float32)))
    host.add("l3f_swd", pack_lhsT(np.asarray(inp["L3_ffn_sw_down"], np.float32)))
    host.add("l3f_wg", pack_lhsT(np.asarray(inp["L3_ffn_gate_w"], np.float32), ow=8))
    host.add("l3f_dbp", pack_lhsT(np.asarray(inp["L3_ffn_down_b"], np.float32)))
    host.add("l3f_gbias", np.asarray(inp["L3_ffn_bias"], np.float32).reshape(1, 8))
    up_b = np.asarray(inp["L3_ffn_up_b"], np.float32)
    sb_up = np.asarray(inp["L3_ffn_sb_up"], np.float32)
    sb_down = np.asarray(inp["L3_ffn_sb_down"], np.float32)
    bp = np.zeros((128, 296), np.float32)
    for e in range(8):
        for c in range(32):
            bp[:, e * 32 + c] = up_b[e, c * 128:(c + 1) * 128]
    for c in range(32):
        bp[:, 256 + c] = sb_up[c * 128:(c + 1) * 128]
    for c in range(8):
        bp[:, 288 + c] = sb_down[c * 128:(c + 1) * 128]
    host.add("l3f_bpa", bp)
    return {"upw": "l3f_upw", "dnw": "l3f_dnw", "swu": "l3f_swu", "swd": "l3f_swd",
            "wg": "l3f_wg", "dbp": "l3f_dbp", "gbias": "l3f_gbias", "bp": "l3f_bpa"}



def build_slot_program(slots, host_arrays, param_array, want_trace=False):
    nc = bass.Bass("TRN2", target_bir_lowering=False)
    d = {}
    for name, arr in host_arrays.items():
        d[name] = nc.dram_tensor(name, list(arr.shape), F32, kind="ExternalInput").ap()
    d_y = nc.dram_tensor("y_fm", [D, T], F32, kind="ExternalOutput").ap()
    prog = Prog(nc)
    k = Ctx(nc, prog)
    k.xsp = nc.dram_tensor("xsp", [D, T], F32, kind="Internal").ap()
    load_basics_ops(k, d["consts_t"], d["params_t"], param_array.shape[1])
    emit_load_x(k, d["x_fm"])
    for emit_fn, meta in slots:
        emit_fn(k, {key: (d[v] if isinstance(v, str) and v in d else v)
                    for key, v in meta.items()})
    emit_store_x(k, d_y, ch="gpsimd")
    prog.build()
    k.close()
    return nc


def load_basics_ops(k, d_consts, d_params, n_param_cols):
    k.p.op("sync", lambda e: e.dma_start(out=k.consts.ap()[:, :], in_=d_consts),
           writes=[("consts",)], dma=True, tag="ld_consts")
    k.p.op("sync", lambda e, n=n_param_cols: e.dma_start(
        out=k.params_t.ap()[:, 0:n], in_=d_params[:, 0:n]),
        writes=[("params",)], dma=True, tag="ld_params")



def prep_l0_attn(host, pc, inp):
    wifzo = np.concatenate([inp["L0_attn_Wi"], inp["L0_attn_Wf"],
                            inp["L0_attn_Wz"], inp["L0_attn_Wo"]], axis=0)
    host.add("l0a_wifzo", pack_lhsT(wifzo.T, ow=64))
    host.add("l0a_wp", pack_lhsT(np.asarray(inp["L0_attn_Wp"]).T))
    meta = {
        "wifzo": "l0a_wifzo", "wp": "l0a_wp",
        "ln_w": pc.add_vecD(inp["L0_attn_ln_w"]),
        "ln_b": pc.add_vecD(inp["L0_attn_ln_b"]),
        "b_i": pc.add(inp["L0_attn_bi"]),
        "b_f": pc.add(inp["L0_attn_bf"]),
        "b_z": pc.add(inp["L0_attn_bz"]),
        "b_o": pc.add(inp["L0_attn_bo"]),
        "bp": pc.add_vecD(inp["L0_attn_bp"]),
    }
    return meta


SLOT_PREP = {"l0_attn": (prep_l0_attn, emit_l0_attn),
             "l0_ffn": (prep_l0_ffn, emit_l0_ffn),
             "l1_attn": (prep_l1_attn, emit_l1_attn),
             "l1_ffn": (prep_l1_ffn, emit_l1_ffn),
             "l2_attn": (prep_l2_attn, emit_l2_attn),
             "l2_ffn": (prep_l2_ffn, emit_l2_ffn),
             "l3_attn": (prep_l3_attn, emit_l3_attn),
             "l3_ffn": (prep_l3_ffn, emit_l3_ffn)}


def run_slots(slot_names, inputs, x_fm_percore):
    host = Host()
    pc = ParamCols()
    slots = []
    for s in slot_names:
        prep, emit = SLOT_PREP[s]
        meta = prep(host, pc, inputs)
        slots.append((emit, meta))
    param_array = pc.array()
    host.add("consts_t", make_consts())
    host.add("params_t", param_array)
    host.add("x_fm", x_fm_percore[0])
    nc = build_slot_program(slots, host.arrays, param_array)
    in_maps = []
    for core in range(8):
        m = dict(host.arrays)
        m["x_fm"] = x_fm_percore[core]
        in_maps.append(m)
    res = run_bass_kernel_spmd(nc, in_maps, core_ids=list(range(8)))
    return [r["y_fm"] for r in res.results]


ALL_SLOTS = ["l0_attn", "l0_ffn", "l1_attn", "l1_ffn", "l2_attn", "l2_ffn",
             "l3_attn", "l3_ffn"]


def kernel(**inputs):
    x = np.asarray(inputs["L0_attn_x"], np.float32)
    x_fm = [np.ascontiguousarray(x[b].T) for b in range(8)]
    outs = run_slots(ALL_SLOTS, inputs, x_fm)
    return np.stack([np.asarray(o).T for o in outs]).astype(np.float32)


# version 04
# speedup vs baseline: 1.1420x; 1.0009x over previous
import numpy as np
import ml_dtypes
import concourse.bass as bass
import concourse.mybir as mybir
from concourse.bass_utils import run_bass_kernel_spmd

F32 = mybir.dt.float32
BF16 = mybir.dt.bfloat16
AF = mybir.ActivationFunctionType
ALU = mybir.AluOpType
AXL = mybir.AxisListType
BF_NP = ml_dtypes.bfloat16

B, T, D = 8, 2048, 1024
DBG_SINK = {}
NCHUNK = D // 128
NTSL = T // 512
EPS = 1e-5



class Prog:

    def __init__(self, nc):
        self.nc = nc
        self.ops = []
        self.last_write = {}
        self.readers = {}

    def op(self, engine, emit, reads=(), writes=(), dma=False, tag="", slow=False,
           width=0):
        idx = len(self.ops)
        deps = set()
        for r in reads:
            w = self.last_write.get(r)
            if w is not None:
                deps.add(w)
        for r in writes:
            w = self.last_write.get(r)
            if w is not None:
                deps.add(w)
            for rd in self.readers.get(r, ()):
                deps.add(rd)
        for r in writes:
            self.last_write[r] = idx
            self.readers[r] = []
        for r in reads:
            self.readers.setdefault(r, []).append(idx)
        self.ops.append(dict(engine=engine, emit=emit, deps=sorted(deps),
                             dma=dma, tag=tag, idx=idx, slow=slow, width=width))
        return idx

    def build(self):
        nc = self.nc
        ops = self.ops
        n = len(ops)

        def domain(o):
            return ("dma_" + o["engine"]) if o["dma"] else o["engine"]

        def skip_edge(po, o):
            if po["dma"] or o["dma"] or po["engine"] != o["engine"]:
                return False
            if po["engine"] == "tensor":
                return True
            return (not po["slow"]) and po["width"] >= 512 and o["width"] >= 512

        needed = [False] * n
        for o in ops:
            for d in o["deps"]:
                if skip_edge(ops[d], o):
                    continue
                needed[d] = True
        for o in ops:
            if o["dma"]:
                needed[o["idx"]] = True
        counters = {}
        val = [0] * n
        for o in ops:
            if needed[o["idx"]]:
                d = domain(o)
                counters[d] = counters.get(d, 0) + (16 if o["dma"] else 1)
                val[o["idx"]] = counters[d]
        sem_ctx = {}
        sems = []
        for d in sorted(counters.keys()):
            cm = nc.semaphore("sem_" + d)
            sem_ctx[d] = cm.__enter__()
            sems.append(cm)
        waited = {}
        plans = {}
        for o in ops:
            plans.setdefault(o["engine"], [])
        for o in ops:
            wl = {}
            for d in o["deps"]:
                po = ops[d]
                if skip_edge(po, o):
                    continue
                pd = domain(po)
                wl[pd] = max(wl.get(pd, 0), val[po["idx"]])
            eng = o["engine"]
            final = []
            for pd, v in sorted(wl.items()):
                kk = (eng, pd)
                if waited.get(kk, 0) >= v:
                    continue
                waited[kk] = v
                final.append((pd, v))
            plans[eng].append((o, final, val[o["idx"]] if needed[o["idx"]] else None))
        term = {}
        for d, c in counters.items():
            if d.startswith("dma_"):
                term[d[4:]] = (d, c)
        for ch in list(term):
            if ch not in plans:
                plans[ch] = []

        self.trace = []
        for eng in plans:
            for (o, waits, myval) in plans[eng]:
                self.trace.append((o["idx"], eng, o["tag"], waits,
                                   myval, ("dma" if o["dma"] else "")))
        self.trace.sort()
        block_cm = nc.Block()
        block = block_cm.__enter__()
        for eng in plans:
            def body(e, eng=eng):
                for (o, waits, myval) in plans[eng]:
                    for (pd, v) in waits:
                        e.wait_ge(sem_ctx[pd], v)
                    inst = o["emit"](e)
                    if myval is not None:
                        assert inst is not None, f"op {o['tag']} needs inc, emitted None"
                        inst.then_inc(sem_ctx[("dma_" + eng) if o["dma"] else eng],
                                      16 if o["dma"] else 1)
                if eng in term:
                    pd, c = term[eng]
                    e.wait_ge(sem_ctx[pd], c)
            getattr(block, eng)(body)
        block_cm.__exit__(None, None, None)
        for cm in sems:
            cm.__exit__(None, None, None)



class Host:
    def __init__(self):
        self.arrays = {}

    def add(self, name, arr):
        self.arrays[name] = np.ascontiguousarray(arr)
        return name


class ParamCols:

    def __init__(self):
        self.cols = []

    def add(self, vec):
        v = np.zeros(128, np.float32)
        v[: len(vec)] = np.asarray(vec, np.float32).reshape(-1)
        self.cols.append(v)
        return len(self.cols) - 1

    def add_vecD(self, vecD):
        base = len(self.cols)
        for c in range(NCHUNK):
            self.add(np.asarray(vecD, np.float32).reshape(-1)[c * 128:(c + 1) * 128])
        return base

    def array(self):
        if not self.cols:
            return np.zeros((128, 1), np.float32)
        return np.stack(self.cols, axis=1).astype(np.float32)


def pack_bf16_as_f32(bfarr):
    raw = np.ascontiguousarray(bfarr.astype(BF_NP)).view(np.uint16)
    lo = raw[:, 0::2].astype(np.uint32)
    hi = raw[:, 1::2].astype(np.uint32)
    return ((hi << 16) | lo).view(np.float32)


def pack_lhsT(w_in_out, ow=128):
    w = np.asarray(w_in_out, np.float32)
    n_in, n_out = w.shape
    kp = min(n_in, 128)
    assert n_in % kp == 0 and n_out % ow == 0
    n_ic, n_ob = n_in // kp, n_out // ow
    a = w.reshape(n_ic, kp, n_ob, ow)
    return np.ascontiguousarray(a.transpose(2, 1, 0, 3).reshape(n_ob, kp, n_ic * ow))


CONSTF = [1e-5, 1e-6, 0.5, -0.5, 1.0, -1.0, 20.0, -20.0, 2.0]


GAMMA = 0.96875


def make_consts():
    bf = np.zeros((128, 256), BF_NP)
    bf[:, 0:128] = BF_NP(1.0)
    bf[:, 128:256] = np.eye(128, dtype=BF_NP)
    out = np.zeros((128, 1536), np.float32)
    out[:, 0:128] = 1.0
    out[:, 128:256] = pack_bf16_as_f32(bf)
    for i, v in enumerate(CONSTF):
        out[:, 256 + i] = v
    ii = np.arange(128)
    out[:, 320:448] = np.where(ii[None, :] <= ii[:, None], 0.0, -2000.0).astype(np.float32)
    sw = np.zeros((128, 128), BF_NP)
    sw[ii, ii ^ 1] = BF_NP(1.0)
    out[:, 448:512] = pack_bf16_as_f32(sw)
    j = np.arange(128)
    out = np.concatenate([out, np.zeros((128, 128), np.float32)], axis=1)
    out[:, 512:640] = (GAMMA ** j)[None, :].astype(np.float32)
    out[:, 640:768] = (GAMMA ** (-j.astype(np.float64)))[None, :].astype(np.float32)
    out[:, 768:896] = (ii[None, :] <= ii[:, None]).astype(np.float32)
    return np.ascontiguousarray(out[:, 0:896])



ARENAS = [("AX", 16384), ("AD", 16384), ("AC", 8192), ("AB", 4096), ("AW", 6144)]
ARENA_REG = {"AX": 2048, "AD": 1024, "AC": 1024, "AB": 512, "AW": 512}


class V:
    __slots__ = ("ap", "regs")

    def __init__(self, ap, regs):
        self.ap = ap
        self.regs = regs


class Ctx:
    def __init__(self, nc, prog):
        self.nc = nc
        self.p = prog
        self.stack = []
        self.sb = {}
        for nm, cols in ARENAS:
            cm = nc.sbuf_tensor(nm, [128, cols], F32)
            self.stack.append(cm)
            self.sb[nm] = cm.__enter__()
        self.pb = []
        for i in range(8):
            cm = nc.psum_tensor(f"pb{i}", [128, 512], F32)
            self.stack.append(cm)
            self.pb.append(cm.__enter__())
        cm = nc.sbuf_tensor("consts", [128, 896], F32)
        self.stack.append(cm)
        self.consts = cm.__enter__()
        for i, v in enumerate(CONSTF):
            nc.const_aps.register(F32, v, self.consts.ap()[:, 256 + i:257 + i])
        cm = nc.sbuf_tensor("params", [128, 128], F32)
        self.stack.append(cm)
        self.params_t = cm.__enter__()
        cm = nc.sbuf_tensor("SC", [128, 64], F32)
        self.stack.append(cm)
        self.sc_t = cm.__enter__()
        cm = nc.sbuf_tensor("BP", [128, 512], F32)
        self.stack.append(cm)
        self.bp_t = cm.__enter__()
        self.rr = 0

    def bpcol(self, col):
        return self.bp_t.ap()[:, col:col + 1]

    def load_bp(self, d_bp, ncols, tag="ldbp"):
        self.p.op("sync", lambda e, d=d_bp, n=ncols: e.dma_start(
            out=self.bp_t.ap()[:, 0:n], in_=d), writes=[("BP",)], dma=True, tag=tag)

    def scol(self, col, n=1):
        return V(self.sc_t.ap()[:, col:col + n], [("SC", col)])

    def close(self):
        for cm in reversed(self.stack):
            cm.__exit__(None, None, None)

    def f32(self, name, col0, ncols):
        g = ARENA_REG[name]
        regs = [(name, i) for i in range(col0 // g, (col0 + ncols + g - 1) // g)]
        return V(self.sb[name].ap()[:, col0:col0 + ncols], regs)

    def bf(self, name, bcol0, nbcols):
        g = ARENA_REG[name]
        c0, c1 = bcol0 // 2, (bcol0 + nbcols + 1) // 2
        regs = [(name, i) for i in range(c0 // g, (c1 + g - 1) // g)]
        return V(self.sb[name].ap().bitcast(BF16)[:, bcol0:bcol0 + nbcols], regs)

    def psum(self, i, ncols=512):
        return V(self.pb[i].ap()[:, 0:ncols], [(f"pb{i}",)])

    def psum_bf(self, i, bcol0, nbcols):
        return V(self.pb[i].ap().bitcast(BF16)[:, bcol0:bcol0 + nbcols], [(f"pb{i}",)])

    def pcol(self, col):
        return self.params_t.ap()[:, col:col + 1]

    @property
    def ones_bf(self):
        return self.consts.ap().bitcast(BF16)[:, 256:384]

    @property
    def ident_bf(self):
        return self.consts.ap().bitcast(BF16)[:, 384:512]

    @property
    def causal_mask(self):
        return self.consts.ap()[:, 320:448]

    @property
    def swap_bf(self):
        return self.consts.ap().bitcast(BF16)[:, 896:1024]

    def gamma_q(self, w):
        assert w <= 128
        return self.consts.ap()[:, 512:512 + w]

    def gamma_k(self, w):
        assert w <= 128
        return self.consts.ap()[:, 640:640 + w]

    @property
    def caus01(self):
        return self.consts.ap()[:, 768:896]

    def alt(self, engines=("vector", "gpsimd")):
        self.rr += 1
        return engines[self.rr % len(engines)]

    def dma(self, dst, src, ch="sync", tag="dma"):
        dv, dr = (dst.ap, dst.regs) if isinstance(dst, V) else (dst, [("dram", dst.tensor.name)])
        sv, sr = (src.ap, src.regs) if isinstance(src, V) else (src, [("dram", src.tensor.name)])
        return self.p.op(ch, lambda e, a=dv, b=sv: e.dma_start(out=a, in_=b),
                         reads=sr, writes=dr, dma=True, tag=tag)

    def tt(self, eng, out, a, b, op, tag="tt"):
        return self.p.op(eng, lambda e, o=out.ap, x=a.ap, y=b.ap: e.tensor_tensor(o, x, y, op),
                         reads=a.regs + b.regs, writes=out.regs, tag=tag,
                         width=out.ap.shape[-1])

    @staticmethod
    def _sc(s):
        if s is None or np.isscalar(s):
            return s, []
        if isinstance(s, V):
            return s.ap, list(s.regs)
        return s, [("params",)]

    def ts(self, eng, out, a, s1, s2, op0, op1=ALU.bypass, accum=None, tag="ts"):
        s1v, r1 = self._sc(s1)
        s2v, r2 = self._sc(s2)
        wr = out.regs + (accum.regs if accum is not None else [])
        return self.p.op(eng, lambda e, o=out.ap, x=a.ap: e.tensor_scalar(
            o, x, s1v, s2v, op0, op1, accum_out=(accum.ap if accum is not None else None)),
            reads=a.regs + r1 + r2, writes=wr, tag=tag, width=out.ap.shape[-1])

    def stt(self, eng, out, a, s, b, op0, op1, tag="stt"):
        if eng == "gpsimd":
            eng = "vector"
        sv, r1 = self._sc(s)
        return self.p.op(eng, lambda e, o=out.ap, x=a.ap, y=b.ap: e.scalar_tensor_tensor(
            o, x, sv, y, op0, op1), reads=a.regs + b.regs + r1, writes=out.regs,
            tag=tag, width=out.ap.shape[-1])

    def act(self, out, a, func, bias=0.0, scale=1.0, accum=None, tag="act"):
        bv, r1 = self._sc(bias)
        sv, r2 = self._sc(scale)
        extra = [("consts",)] + r1 + r2
        wr = out.regs + (accum.regs if accum is not None else [])
        return self.p.op("scalar", lambda e, o=out.ap, x=a.ap: e.activation(
            o, x, func, bias=bv, scale=sv,
            accum_out=(accum.ap if accum is not None else None)),
            reads=a.regs + extra, writes=wr, tag=tag, width=out.ap.shape[-1])

    def copy(self, eng, out, a, tag="copy"):
        if eng == "scalar":
            return self.p.op(eng, lambda e, o=out.ap, x=a.ap: e.copy(o, x),
                             reads=a.regs, writes=out.regs, tag=tag,
                             width=out.ap.shape[-1])
        return self.p.op(eng, lambda e, o=out.ap, x=a.ap: e.tensor_copy(o, x),
                         reads=a.regs, writes=out.regs, tag=tag,
                         width=out.ap.shape[-1])

    def mm(self, psum, lhsT, rhs, start, stop, tag="mm"):
        return self.p.op("tensor", lambda e, o=psum.ap, l=lhsT.ap, r=rhs.ap:
                         e.matmul(o, l, r, start=start, stop=stop),
                         reads=lhsT.regs + rhs.regs, writes=psum.regs, tag=tag,
                         width=psum.ap.shape[-1])

    def transpose(self, psum, src, tag="tr"):
        cregs = [("consts",)]
        return self.p.op("tensor", lambda e, o=psum.ap, s=src.ap:
                         e.transpose(o, s, self.ident_bf),
                         reads=src.regs + cregs, writes=psum.regs, tag=tag)

    def reduce(self, eng, out, a, op, absval=False, tag="red"):
        return self.p.op(eng, lambda e, o=out.ap, x=a.ap: e.tensor_reduce(
            o, x, AXL.X, op, apply_absolute_value=absval),
            reads=a.regs, writes=out.regs, tag=tag, slow=True)

    def recip(self, out, a, tag="recip"):
        return self.p.op("vector", lambda e, o=out.ap, x=a.ap: e.reciprocal(o, x),
                         reads=a.regs, writes=out.regs, tag=tag, slow=True)

    def scan(self, out, d0, d1, init, op0, op1, tag="scan"):
        return self.p.op("vector", lambda e, o=out.ap, a=d0.ap, b=d1.ap:
                         e.tensor_tensor_scan(o, a, b, init, op0, op1),
                         reads=d0.regs + d1.regs, writes=out.regs, tag=tag, slow=True)

    def memset(self, eng, out, v, tag="memset"):
        return self.p.op(eng, lambda e, o=out.ap: e.memset(o, v),
                         reads=[], writes=out.regs, tag=tag, slow=True)

    def vconst(self, v):
        return V(v, [("params",)])


def dram_v(ap):
    return V(ap, [("dram", ap.tensor.name)])



def x_chunks(k):
    return [k.f32("AX", c * 2048, 2048) for c in range(NCHUNK)]


def emit_load_x(k, d_x):
    for c in range(NCHUNK):
        dst = k.f32("AX", c * 2048, 2048)
        k.dma(dst, dram_v(d_x[c * 128:(c + 1) * 128, :]), tag=f"ldx{c}")


def emit_store_x(k, d_y, ch="gpsimd"):
    for c in range(NCHUNK):
        src = k.f32("AX", c * 2048, 2048)
        k.dma(dram_v(d_y[c * 128:(c + 1) * 128, :]), src, ch=ch, tag=f"stx{c}")


def emit_ln(k, src, out_bf_base=None, out_f32_base=None, affine=None,
            xbf_base=("AC", 0), tag="ln"):
    xbf = []
    for c in range(NCHUNK):
        dst = k.bf(xbf_base[0], xbf_base[1] + c * 2048, 2048)
        k.copy(k.alt(), dst, src[c], tag=f"{tag}_xbf{c}")
        xbf.append(dst)
    for c in range(NCHUNK):
        sq = k.bf("AW", 8192 + (c % 2) * 2048, 2048)
        k.act(sq, src[c], AF.Square, tag=f"{tag}_sq{c}")
        for tsl in range(NTSL):
            pbv = k.psum(4 + tsl)
            k.p.op("tensor", lambda e, o=pbv.ap, s=sq.ap, t=tsl, st=(c == 0), sp=(c == NCHUNK - 1):
                   e.matmul(o, k.ones_bf, s[:, t * 512:(t + 1) * 512], start=st, stop=sp),
                   reads=sq.regs + [("consts",)], writes=pbv.regs, tag=f"{tag}_mmq{c}_{tsl}")
    for c in range(NCHUNK):
        for tsl in range(NTSL):
            pbv = k.psum(tsl)
            k.p.op("tensor", lambda e, o=pbv.ap, s=xbf[c].ap, t=tsl, st=(c == 0), sp=(c == NCHUNK - 1):
                   e.matmul(o, k.ones_bf, s[:, t * 512:(t + 1) * 512], start=st, stop=sp),
                   reads=xbf[c].regs + [("consts",)], writes=pbv.regs, tag=f"{tag}_mms{c}_{tsl}")
    mu = k.f32("AB", 0, 2048)
    rstd = k.f32("AB", 2048, 2048)
    for tsl in range(NTSL):
        musl = k.f32("AB", tsl * 512, 512)
        rsl = k.f32("AB", 2048 + tsl * 512, 512)
        k.ts("vector", musl, k.psum(tsl), 1.0 / D, None, ALU.mult, tag=f"{tag}_mu{tsl}")
        k.act(rsl, musl, AF.Square, tag=f"{tag}_mu2_{tsl}")
        k.stt("vector", rsl, k.psum(4 + tsl), 1.0 / D, rsl, ALU.mult, ALU.subtract,
              tag=f"{tag}_var{tsl}")
        k.act(rsl, rsl, AF.Ln, bias=EPS, tag=f"{tag}_lnv{tsl}")
        k.act(rsl, rsl, AF.Exp, scale=-0.5, tag=f"{tag}_rstd{tsl}")
    for c in range(NCHUNK):
        scr = k.f32("AW", (c % 2) * 4096, 2048)
        k.tt("vector", scr, src[c], mu, ALU.subtract, tag=f"{tag}_sub{c}")
        if affine is None:
            if out_f32_base is not None:
                dstf = k.f32(out_f32_base[0], out_f32_base[1] + c * 2048, 2048)
                k.tt("vector", dstf, scr, rstd, ALU.mult, tag=f"{tag}_mulf{c}")
                if out_bf_base is not None:
                    dstb = k.bf(out_bf_base[0], out_bf_base[1] + c * 2048, 2048)
                    k.copy(k.alt(("gpsimd", "scalar")), dstb, dstf, tag=f"{tag}_cpb{c}")
            elif out_bf_base is not None:
                dstb = k.bf(out_bf_base[0], out_bf_base[1] + c * 2048, 2048)
                k.tt("vector", dstb, scr, rstd, ALU.mult, tag=f"{tag}_mulb{c}")
        else:
            wcol, bcol = affine
            scr2 = k.f32("AW", ((c + 1) % 2) * 4096, 2048)
            k.tt("vector", scr2, scr, rstd, ALU.mult, tag=f"{tag}_mul{c}")
            dstb = k.bf(out_bf_base[0], out_bf_base[1] + c * 2048, 2048)
            k.ts("vector", dstb, scr2, k.pcol(wcol + c), k.pcol(bcol + c),
                 ALU.mult, ALU.add, tag=f"{tag}_aff{c}")
    return mu, rstd


def emit_stream_w(k, d_wblocks, oc, g, gi, buf, ow=128, kp=128, neg=False, tag="w"):
    assert gi * ow <= 1024
    stage = k.f32("AW", 2048 + (buf % 2) * 1024, gi * ow)
    stage = V(stage.ap[0:kp, :], stage.regs)
    k.dma(stage, dram_v(d_wblocks[oc, :, g * ow:(g + gi) * ow]), tag=f"{tag}_ld{oc}_{g}")
    wbfv = k.bf("AW", 8192 + (buf % 2) * 1024, gi * ow)
    wbfv = V(wbfv.ap[0:kp, :], wbfv.regs)
    if neg:
        k.ts("vector", wbfv, stage, -1.0, None, ALU.mult, tag=f"{tag}_cv{oc}_{g}")
    else:
        k.copy(k.alt(("vector", "scalar")), wbfv, stage, tag=f"{tag}_cv{oc}_{g}")
    return [V(wbfv.ap[:, i * ow:(i + 1) * ow], wbfv.regs) for i in range(gi)]


def emit_proj(k, d_wblocks, n_ic, oc_list, rhs, evac,
              psum_sets=((0, 1, 2, 3), (4, 5, 6, 7)), ntsl=NTSL, ow=128, kp=128,
              tag="proj", buf0=0):
    bi = buf0
    gmax = 1024 // ow
    for j, oc in enumerate(oc_list):
        pset = psum_sets[j % len(psum_sets)]
        for g in range(0, n_ic, gmax):
            gi = min(gmax, n_ic - g)
            wt = emit_stream_w(k, d_wblocks, oc, g, gi, bi % 2, ow=ow, kp=kp, tag=tag)
            bi += 1
            for tsl in range(ntsl):
                pv = k.psum(pset[tsl])
                pv = V(pv.ap[0:ow, :], pv.regs)
                for i in range(gi):
                    ic = g + i
                    k.mm(pv, wt[i], rhs(ic, tsl), start=(ic == 0), stop=(ic == n_ic - 1),
                         tag=f"{tag}_mm{oc}_{tsl}_{ic}")
        for tsl in range(ntsl):
            pv = k.psum(pset[tsl])
            evac(oc, tsl, V(pv.ap[0:ow, :], pv.regs))
    return bi


def emit_l0_attn(k, meta):
    xs = x_chunks(k)
    emit_ln(k, xs, out_f32_base=("AD", 0), xbf_base=("AC", 0), tag="l0a_ln1")
    xn0 = [k.f32("AD", c * 2048, 2048) for c in range(NCHUNK)]
    emit_ln(k, xn0, out_bf_base=("AC", 0), affine=(meta["ln_w"], meta["ln_b"]),
            xbf_base=("AC", 0), tag="l0a_ln2")
    xn = [k.bf("AC", c * 2048, 2048) for c in range(NCHUNK)]

    def r64(v):
        return V(v.ap[0:64, :], v.regs)

    li = r64(k.f32("AB", 0, 2048))
    lf = r64(k.f32("AB", 2048, 2048))
    zt = r64(k.f32("AD", 4096, 2048))
    ot = r64(k.f32("AD", 6144, 2048))
    mt = r64(k.f32("AD", 8192, 2048))
    mprev = r64(k.f32("AD", 10240, 2048))
    fs = r64(k.f32("AD", 12288, 2048))
    isz = r64(k.f32("AD", 14336, 2048))
    dsts = [li, lf, zt, ot]
    bcols = [meta["b_i"], meta["b_f"], meta["b_z"], meta["b_o"]]

    def rhs(ic, tsl):
        return V(xn[ic].ap[:, tsl * 512:(tsl + 1) * 512], xn[ic].regs)

    def evac(oc, tsl, pv):
        dst = V(dsts[oc].ap[:, tsl * 512:(tsl + 1) * 512], dsts[oc].regs)
        bias = k.pcol(bcols[oc])[0:64, :]
        if oc == 0 or oc == 1:
            k.ts("vector", dst, pv, bias, None, ALU.add, tag=f"l0a_b{oc}_{tsl}")
            k.ts("vector", dst, dst, 20.0, -20.0, ALU.min, ALU.max,
                 tag=f"l0a_clip{oc}_{tsl}")
        elif oc == 2:
            k.act(dst, pv, AF.Tanh, bias=bias, tag=f"l0a_z{tsl}")
        else:
            k.act(dst, pv, AF.Sigmoid, bias=bias, tag=f"l0a_o{tsl}")

    emit_proj(k, meta["wifzo"], 8, [0, 1, 2, 3], rhs, evac, ow=64, tag="l0a_p")

    k.scan(mt, lf, li, 0.0, ALU.add, ALU.max, tag="l0a_mscan")
    k.memset("gpsimd", V(mprev.ap[:, 0:1], mprev.regs), 0.0, tag="l0a_mp0")
    k.copy("gpsimd", V(mprev.ap[:, 1:2048], mprev.regs), V(mt.ap[:, 0:2047], mt.regs),
           tag="l0a_mprev")
    k.tt("vector", fs, mprev, mt, ALU.subtract, tag="l0a_d1")
    k.tt("vector", fs, fs, lf, ALU.add, tag="l0a_d1b")
    k.act(fs, fs, AF.Exp, tag="l0a_fs")
    k.tt("vector", isz, li, mt, ALU.subtract, tag="l0a_d2")
    k.act(isz, isz, AF.Exp, tag="l0a_is")
    k.tt("vector", isz, isz, zt, ALU.mult, tag="l0a_isz")
    ct = li
    k.scan(ct, fs, isz, 0.0, ALU.mult, ALU.add, tag="l0a_cscan")
    th = zt
    k.act(th, ct, AF.Tanh, tag="l0a_tanh")
    hbf = V(k.sb["AD"].ap().bitcast(BF16)[0:64, 0:2048], [("AD", 0), ("AD", 1)])
    k.tt("vector", hbf, th, ot, ALU.mult, tag="l0a_h")

    def rhs2(ic, tsl):
        return V(hbf.ap[:, tsl * 512:(tsl + 1) * 512], hbf.regs)

    def evac2(oc, tsl, pv):
        xv = k.f32("AX", oc * 2048 + tsl * 512, 512)
        k.stt("vector", xv, pv, k.pcol(meta["bp"] + oc), xv, ALU.add, ALU.add,
              tag=f"l0a_res{oc}_{tsl}")

    emit_proj(k, meta["wp"], 1, list(range(NCHUNK)), rhs2, evac2, kp=64, tag="l0a_op")


def emit_l0_ffn(k, meta):
    xs = x_chunks(k)
    emit_ln(k, xs, out_bf_base=("AC", 0), xbf_base=("AC", 0), tag="l0f_ln")
    xn = [k.bf("AC", c * 2048, 2048) for c in range(NCHUNK)]
    for c in range(NCHUNK):
        k.dma(k.f32("AD", c * 512, 512), dram_v(meta["fr"][c]), tag=f"l0f_fr{c}")
        k.dma(k.f32("AD", 4096 + c * 512, 512), dram_v(meta["fi"][c]), tag=f"l0f_fi{c}")

    def fr_rhs(c, sl):
        v = k.bf("AD", c * 1024, 1024)
        return V(v.ap[:, sl * 512:(sl + 1) * 512], v.regs)

    def fi_rhs(c, sl):
        v = k.bf("AD", 8192 + c * 1024, 1024)
        return V(v.ap[:, sl * 512:(sl + 1) * 512], v.regs)

    def pt_pass(wa, wb, negb, dst_bfbase, tag):
        bi = 0
        for ob in range(NCHUNK):
            wta = emit_stream_w(k, meta[wa], ob, 0, 8, bi % 2, tag=tag + "a")
            bi += 1
            wtb = emit_stream_w(k, meta[wb], ob, 0, 8, bi % 2, neg=negb, tag=tag + "b")
            bi += 1
            for sl in range(2):
                pv = k.psum(4 * (ob % 2) + sl)
                for ic in range(8):
                    k.mm(pv, wta[ic], fr_rhs(ic, sl), start=(ic == 0), stop=False,
                         tag=f"{tag}_ma{ob}_{sl}_{ic}")
                for ic in range(8):
                    k.mm(pv, wtb[ic], fi_rhs(ic, sl), start=False, stop=(ic == 7),
                         tag=f"{tag}_mb{ob}_{sl}_{ic}")
                dst = k.bf("AB", dst_bfbase + ob * 1024 + sl * 512, 512)
                k.copy("vector", dst, pv, tag=f"{tag}_ev{ob}_{sl}")

    pt_pass("wr", "wi", True, 0, "l0f_prt")
    pt = [k.bf("AB", ob * 1024, 1024) for ob in range(NCHUNK)]

    def m_pass(f_rhs, first, tag):
        for db in range(NCHUNK):
            for sl in range(2):
                pv = k.psum(4 * (db % 2) + sl)
                for oc in range(NCHUNK):
                    lhs = V(pt[oc].ap[:, db * 128:(db + 1) * 128], pt[oc].regs)
                    k.mm(pv, lhs, f_rhs(oc, sl), start=(oc == 0), stop=(oc == 7),
                         tag=f"{tag}_{db}_{sl}_{oc}")
                dst = k.f32("AD", 8192 + db * 1024 + sl * 512, 512)
                if first:
                    k.ts("vector", dst, pv, 1.0 / 1024.0, None, ALU.mult,
                         tag=f"{tag}_ev{db}_{sl}")
                else:
                    k.stt("vector", dst, pv, 1.0 / 1024.0, dst, ALU.mult, ALU.add,
                          tag=f"{tag}_ev{db}_{sl}")

    m_pass(fr_rhs, True, "l0f_m1")
    pt_pass("wi", "wr", False, 0, "l0f_pit")
    m_pass(fi_rhs, False, "l0f_m2")
    m_bf = []
    for db in range(NCHUNK):
        src_v = k.f32("AD", 8192 + db * 1024, 1024)
        dst = k.bf("AB", db * 1024, 1024)
        k.copy(k.alt(), dst, src_v, tag=f"l0f_mcv{db}")
        m_bf.append(dst)
    for ob in range(NCHUNK):
        for tsl in range(NTSL):
            pv = k.psum(4 * (ob % 2) + tsl % 4)
            for dc in range(NCHUNK):
                lhs = V(m_bf[dc].ap[:, ob * 128:(ob + 1) * 128], m_bf[dc].regs)
                r = V(xn[dc].ap[:, tsl * 512:(tsl + 1) * 512], xn[dc].regs)
                k.mm(pv, lhs, r, start=(dc == 0), stop=(dc == 7),
                     tag=f"l0f_y{ob}_{tsl}_{dc}")
            xv = k.f32("AX", ob * 2048 + tsl * 512, 512)
            k.stt("vector", xv, pv, k.pcol(meta["bias"] + ob), xv, ALU.add, ALU.add,
                  tag=f"l0f_res{ob}_{tsl}")


def prep_l0_ffn(host, pc, inp):
    wr = np.asarray(inp["L0_ffn_w_freq_real"], np.float32)
    wi = np.asarray(inp["L0_ffn_w_freq_imag"], np.float32)
    host.add("l0f_wr", pack_lhsT(wr))
    host.add("l0f_wi", pack_lhsT(wi))
    N = 1024
    jk = np.outer(np.arange(N, dtype=np.float64), np.arange(N, dtype=np.float64))
    ang = 2.0 * np.pi * (jk % N) / N
    Fr = np.cos(ang).astype(np.float32)
    Fi = (-np.sin(ang)).astype(np.float32)
    frp = np.stack([pack_bf16_as_f32(Fr[c * 128:(c + 1) * 128, :]) for c in range(8)])
    fip = np.stack([pack_bf16_as_f32(Fi[c * 128:(c + 1) * 128, :]) for c in range(8)])
    host.add("l0f_frb", frp)
    host.add("l0f_fib", fip)
    return {
        "wr": "l0f_wr", "wi": "l0f_wi", "fr": "l0f_frb", "fi": "l0f_fib",
        "bias": pc.add_vecD(inp["L0_ffn_bias"]),
    }


def emit_spill_x(k, tag="spill"):
    for c in range(NCHUNK):
        k.dma(V(k.xsp[c * 128:(c + 1) * 128, :], [("dram", "xsp")]),
              k.f32("AX", c * 2048, 2048), ch="gpsimd", tag=f"{tag}{c}")


def emit_restore_x_to_AD(k, tag="rest"):
    for c in range(NCHUNK):
        k.dma(k.f32("AD", c * 2048, 2048),
              V(k.xsp[c * 128:(c + 1) * 128, :], [("dram", "xsp")]), tag=f"{tag}{c}")


def emit_copy_AD_to_AX(k, tag="adax"):
    for c in range(NCHUNK):
        k.copy(k.alt(), k.f32("AX", c * 2048, 2048), k.f32("AD", c * 2048, 2048),
               tag=f"{tag}{c}")


def emit_l1_attn(k, meta):
    xs = x_chunks(k)
    emit_ln(k, xs, out_bf_base=("AC", 0), xbf_base=("AC", 0), tag="l1a_ln")
    xn = [k.bf("AC", c * 2048, 2048) for c in range(NCHUNK)]
    emit_spill_x(k, tag="l1a_sp")

    def rhs_xn(ic, tsl):
        return V(xn[ic].ap[:, tsl * 512:(tsl + 1) * 512], xn[ic].regs)

    def evq(oc, tsl, pv):
        dst = k.bf("AD", oc * 2048 + tsl * 512, 512)
        k.copy("vector", dst, pv, tag=f"l1a_evq{oc}_{tsl}")

    def evk(oc, tsl, pv):
        dst = k.bf("AD", 16384 + oc * 2048 + tsl * 512, 512)
        k.copy("vector", dst, pv, tag=f"l1a_evk{oc}_{tsl}")

    emit_proj(k, meta["wq"], 8, list(range(8)), rhs_xn, evq, tag="l1a_q")
    emit_proj(k, meta["wk"], 8, list(range(8)), rhs_xn, evk, tag="l1a_k")
    wv_t = []
    for ic in range(8):
        stage = k.f32("AW", 2048 + (ic % 2) * 1024, 1024)
        k.dma(stage, dram_v(meta["wv"][0, :, ic * 1024:(ic + 1) * 1024]),
              tag=f"l1a_wvld{ic}")
        dst = k.bf("AB", ic * 1024, 1024)
        k.copy(k.alt(), dst, stage, tag=f"l1a_wvcv{ic}")
        wv_t.append(dst)
    for tt in range(16):
        for eh in range(2):
            pv = k.psum(4 + (tt % 2) * 2 + eh)
            for ic in range(8):
                lhs = V(xn[ic].ap[:, tt * 128:(tt + 1) * 128], xn[ic].regs)
                r = V(wv_t[ic].ap[:, eh * 512:(eh + 1) * 512], wv_t[ic].regs)
                k.mm(pv, lhs, r, start=(ic == 0), stop=(ic == 7),
                     tag=f"l1a_v{tt}_{eh}_{ic}")
            dst = k.bf("AX", tt * 1024 + eh * 512, 512)
            k.copy("vector", dst, pv, tag=f"l1a_evv{tt}_{eh}")
    v_tok = [k.bf("AX", tt * 1024, 1024) for tt in range(16)]

    for c in range(NCHUNK):
        cs = k.f32("AB", 0, 2048)
        ss = k.f32("AB", 2048, 2048)
        k.dma(cs, dram_v(meta["cs"][c * 128:(c + 1) * 128, :]), tag=f"l1a_cs{c}")
        k.dma(ss, dram_v(meta["ss"][c * 128:(c + 1) * 128, :]), tag=f"l1a_ss{c}")
        for base, nm in ((0, "q"), (16384, "k")):
            t = k.bf("AD", base + c * 2048, 2048)
            tmp = k.bf("AW", 8192, 2048)
            for sl in range(4):
                pv = k.psum(sl % 2)
                k.mm(pv, V(k.swap_bf, [("consts",)]),
                     V(t.ap[:, sl * 512:(sl + 1) * 512], t.regs),
                     start=True, stop=True, tag=f"l1a_sw{nm}{c}_{sl}")
                k.tt("vector", V(tmp.ap[:, sl * 512:(sl + 1) * 512], tmp.regs), pv,
                     V(ss.ap[:, sl * 512:(sl + 1) * 512], ss.regs), ALU.mult,
                     tag=f"l1a_swm{nm}{c}_{sl}")
            k.tt("vector", t, t, cs, ALU.mult, tag=f"l1a_rc{nm}{c}")
            k.tt("vector", t, t, tmp, ALU.add, tag=f"l1a_ra{nm}{c}")
    q_c = [k.bf("AD", c * 2048, 2048) for c in range(NCHUNK)]
    k_c = [k.bf("AD", 16384 + c * 2048, 2048) for c in range(NCHUNK)]

    for qt in range(16):
        L = (qt + 1) * 128
        nkb = qt // 4 + 1
        scores = k.f32("AB", 0, 2048)
        for kb in range(nkb):
            pv = k.psum(kb % 4)
            for ec in range(8):
                lhs = V(q_c[ec].ap[:, qt * 128:(qt + 1) * 128], q_c[ec].regs)
                r = V(k_c[ec].ap[:, kb * 512:(kb + 1) * 512], k_c[ec].regs)
                k.mm(pv, lhs, r, start=(ec == 0), stop=(ec == 7),
                     tag=f"l1a_s{qt}_{kb}_{ec}")
            full = (qt % 4) * 128 if kb == qt // 4 else 512
            if full > 0:
                k.copy("vector", V(scores.ap[:, kb * 512:kb * 512 + full], scores.regs),
                       V(pv.ap[:, 0:full], pv.regs), tag=f"l1a_se{qt}_{kb}")
            if kb == qt // 4:
                dcol = (qt % 4) * 128
                k.tt("vector", V(scores.ap[:, kb * 512 + dcol:kb * 512 + dcol + 128], scores.regs),
                     V(pv.ap[:, dcol:dcol + 128], pv.regs),
                     V(k.causal_mask, [("consts",)]), ALU.add, tag=f"l1a_sd{qt}")
        rmax = k.scol(0)
        nb = k.scol(1)
        rsum = k.scol(2)
        rrec = k.scol(3)
        k.reduce("vector", rmax, V(scores.ap[:, 0:L], scores.regs), ALU.max,
                 tag=f"l1a_rm{qt}")
        k.ts("vector", nb, rmax, -1.0 / 32.0, None, ALU.mult, tag=f"l1a_nb{qt}")
        ef = k.f32("AW", 0, 2048)
        efL = V(ef.ap[:, 0:L], ef.regs)
        k.p.op("scalar", lambda e, o=efL.ap, x=scores.ap[:, 0:L], b=nb.ap:
               e.activation(o, x, AF.Exp, bias=b, scale=1.0 / 32.0),
               reads=scores.regs + nb.regs + [("consts",)], writes=efL.regs,
               tag=f"l1a_exp{qt}")
        k.reduce("vector", rsum, efL, ALU.add, tag=f"l1a_rs{qt}")
        k.recip(rrec, rsum, tag=f"l1a_rc{qt}")
        pbf = k.bf("AW", 8192, 2048)
        pL = V(pbf.ap[:, 0:L], pbf.regs)
        k.ts("vector", pL, efL, rrec, None, ALU.mult, tag=f"l1a_pn{qt}")
        if DBG_SINK and qt == DBG_SINK.get("qt", -1):
            k.dma(dram_v(DBG_SINK["scores"]), scores, ch="gpsimd", tag="dbg_sc")
            k.dma(dram_v(DBG_SINK["ef"]), ef, ch="gpsimd", tag="dbg_ef")
            k.dma(dram_v(DBG_SINK["pbf"]), k.f32("AW", 4096, 1024), ch="gpsimd", tag="dbg_p")
            k.dma(dram_v(DBG_SINK["stats"]), k.scol(0, 8), ch="gpsimd", tag="dbg_st")
        ptq = k.bf("AW", 10240, 2048)
        for kt in range(qt + 1):
            pvt = k.psum_bf(6 + (kt % 2), 0, 128)
            k.transpose(pvt, V(pbf.ap[:, kt * 128:(kt + 1) * 128], pbf.regs),
                        tag=f"l1a_tr{qt}_{kt}")
            k.copy("vector", V(ptq.ap[:, kt * 128:(kt + 1) * 128], ptq.regs), pvt,
                   tag=f"l1a_tre{qt}_{kt}")
        for do in range(8):
            pv = k.psum(do % 6)
            pv = V(pv.ap[:, 0:128], pv.regs)
            for kt in range(qt + 1):
                lhs = V(v_tok[kt].ap[:, do * 128:(do + 1) * 128], v_tok[kt].regs)
                r = V(ptq.ap[:, kt * 128:(kt + 1) * 128], ptq.regs)
                k.mm(pv, lhs, r, start=(kt == 0), stop=(kt == qt),
                     tag=f"l1a_o{qt}_{do}_{kt}")
            ob = k.bf("AC", do * 2048 + qt * 128, 128)
            k.copy("vector", ob, pv, tag=f"l1a_oe{qt}_{do}")

    emit_restore_x_to_AD(k, tag="l1a_rs")
    for c in range(NCHUNK):
        k.tt(k.alt(), k.f32("AX", c * 2048, 2048), k.f32("AD", c * 2048, 2048),
             k.bf("AC", c * 2048, 2048), ALU.add, tag=f"l1a_fin{c}")


def prep_l1_attn(host, pc, inp):
    host.add("l1a_wq", pack_lhsT(np.asarray(inp["L1_attn_w_q"]).T))
    host.add("l1a_wk", pack_lhsT(np.asarray(inp["L1_attn_w_k"]).T))
    host.add("l1a_wv", pack_lhsT(np.asarray(inp["L1_attn_w_v"]).T, ow=1024))
    d = 1024
    i = np.arange(d // 2, dtype=np.float64)
    theta = np.power(10000.0, -2.0 * (i - 1.0) / d)
    pos = np.arange(T, dtype=np.float64)
    ang = pos[None, :] * theta[:, None]
    cos, sin = np.cos(ang), np.sin(ang)
    cs = np.zeros((d, T), np.float32)
    ss = np.zeros((d, T), np.float32)
    cs[0::2, :] = cos
    cs[1::2, :] = cos
    ss[0::2, :] = sin
    ss[1::2, :] = -sin
    host.add("l1a_cs", cs)
    host.add("l1a_ss", ss)
    return {"wq": "l1a_wq", "wk": "l1a_wk", "wv": "l1a_wv",
            "cs": "l1a_cs", "ss": "l1a_ss"}


def emit_l1_ffn(k, meta):
    xs = x_chunks(k)
    emit_ln(k, xs, out_bf_base=("AC", 0), xbf_base=("AC", 0), tag="l1f_ln")
    xn = [k.bf("AC", c * 2048, 2048) for c in range(NCHUNK)]
    k.load_bp(meta["bp"], 136, tag="l1f_bp")
    for c in range(NCHUNK):
        k.copy(k.alt(), k.f32("AD", c * 2048, 2048), k.f32("AX", c * 2048, 2048),
               tag=f"l1f_xcp{c}")
    sc = {"w1": 0, "w2": 32, "h1": 64, "h2": 96, "w3": 128}
    for tq in range(4):
        t0 = tq * 512
        hA = [k.bf("AX", oc * 512, 512) for oc in range(32)]
        hB = [k.bf("AX", 16384 + oc * 512, 512) for oc in range(32)]
        bi = 0
        for oc in range(32):
            wtA = emit_stream_w(k, meta["w1"], oc, 0, 8, bi % 2, tag="l1f_w1")
            bi += 1
            wtB = emit_stream_w(k, meta["w2"], oc, 0, 8, bi % 2, tag="l1f_w2")
            bi += 1
            pA = k.psum((oc % 2) * 2)
            pB = k.psum((oc % 2) * 2 + 1)
            for ic in range(8):
                r = V(xn[ic].ap[:, t0:t0 + 512], xn[ic].regs)
                k.mm(pA, wtA[ic], r, start=(ic == 0), stop=(ic == 7),
                     tag=f"l1f_mA{tq}_{oc}_{ic}")
            for ic in range(8):
                r = V(xn[ic].ap[:, t0:t0 + 512], xn[ic].regs)
                k.mm(pB, wtB[ic], r, start=(ic == 0), stop=(ic == 7),
                     tag=f"l1f_mB{tq}_{oc}_{ic}")
            a = k.f32("AB", (oc % 2) * 512, 512)
            k.act(a, pA, AF.Silu, scale=V(k.bpcol(sc["w1"] + oc), [("BP",)]),
                  tag=f"l1f_silu{tq}_{oc}")
            k.stt("vector", hA[oc], pB, V(k.bpcol(sc["w2"] + oc), [("BP",)]), a,
                  ALU.mult, ALU.mult, tag=f"l1f_h{tq}_{oc}")

        def mk_rhs(buf):
            def rhs(ic, tsl):
                return V(buf[ic].ap, buf[ic].regs)
            return rhs

        def mk_gelu_evac(dstbuf, colbase, tag):
            def evac(oc, tsl, pv):
                g = k.f32("AB", (oc % 2) * 512, 512)
                k.act(g, pv, AF.Gelu, scale=V(k.bpcol(colbase + oc), [("BP",)]),
                      tag=f"{tag}_g{oc}")
                k.copy(k.alt(), dstbuf[oc], g, tag=f"{tag}_c{oc}")
            return evac

        emit_proj(k, meta["h1"], 32, list(range(32)), mk_rhs(hA),
                  mk_gelu_evac(hB, sc["h1"], f"l1f_s2_{tq}"),
                  psum_sets=((4,), (5,), (6,), (7,)), ntsl=1, tag=f"l1f_h1_{tq}")
        emit_proj(k, meta["h2"], 32, list(range(32)), mk_rhs(hB),
                  mk_gelu_evac(hA, sc["h2"], f"l1f_s3_{tq}"),
                  psum_sets=((4,), (5,), (6,), (7,)), ntsl=1, tag=f"l1f_h2_{tq}")

        def evac_w3(oc, tsl, pv):
            acc = k.f32("AD", oc * 2048 + t0, 512)
            k.stt("vector", acc, pv, V(k.bpcol(sc["w3"] + oc), [("BP",)]), acc,
                  ALU.mult, ALU.add, tag=f"l1f_w3e{tq}_{oc}")

        emit_proj(k, meta["w3"], 32, list(range(8)), mk_rhs(hA), evac_w3,
                  psum_sets=((4,), (5,), (6,), (7,)), ntsl=1, tag=f"l1f_w3_{tq}")
    emit_copy_AD_to_AX(k, tag="l1f_fin")


def prep_l1_ffn(host, pc, inp):
    host.add("l1f_w1", pack_lhsT(np.asarray(inp["L1_ffn_w1_w"]).T))
    host.add("l1f_w2", pack_lhsT(np.asarray(inp["L1_ffn_w2_w"]).T))
    host.add("l1f_h1", pack_lhsT(np.asarray(inp["L1_ffn_h1_w"]).T))
    host.add("l1f_h2", pack_lhsT(np.asarray(inp["L1_ffn_h2_w"]).T))
    host.add("l1f_w3", pack_lhsT(np.asarray(inp["L1_ffn_w3_w"]).T))
    bp = np.zeros((128, 136), np.float32)
    for j, nm in enumerate(["w1_s", "w2_s", "h1_s", "h2_s"]):
        v = np.asarray(inp[f"L1_ffn_{nm}"], np.float32)
        for c in range(32):
            bp[:, j * 32 + c] = v[c * 128:(c + 1) * 128]
    v = np.asarray(inp["L1_ffn_w3_s"], np.float32)
    for c in range(8):
        bp[:, 128 + c] = v[c * 128:(c + 1) * 128]
    host.add("l1f_bpa", bp)
    return {"w1": "l1f_w1", "w2": "l1f_w2", "h1": "l1f_h1", "h2": "l1f_h2",
            "w3": "l1f_w3", "bp": "l1f_bpa"}


def emit_l2_attn(k, meta):
    xs = x_chunks(k)
    emit_ln(k, xs, out_f32_base=("AD", 0), xbf_base=("AC", 0), tag="l2a_ln")
    xn = [k.f32("AD", c * 2048, 2048) for c in range(NCHUNK)]
    emit_spill_x(k, tag="l2a_sp")
    mixdst = [("AC", 0, meta["tmk"]), ("AX", 0, meta["tmv"]), ("AX", 16384, meta["tmr"])]
    for c in range(NCHUNK):
        tmp = k.f32("AB", 0, 2047)
        k.tt("vector", tmp, V(xn[c].ap[:, 1:2048], xn[c].regs),
             V(xn[c].ap[:, 0:2047], xn[c].regs), ALU.subtract, tag=f"l2a_d{c}")
        for (ar, base, tmcol) in mixdst:
            dst = k.bf(ar, base + c * 2048, 2048)
            k.stt(k.alt(), V(dst.ap[:, 1:2048], dst.regs), tmp, k.pcol(tmcol + c),
                  V(xn[c].ap[:, 0:2047], xn[c].regs), ALU.mult, ALU.add,
                  tag=f"l2a_m{ar}{base}{c}")
            k.ts(k.alt(), V(dst.ap[:, 0:1], dst.regs), V(xn[c].ap[:, 0:1], xn[c].regs),
                 k.pcol(tmcol + c), None, ALU.mult, tag=f"l2a_m0{ar}{base}{c}")
    mix_k = [k.bf("AC", c * 2048, 2048) for c in range(NCHUNK)]
    mix_v = [k.bf("AX", c * 2048, 2048) for c in range(NCHUNK)]
    mix_r = [k.bf("AX", 16384 + c * 2048, 2048) for c in range(NCHUNK)]

    def mk_rhs(buf):
        def rhs(ic, tsl):
            return V(buf[ic].ap[:, tsl * 512:(tsl + 1) * 512], buf[ic].regs)
        return rhs

    def mk_ev(ar, base, tag):
        def evac(oc, tsl, pv):
            k.copy("vector", k.bf(ar, base + oc * 2048 + tsl * 512, 512), pv,
                   tag=f"{tag}{oc}_{tsl}")
        return evac

    emit_proj(k, meta["wk"], 8, list(range(8)), mk_rhs(mix_k), mk_ev("AD", 0, "l2a_evk"),
              tag="l2a_k")
    emit_proj(k, meta["wv"], 8, list(range(8)), mk_rhs(mix_v), mk_ev("AD", 16384, "l2a_evv"),
              tag="l2a_v")
    emit_proj(k, meta["wr"], 8, list(range(8)), mk_rhs(mix_r), mk_ev("AX", 0, "l2a_evr"),
              tag="l2a_r")
    k_c = [k.bf("AD", c * 2048, 2048) for c in range(NCHUNK)]
    v_c = [k.bf("AD", 16384 + c * 2048, 2048) for c in range(NCHUNK)]
    r_c = [k.bf("AX", c * 2048, 2048) for c in range(NCHUNK)]

    for c in range(NCHUNK):
        dec = k.scol(8 + c)
        eU = k.scol(16 + c)
        k.act(dec, V(k.pcol(meta["W"] + c), [("params",)]), AF.Exp, tag=f"l2a_w{c}")
        k.act(dec, dec, AF.Exp, scale=-1.0, tag=f"l2a_dec{c}")
        k.act(eU, V(k.pcol(meta["U"] + c), [("params",)]), AF.Exp, tag=f"l2a_eu{c}")
    ones_b = k.f32("AW", 0, 2048)
    k.memset("vector", ones_b, 1.0, tag="l2a_ones")

    for c in range(NCHUNK):
        ek = k.f32("AB", 0, 2048)
        ekv = k.f32("AB", 2048, 2048)
        dec_b = k.f32("AW", 2048, 2048)
        Abuf = k.f32("AW", 4096, 2048)
        k.act(ek, k_c[c], AF.Exp, tag=f"l2a_ek{c}")
        k.tt("vector", ekv, ek, v_c[c], ALU.mult, tag=f"l2a_ekv{c}")
        k.ts("vector", dec_b, ones_b, k.scol(8 + c), None, ALU.mult, tag=f"l2a_db{c}")
        k.scan(Abuf, dec_b, ekv, 0.0, ALU.mult, ALU.add, tag=f"l2a_A{c}")
        k.stt("vector", V(ekv.ap[:, 1:2048], ekv.regs), V(ekv.ap[:, 1:2048], ekv.regs),
              k.scol(16 + c), V(Abuf.ap[:, 0:2047], Abuf.regs), ALU.mult, ALU.add,
              tag=f"l2a_num{c}")
        k.ts("vector", V(ekv.ap[:, 0:1], ekv.regs), V(ekv.ap[:, 0:1], ekv.regs),
             k.scol(16 + c), None, ALU.mult, tag=f"l2a_num0{c}")
        k.scan(Abuf, dec_b, ek, 0.0, ALU.mult, ALU.add, tag=f"l2a_B{c}")
        k.stt("vector", V(ek.ap[:, 1:2048], ek.regs), V(ek.ap[:, 1:2048], ek.regs),
              k.scol(16 + c), V(Abuf.ap[:, 0:2047], Abuf.regs), ALU.mult, ALU.add,
              tag=f"l2a_den{c}")
        k.ts("vector", V(ek.ap[:, 0:1], ek.regs), V(ek.ap[:, 0:1], ek.regs),
             k.scol(16 + c), None, ALU.mult, tag=f"l2a_den0{c}")
        k.act(ek, ek, AF.Ln, tag=f"l2a_ln{c}")
        k.act(ek, ek, AF.Exp, scale=-1.0, tag=f"l2a_rden{c}")
        k.tt("vector", ekv, ekv, ek, ALU.mult, tag=f"l2a_y{c}")
        sig = k.f32("AW", 2048, 2048)
        k.act(sig, r_c[c], AF.Sigmoid, tag=f"l2a_sig{c}")
        k.tt("vector", k.bf("AC", c * 2048, 2048), ekv, sig, ALU.mult, tag=f"l2a_rw{c}")
    rw = [k.bf("AC", c * 2048, 2048) for c in range(NCHUNK)]

    emit_restore_x_to_AD(k, tag="l2a_rs")

    def evac_o(oc, tsl, pv):
        xv = k.f32("AD", oc * 2048 + tsl * 512, 512)
        k.tt("vector", xv, pv, xv, ALU.add, tag=f"l2a_oe{oc}_{tsl}")

    emit_proj(k, meta["wo"], 8, list(range(8)), mk_rhs(rw), evac_o, tag="l2a_o")
    emit_copy_AD_to_AX(k, tag="l2a_fin")


def prep_l2_attn(host, pc, inp):
    host.add("l2a_wk", pack_lhsT(np.asarray(inp["L2_attn_Wk"]).T))
    host.add("l2a_wv", pack_lhsT(np.asarray(inp["L2_attn_Wv"]).T))
    host.add("l2a_wr", pack_lhsT(np.asarray(inp["L2_attn_Wr"]).T))
    host.add("l2a_wo", pack_lhsT(np.asarray(inp["L2_attn_Wo"]).T))
    return {"wk": "l2a_wk", "wv": "l2a_wv", "wr": "l2a_wr", "wo": "l2a_wo",
            "W": pc.add_vecD(inp["L2_attn_W"]),
            "U": pc.add_vecD(inp["L2_attn_U"]),
            "tmk": pc.add_vecD(np.asarray(inp["L2_attn_time_mix_k"]).reshape(-1)),
            "tmv": pc.add_vecD(np.asarray(inp["L2_attn_time_mix_v"]).reshape(-1)),
            "tmr": pc.add_vecD(np.asarray(inp["L2_attn_time_mix_r"]).reshape(-1))}


def emit_l2_ffn(k, meta):
    xs = x_chunks(k)
    emit_ln(k, xs, out_f32_base=("AD", 0), out_bf_base=("AC", 0),
            xbf_base=("AC", 0), tag="l2f_ln")
    xn32 = [k.f32("AD", c * 2048, 2048) for c in range(NCHUNK)]
    xn = [k.bf("AC", c * 2048, 2048) for c in range(NCHUNK)]
    k.load_bp(meta["bp"], 256, tag="l2f_bp")
    wgs = k.f32("AW", 2048, 64)
    k.dma(wgs, dram_v(meta["wg"][0]), tag="l2f_wgld")
    glog = k.f32("AB", 0, 128)
    for tt in range(16):
        pv = k.psum(4 + tt % 2)
        pv = V(pv.ap[:, 0:8], pv.regs)
        for ic in range(NCHUNK):
            lhs = V(xn32[ic].ap[:, tt * 128:(tt + 1) * 128], xn32[ic].regs)
            r = V(wgs.ap[:, ic * 8:(ic + 1) * 8], wgs.regs)
            k.mm(pv, lhs, r, start=(ic == 0), stop=(ic == 7), tag=f"l2f_g{tt}_{ic}")
        k.copy("vector", V(glog.ap[:, tt * 8:(tt + 1) * 8], glog.regs), pv,
               tag=f"l2f_ge{tt}")
    for c in range(NCHUNK):
        k.copy(k.alt(), k.f32("AD", c * 2048, 2048), k.f32("AX", c * 2048, 2048),
               tag=f"l2f_xcp{c}")
    f_all = k.f32("AB", 128, 128)
    for tt in range(16):
        g = V(glog.ap[:, tt * 8:(tt + 1) * 8], glog.regs)
        m1, m2, q, rden, w2c = k.scol(24), k.scol(25), k.scol(26), k.scol(27), k.scol(28)
        eq1 = k.f32("AB", 256, 8)
        eq2 = k.f32("AB", 264, 8)
        k.reduce("vector", m1, g, ALU.max, tag=f"l2f_m1{tt}")
        k.ts("vector", eq1, g, m1, None, ALU.is_equal, tag=f"l2f_eq1{tt}")
        k.stt("vector", eq2, eq1, -60000.0, g, ALU.mult, ALU.add, tag=f"l2f_msk{tt}")
        k.reduce("vector", m2, eq2, ALU.max, tag=f"l2f_m2{tt}")
        k.ts("vector", eq2, eq2, m2, None, ALU.is_equal, tag=f"l2f_eq2{tt}")
        k.ts("vector", q, m2, m1, None, ALU.subtract, tag=f"l2f_q{tt}")
        k.act(q, q, AF.Exp, tag=f"l2f_eq{tt}")
        k.ts("vector", rden, q, 1.0, None, ALU.add, tag=f"l2f_d{tt}")
        k.recip(rden, rden, tag=f"l2f_r{tt}")
        k.ts("vector", w2c, q, rden, None, ALU.mult, tag=f"l2f_w2{tt}")
        fv = V(f_all.ap[:, tt * 8:(tt + 1) * 8], f_all.regs)
        k.ts("vector", fv, eq1, rden, None, ALU.mult, tag=f"l2f_f1{tt}")
        k.stt("vector", fv, eq2, w2c, fv, ALU.mult, ALU.add, tag=f"l2f_f2{tt}")
    fbf_all = V(k.sb["AB"].ap().bitcast(BF16)[:, 6656:6784], [("AB", 6)])
    k.copy("vector", fbf_all, f_all, tag="l2f_fcv")
    f_fm_bf = V(k.sb["AW"].ap().bitcast(BF16)[0:8, 10240:12288],
                [("AW", 10), ("AW", 11)])
    for tt in range(16):
        pvt = k.psum_bf(6, 0, 128)
        pvt = V(pvt.ap[0:8, :], pvt.regs)
        k.p.op("tensor", lambda e, o=pvt.ap, s=fbf_all.ap[:, tt * 8:(tt + 1) * 8]:
               e.transpose(o, s, k.ident_bf),
               reads=fbf_all.regs + [("consts",)], writes=pvt.regs, tag=f"l2f_tr{tt}")
        k.copy("vector", V(f_fm_bf.ap[:, tt * 128:(tt + 1) * 128], f_fm_bf.regs), pvt,
               tag=f"l2f_tre{tt}")

    if DBG_SINK.get("l2f"):
        k.dma(dram_v(DBG_SINK["glog"]), k.f32("AB", 0, 128), ch="gpsimd", tag="dbg_gl")
        k.dma(dram_v(DBG_SINK["fall"]), f_all, ch="gpsimd", tag="dbg_fa")
        k.dma(dram_v(DBG_SINK["ffm"]), V(k.sb["AW"].ap()[0:8, 5120:6144], f_fm_bf.regs),
              ch="gpsimd", tag="dbg_ffm")
    for e in range(8):
        frow = V(k.sb["AW"].ap().bitcast(BF16)[0:1, 2048:4096], [("AW", 2), ("AW", 3)])
        for tt in range(16):
            pvt = k.psum_bf(7, 0, 128)
            pvt = V(pvt.ap[0:1, :], pvt.regs)
            k.p.op("tensor", lambda ee, o=pvt.ap, s=fbf_all.ap[:, tt * 8 + e:tt * 8 + e + 1]:
                   ee.transpose(o, s, k.ident_bf),
                   reads=fbf_all.regs + [("consts",)], writes=pvt.regs,
                   tag=f"l2f_fr{e}_{tt}")
            k.copy("vector", V(frow.ap[:, tt * 128:(tt + 1) * 128], frow.regs), pvt,
                   tag=f"l2f_fre{e}_{tt}")
        for th in range(2):
            fbc = k.f32("AW", 0, 1024)
            for sl in range(2):
                pv = k.psum(4)
                lhs = V(k.ones_bf[0:1, :], [("consts",)])
                r = V(frow.ap[:, th * 1024 + sl * 512: th * 1024 + (sl + 1) * 512],
                      frow.regs)
                k.mm(pv, lhs, r, start=True, stop=True, tag=f"l2f_fb{th}_{e}_{sl}")
                k.copy("vector", V(fbc.ap[:, sl * 512:(sl + 1) * 512], fbc.regs), pv,
                       tag=f"l2f_fbe{th}_{e}_{sl}")
            bi = 0
            for oc in range(32):
                wt = emit_stream_w(k, meta["w1"][e], oc, 0, 8, bi % 2, tag="l2f_w1")
                bi += 1
                for sl in range(2):
                    pv = k.psum((oc % 2) * 2 + sl)
                    for ic in range(NCHUNK):
                        r = V(xn[ic].ap[:, th * 1024 + sl * 512:th * 1024 + (sl + 1) * 512],
                              xn[ic].regs)
                        k.mm(pv, wt[ic], r, start=(ic == 0), stop=(ic == 7),
                             tag=f"l2f_u{th}_{e}_{oc}_{sl}_{ic}")
                    a = k.f32("AB", 2304 + (oc % 2) * 512, 512)
                    k.act(a, pv, AF.Relu, bias=V(k.bpcol(e * 32 + oc), [("BP",)]),
                          tag=f"l2f_re{th}_{e}_{oc}_{sl}")
                    hb = k.bf("AX", oc * 1024 + sl * 512, 512)
                    k.tt("vector", hb, a, V(fbc.ap[:, sl * 512:(sl + 1) * 512], fbc.regs),
                         ALU.mult, tag=f"l2f_hf{th}_{e}_{oc}_{sl}")
            h_e = [k.bf("AX", oc * 1024, 1024) for oc in range(32)]
            bi2 = 0
            for ob in range(NCHUNK):
                pbase = (5, 7) if ob % 2 == 0 else (6, 4)
                for g in range(4):
                    wt = emit_stream_w(k, meta["w2"][e], ob, g * 8, 8, bi2 % 2,
                                       tag="l2f_w2")
                    bi2 += 1
                    for sl in range(2):
                        pv = k.psum(pbase[sl])
                        for i in range(8):
                            ic = g * 8 + i
                            r = V(h_e[ic].ap[:, sl * 512:(sl + 1) * 512], h_e[ic].regs)
                            k.mm(pv, wt[i], r, start=(ic == 0), stop=(ic == 31),
                                 tag=f"l2f_d{th}_{e}_{ob}_{sl}_{ic}")
                for sl in range(2):
                    pv = k.psum(pbase[sl])
                    acc = k.f32("AD", ob * 2048 + th * 1024 + sl * 512, 512)
                    k.tt("vector", acc, pv, acc, ALU.add, tag=f"l2f_de{th}_{e}_{ob}_{sl}")
            if DBG_SINK.get("l2f") and e == 0 and th == 1:
                for cdump in range(NCHUNK):
                    k.dma(dram_v(DBG_SINK["acc0"][cdump * 128:(cdump + 1) * 128, :]),
                          k.f32("AD", cdump * 2048, 2048), ch="gpsimd", tag=f"dbg_a{cdump}")
                for hdump in range(8):
                    k.dma(dram_v(DBG_SINK["h0"][hdump * 128:(hdump + 1) * 128, :]),
                          k.f32("AX", hdump * 512, 512), ch="gpsimd", tag=f"dbg_h{hdump}")
    for ob in range(NCHUNK):
        wt = emit_stream_w(k, meta["b2"], ob, 0, 1, ob % 2, ow=128, kp=8, tag="l2f_b2")
        for tsl in range(4):
            pv = k.psum(7)
            r = V(f_fm_bf.ap[:, tsl * 512:(tsl + 1) * 512], f_fm_bf.regs)
            k.mm(pv, wt[0], r, start=True, stop=True, tag=f"l2f_b2m{ob}_{tsl}")
            acc = k.f32("AD", ob * 2048 + tsl * 512, 512)
            k.tt("vector", acc, pv, acc, ALU.add, tag=f"l2f_b2e{ob}_{tsl}")
    emit_copy_AD_to_AX(k, tag="l2f_fin")


def prep_l2_ffn(host, pc, inp):
    W1 = np.asarray(inp["L2_ffn_W1"], np.float32)
    W2 = np.asarray(inp["L2_ffn_W2"], np.float32)
    host.add("l2f_w1", np.stack([pack_lhsT(W1[e]) for e in range(8)]))
    host.add("l2f_w2", np.stack([pack_lhsT(W2[e]) for e in range(8)]))
    host.add("l2f_wg", pack_lhsT(np.asarray(inp["L2_ffn_Wg"], np.float32), ow=8))
    host.add("l2f_b2p", pack_lhsT(np.asarray(inp["L2_ffn_b2"], np.float32)))
    b1 = np.asarray(inp["L2_ffn_b1"], np.float32)
    bp = np.zeros((128, 256), np.float32)
    for e in range(8):
        for c in range(32):
            bp[:, e * 32 + c] = b1[e, c * 128:(c + 1) * 128]
    host.add("l2f_bpa", bp)
    return {"w1": "l2f_w1", "w2": "l2f_w2", "wg": "l2f_wg", "b2": "l2f_b2p",
            "bp": "l2f_bpa"}


L3_BAND = 8


def emit_l3_attn(k, meta):
    xs = x_chunks(k)
    emit_ln(k, xs, out_bf_base=("AC", 0), xbf_base=("AC", 0), tag="l3a_ln")
    xn = [k.bf("AC", c * 2048, 2048) for c in range(NCHUNK)]
    emit_spill_x(k, tag="l3a_sp")

    def rhs_xn(ic, tsl):
        return V(xn[ic].ap[:, tsl * 512:(tsl + 1) * 512], xn[ic].regs)

    gq = V(k.gamma_q(128), [("consts",)])
    gk = V(k.gamma_k(128), [("consts",)])

    def mk_qk_ev(base, bcol, gtile, tag):
        def evac(oc, tsl, pv):
            for sub in range(4):
                dst = k.bf("AD", base + oc * 2048 + tsl * 512 + sub * 128, 128)
                k.stt("vector", dst, V(pv.ap[:, sub * 128:(sub + 1) * 128], pv.regs),
                      k.pcol(bcol + oc), gtile, ALU.add, ALU.mult,
                      tag=f"{tag}{oc}_{tsl}_{sub}")
        return evac

    emit_proj(k, meta["wq"], 8, list(range(8)), rhs_xn,
              mk_qk_ev(0, meta["bq"], gq, "l3a_evq"), tag="l3a_q")
    emit_proj(k, meta["wk"], 8, list(range(8)), rhs_xn,
              mk_qk_ev(16384, meta["bk"], gk, "l3a_evk"), tag="l3a_k")
    wv_t = []
    for ic in range(8):
        stage = k.f32("AW", 2048 + (ic % 2) * 1024, 1024)
        k.dma(stage, dram_v(meta["wv"][0, :, ic * 1024:(ic + 1) * 1024]),
              tag=f"l3a_wvld{ic}")
        dst = k.bf("AB", ic * 1024, 1024)
        k.copy(k.alt(), dst, stage, tag=f"l3a_wvcv{ic}")
        wv_t.append(dst)
    for tt in range(16):
        for eh in range(2):
            pv = k.psum(4 + (tt % 2) * 2 + eh)
            for ic in range(8):
                lhs = V(xn[ic].ap[:, tt * 128:(tt + 1) * 128], xn[ic].regs)
                r = V(wv_t[ic].ap[:, eh * 512:(eh + 1) * 512], wv_t[ic].regs)
                k.mm(pv, lhs, r, start=(ic == 0), stop=(ic == 7),
                     tag=f"l3a_v{tt}_{eh}_{ic}")
            dst = k.bf("AX", tt * 1024 + eh * 512, 512)
            k.copy("vector", dst, pv, tag=f"l3a_evv{tt}_{eh}")
    v_tok = [k.bf("AX", tt * 1024, 1024) for tt in range(16)]
    q_c = [k.bf("AD", c * 2048, 2048) for c in range(NCHUNK)]
    k_c = [k.bf("AD", 16384 + c * 2048, 2048) for c in range(NCHUNK)]

    emit_restore_x_to_AD_late = False
    for qt in range(16):
        kt_lo = max(0, qt - L3_BAND)
        pbf = k.bf("AW", 8192, 2048)
        kb_lo, kb_hi = kt_lo // 4, qt // 4
        for kb in range(kb_lo, kb_hi + 1):
            pv = k.psum(kb % 4)
            for ec in range(8):
                lhs = V(q_c[ec].ap[:, qt * 128:(qt + 1) * 128], q_c[ec].regs)
                r = V(k_c[ec].ap[:, kb * 512:(kb + 1) * 512], k_c[ec].regs)
                k.mm(pv, lhs, r, start=(ec == 0), stop=(ec == 7),
                     tag=f"l3a_s{qt}_{kb}_{ec}")
            for sub in range(4):
                kt = kb * 4 + sub
                if kt < kt_lo or kt > qt:
                    continue
                dlt = qt - kt
                cfac = float(GAMMA ** (128 * dlt))
                dst = V(pbf.ap[:, kt * 128:(kt + 1) * 128], pbf.regs)
                pslice = V(pv.ap[:, sub * 128:(sub + 1) * 128], pv.regs)
                if kt == qt:
                    k.tt("vector", dst, pslice, V(k.caus01, [("consts",)]), ALU.mult,
                         tag=f"l3a_sd{qt}")
                else:
                    k.ts("vector", dst, pslice, cfac, None, ALU.mult,
                         tag=f"l3a_se{qt}_{kt}")
        ptq = k.bf("AW", 10240, 2048)
        for kt in range(kt_lo, qt + 1):
            pvt = k.psum_bf(6 + (kt % 2), 0, 128)
            k.transpose(pvt, V(pbf.ap[:, kt * 128:(kt + 1) * 128], pbf.regs),
                        tag=f"l3a_tr{qt}_{kt}")
            k.copy("vector", V(ptq.ap[:, kt * 128:(kt + 1) * 128], ptq.regs), pvt,
                   tag=f"l3a_tre{qt}_{kt}")
        for do in range(8):
            pv = k.psum(do % 6)
            pv = V(pv.ap[:, 0:128], pv.regs)
            for j, kt in enumerate(range(kt_lo, qt + 1)):
                lhs = V(v_tok[kt].ap[:, do * 128:(do + 1) * 128], v_tok[kt].regs)
                r = V(ptq.ap[:, kt * 128:(kt + 1) * 128], ptq.regs)
                k.mm(pv, lhs, r, start=(j == 0), stop=(kt == qt),
                     tag=f"l3a_o{qt}_{do}_{kt}")
            ob = k.bf("AC", do * 2048 + qt * 128, 128)
            k.copy("vector", ob, pv, tag=f"l3a_oe{qt}_{do}")

    emit_restore_x_to_AD(k, tag="l3a_rs")
    for c in range(NCHUNK):
        k.tt(k.alt(), k.f32("AX", c * 2048, 2048), k.f32("AD", c * 2048, 2048),
             k.bf("AC", c * 2048, 2048), ALU.add, tag=f"l3a_fin{c}")


def prep_l3_attn(host, pc, inp):
    host.add("l3a_wq", pack_lhsT(np.asarray(inp["L3_attn_Wq"]).T))
    host.add("l3a_wk", pack_lhsT(np.asarray(inp["L3_attn_Wk"]).T))
    host.add("l3a_wv", pack_lhsT(np.asarray(inp["L3_attn_Wv"]).T, ow=1024))
    return {"wq": "l3a_wq", "wk": "l3a_wk", "wv": "l3a_wv",
            "bq": pc.add_vecD(inp["L3_attn_bq"]),
            "bk": pc.add_vecD(inp["L3_attn_bk"])}


def emit_l3_ffn(k, meta):
    xs = x_chunks(k)
    emit_ln(k, xs, out_f32_base=("AD", 0), out_bf_base=("AC", 0),
            xbf_base=("AC", 0), tag="l3f_ln")
    xn32 = [k.f32("AD", c * 2048, 2048) for c in range(NCHUNK)]
    xn = [k.bf("AC", c * 2048, 2048) for c in range(NCHUNK)]
    k.load_bp(meta["bp"], 296, tag="l3f_bp")
    gb = V(k.sc_t.ap()[0:1, 40:48], [("SC", 40)])
    k.dma(gb, dram_v(meta["gbias"]), tag="l3f_gb")
    wgs = k.f32("AW", 2048, 64)
    k.dma(wgs, dram_v(meta["wg"][0]), tag="l3f_wgld")
    glog = k.f32("AB", 0, 128)
    for tt in range(16):
        pv = k.psum(4 + tt % 2)
        pv = V(pv.ap[:, 0:8], pv.regs)
        for ic in range(NCHUNK):
            lhs = V(xn32[ic].ap[:, tt * 128:(tt + 1) * 128], xn32[ic].regs)
            r = V(wgs.ap[:, ic * 8:(ic + 1) * 8], wgs.regs)
            k.mm(pv, lhs, r, start=(ic == 0), stop=False, tag=f"l3f_g{tt}_{ic}")
        ones_row = V(k.consts.ap()[0:1, 0:128], [("consts",)])
        k.mm(pv, ones_row, gb, start=False, stop=True, tag=f"l3f_gb{tt}")
        k.act(V(glog.ap[:, tt * 8:(tt + 1) * 8], glog.regs), pv, AF.Sigmoid,
              tag=f"l3f_sig{tt}")
    f_all = k.f32("AB", 128, 128)
    for tt in range(16):
        g = V(glog.ap[:, tt * 8:(tt + 1) * 8], glog.regs)
        m1, m2, den, w1c, w2c = k.scol(24), k.scol(25), k.scol(26), k.scol(27), k.scol(28)
        eq1 = k.f32("AB", 256, 8)
        eq2 = k.f32("AB", 264, 8)
        k.reduce("vector", m1, g, ALU.max, tag=f"l3f_m1{tt}")
        k.ts("vector", eq1, g, m1, None, ALU.is_equal, tag=f"l3f_eq1{tt}")
        k.stt("vector", eq2, eq1, -60000.0, g, ALU.mult, ALU.add, tag=f"l3f_msk{tt}")
        k.reduce("vector", m2, eq2, ALU.max, tag=f"l3f_m2{tt}")
        k.ts("vector", eq2, eq2, m2, None, ALU.is_equal, tag=f"l3f_eq2{tt}")
        k.ts("vector", den, m1, m2, None, ALU.add, tag=f"l3f_d1{tt}")
        k.ts("vector", den, den, 1e-6, None, ALU.add, tag=f"l3f_d2{tt}")
        k.recip(den, den, tag=f"l3f_r{tt}")
        k.ts("vector", w1c, m1, den, None, ALU.mult, tag=f"l3f_w1{tt}")
        k.ts("vector", w2c, m2, den, None, ALU.mult, tag=f"l3f_w2{tt}")
        fv = V(f_all.ap[:, tt * 8:(tt + 1) * 8], f_all.regs)
        k.ts("vector", fv, eq1, w1c, None, ALU.mult, tag=f"l3f_f1{tt}")
        k.stt("vector", fv, eq2, w2c, fv, ALU.mult, ALU.add, tag=f"l3f_f2{tt}")
        k.ts("vector", fv, fv, 0.1, None, ALU.mult, tag=f"l3f_f01{tt}")
    fbf_all = V(k.sb["AB"].ap().bitcast(BF16)[:, 6656:6784], [("AB", 6)])
    k.copy("vector", fbf_all, f_all, tag="l3f_fcv")
    f_fm_bf = V(k.sb["AW"].ap().bitcast(BF16)[0:8, 10240:12288],
                [("AW", 10), ("AW", 11)])
    for tt in range(16):
        pvt = k.psum_bf(6, 0, 128)
        pvt = V(pvt.ap[0:8, :], pvt.regs)
        k.p.op("tensor", lambda e, o=pvt.ap, s=fbf_all.ap[:, tt * 8:(tt + 1) * 8]:
               e.transpose(o, s, k.ident_bf),
               reads=fbf_all.regs + [("consts",)], writes=pvt.regs, tag=f"l3f_tr{tt}")
        k.copy("vector", V(f_fm_bf.ap[:, tt * 128:(tt + 1) * 128], f_fm_bf.regs), pvt,
               tag=f"l3f_tre{tt}")
    emit_spill_x(k, tag="l3f_sp")
    for c in range(NCHUNK):
        k.memset(k.alt(), k.f32("AD", c * 2048, 2048), 0.0, tag=f"l3f_z{c}")
    for c in range(NCHUNK):
        k.ts("vector", k.scol(32 + c), V(k.bpcol(288 + c), [("BP",)]), 0.1, None,
             ALU.mult, tag=f"l3f_sbd{c}")

    def up_pass(wblocks, biasbase, dst_bf_base, rhs, tag):
        bi = 0
        for oc in range(32):
            wt = emit_stream_w(k, wblocks, oc, 0, 8, bi % 2, tag=tag)
            bi += 1
            for sl in range(2):
                pv = k.psum((oc % 2) * 2 + sl)
                for ic in range(NCHUNK):
                    k.mm(pv, wt[ic], rhs(ic, sl), start=(ic == 0), stop=(ic == 7),
                         tag=f"{tag}m{oc}_{sl}_{ic}")
                a = k.f32("AB", 2304 + (oc % 2) * 512, 512)
                k.act(a, pv, AF.Gelu, bias=V(k.bpcol(biasbase + oc), [("BP",)]),
                      tag=f"{tag}g{oc}_{sl}")
                hb = k.bf("AX", dst_bf_base + oc * 1024 + sl * 512, 512)
                k.copy(k.alt(), hb, a, tag=f"{tag}c{oc}_{sl}")

    def down_pass(wblocks, h_bf_base, th, evac, tag):
        h_e = [k.bf("AX", h_bf_base + ic * 1024, 1024) for ic in range(32)]
        bi2 = 0
        for ob in range(NCHUNK):
            pbase = (5, 7) if ob % 2 == 0 else (6, 4)
            for g in range(4):
                wt = emit_stream_w(k, wblocks, ob, g * 8, 8, bi2 % 2, tag=tag)
                bi2 += 1
                for sl in range(2):
                    pv = k.psum(pbase[sl])
                    for i in range(8):
                        ic = g * 8 + i
                        r = V(h_e[ic].ap[:, sl * 512:(sl + 1) * 512], h_e[ic].regs)
                        k.mm(pv, wt[i], r, start=(ic == 0), stop=(ic == 31),
                             tag=f"{tag}m{ob}_{sl}_{ic}")
            for sl in range(2):
                evac(ob, sl, k.psum(pbase[sl]))

    for th in range(2):
        def rhs_th(ic, sl, th=th):
            return V(xn[ic].ap[:, th * 1024 + sl * 512:th * 1024 + (sl + 1) * 512],
                     xn[ic].regs)
        up_pass(meta["swu"], 256, 0, rhs_th, f"l3f_su{th}_")

        def ev_sh(ob, sl, pv, th=th):
            acc = k.f32("AD", ob * 2048 + th * 1024 + sl * 512, 512)
            k.stt("vector", acc, pv, 0.1, acc, ALU.mult, ALU.add,
                  tag=f"l3f_sde{th}_{ob}_{sl}")

        down_pass(meta["swd"], 0, th, ev_sh, f"l3f_sd{th}_")
        for e in range(8):
            frow = V(k.sb["AW"].ap().bitcast(BF16)[0:1, 2048:4096],
                     [("AW", 2), ("AW", 3)])
            if th == 0:
                pass
            for tt in range(16):
                pvt = k.psum_bf(7, 0, 128)
                pvt = V(pvt.ap[0:1, :], pvt.regs)
                k.p.op("tensor", lambda ee, o=pvt.ap,
                       s=fbf_all.ap[:, tt * 8 + e:tt * 8 + e + 1]:
                       ee.transpose(o, s, k.ident_bf),
                       reads=fbf_all.regs + [("consts",)], writes=pvt.regs,
                       tag=f"l3f_fr{th}_{e}_{tt}")
                k.copy("vector", V(frow.ap[:, tt * 128:(tt + 1) * 128], frow.regs), pvt,
                       tag=f"l3f_fre{th}_{e}_{tt}")
            fbc = k.f32("AW", 0, 1024)
            for sl in range(2):
                pv = k.psum(4)
                lhs = V(k.ones_bf[0:1, :], [("consts",)])
                r = V(frow.ap[:, th * 1024 + sl * 512:th * 1024 + (sl + 1) * 512],
                      frow.regs)
                k.mm(pv, lhs, r, start=True, stop=True, tag=f"l3f_fb{th}_{e}_{sl}")
                k.copy("vector", V(fbc.ap[:, sl * 512:(sl + 1) * 512], fbc.regs), pv,
                       tag=f"l3f_fbe{th}_{e}_{sl}")
            up_pass(meta["upw"][e], e * 32, 0, rhs_th, f"l3f_u{th}_{e}_")

            def ev_ex(ob, sl, pv, th=th):
                t1 = k.f32("AB", 2304 + sl * 512, 512)
                k.tt("vector", t1, pv, V(fbc.ap[:, sl * 512:(sl + 1) * 512], fbc.regs),
                     ALU.mult, tag=f"l3f_xf{th}_{e}_{ob}_{sl}")
                acc = k.f32("AD", ob * 2048 + th * 1024 + sl * 512, 512)
                k.tt("vector", acc, t1, acc, ALU.add, tag=f"l3f_xa{th}_{e}_{ob}_{sl}")

            down_pass(meta["dnw"][e], 0, th, ev_ex, f"l3f_d{th}_{e}_")
    for ob in range(NCHUNK):
        wt = emit_stream_w(k, meta["dbp"], ob, 0, 1, ob % 2, ow=128, kp=8, tag="l3f_db")
        for tsl in range(4):
            pv = k.psum(7)
            r = V(f_fm_bf.ap[:, tsl * 512:(tsl + 1) * 512], f_fm_bf.regs)
            k.mm(pv, wt[0], r, start=True, stop=True, tag=f"l3f_dbm{ob}_{tsl}")
            acc = k.f32("AD", ob * 2048 + tsl * 512, 512)
            k.tt("vector", acc, pv, acc, ALU.add, tag=f"l3f_dbe{ob}_{tsl}")
    for ob in range(NCHUNK):
        for tsl in range(4):
            acc = k.f32("AD", ob * 2048 + tsl * 512, 512)
            k.ts("vector", acc, acc, k.scol(32 + ob), None, ALU.add,
                 tag=f"l3f_sba{ob}_{tsl}")
    m_all = k.f32("AB", 0, 2048)
    k.act(m_all, k.f32("AD", 0, 2048), AF.Abs, tag="l3f_ab0")
    for c in range(1, NCHUNK):
        t_abs = k.f32("AW", 4096, 2048)
        k.act(t_abs, k.f32("AD", c * 2048, 2048), AF.Abs, tag=f"l3f_ab{c}")
        k.tt("vector", m_all, m_all, t_abs, ALU.max, tag=f"l3f_am{c}")
    mbf = V(k.sb["AW"].ap().bitcast(BF16)[:, 8192:10240], [("AW", 8), ("AW", 9)])
    k.copy("vector", mbf, m_all, tag="l3f_mbf")
    rrow_bf = V(k.sb["AW"].ap().bitcast(BF16)[0:1, 10240:12288],
                [("AW", 10), ("AW", 11)])
    rcol_bf = V(k.sb["AB"].ap().bitcast(BF16)[:, 6656:6657], [("AB", 6)])
    for tt in range(16):
        pvt = k.psum_bf(6, 0, 128)
        k.transpose(pvt, V(mbf.ap[:, tt * 128:(tt + 1) * 128], mbf.regs),
                    tag=f"l3f_mt{tt}")
        rmax = k.scol(48)
        k.reduce("vector", rmax, pvt, ALU.max, tag=f"l3f_rm{tt}")
        k.ts("vector", rmax, rmax, 1e-6, None, ALU.add, tag=f"l3f_re{tt}")
        k.recip(rmax, rmax, tag=f"l3f_rr{tt}")
        k.copy("vector", rcol_bf, rmax, tag=f"l3f_rc{tt}")
        pv2 = k.psum_bf(7, 0, 128)
        pv2 = V(pv2.ap[0:1, :], pv2.regs)
        k.p.op("tensor", lambda e, o=pv2.ap, s=rcol_bf.ap:
               e.transpose(o, s, k.ident_bf),
               reads=rcol_bf.regs + [("consts",)], writes=pv2.regs, tag=f"l3f_rt{tt}")
        k.copy("vector", V(rrow_bf.ap[:, tt * 128:(tt + 1) * 128], rrow_bf.regs), pv2,
               tag=f"l3f_rre{tt}")
    rb = k.f32("AW", 0, 2048)
    for tsl in range(4):
        pv = k.psum(4)
        k.mm(pv, V(k.ones_bf[0:1, :], [("consts",)]),
             V(rrow_bf.ap[:, tsl * 512:(tsl + 1) * 512], rrow_bf.regs),
             start=True, stop=True, tag=f"l3f_rb{tsl}")
        k.copy("vector", V(rb.ap[:, tsl * 512:(tsl + 1) * 512], rb.regs), pv,
               tag=f"l3f_rbe{tsl}")
    for c in range(NCHUNK):
        xsp = k.f32("AB", 0, 2048)
        k.dma(xsp, V(k.xsp[c * 128:(c + 1) * 128, :], [("dram", "xsp")]),
              tag=f"l3f_xr{c}")
        t = k.f32("AW", 4096, 2048)
        k.tt("vector", t, k.f32("AD", c * 2048, 2048), rb, ALU.mult, tag=f"l3f_n{c}")
        k.tt(k.alt(), k.f32("AX", c * 2048, 2048), t, xsp, ALU.add, tag=f"l3f_o{c}")


def prep_l3_ffn(host, pc, inp):
    upw = np.asarray(inp["L3_ffn_up_w"], np.float32)
    dnw = np.asarray(inp["L3_ffn_down_w"], np.float32)
    host.add("l3f_upw", np.stack([pack_lhsT(upw[e]) for e in range(8)]))
    host.add("l3f_dnw", np.stack([pack_lhsT(dnw[e]) for e in range(8)]))
    host.add("l3f_swu", pack_lhsT(np.asarray(inp["L3_ffn_sw_up"], np.float32)))
    host.add("l3f_swd", pack_lhsT(np.asarray(inp["L3_ffn_sw_down"], np.float32)))
    host.add("l3f_wg", pack_lhsT(np.asarray(inp["L3_ffn_gate_w"], np.float32), ow=8))
    host.add("l3f_dbp", pack_lhsT(np.asarray(inp["L3_ffn_down_b"], np.float32)))
    host.add("l3f_gbias", np.asarray(inp["L3_ffn_bias"], np.float32).reshape(1, 8))
    up_b = np.asarray(inp["L3_ffn_up_b"], np.float32)
    sb_up = np.asarray(inp["L3_ffn_sb_up"], np.float32)
    sb_down = np.asarray(inp["L3_ffn_sb_down"], np.float32)
    bp = np.zeros((128, 296), np.float32)
    for e in range(8):
        for c in range(32):
            bp[:, e * 32 + c] = up_b[e, c * 128:(c + 1) * 128]
    for c in range(32):
        bp[:, 256 + c] = sb_up[c * 128:(c + 1) * 128]
    for c in range(8):
        bp[:, 288 + c] = sb_down[c * 128:(c + 1) * 128]
    host.add("l3f_bpa", bp)
    return {"upw": "l3f_upw", "dnw": "l3f_dnw", "swu": "l3f_swu", "swd": "l3f_swd",
            "wg": "l3f_wg", "dbp": "l3f_dbp", "gbias": "l3f_gbias", "bp": "l3f_bpa"}



def build_slot_program(slots, host_arrays, param_array, want_trace=False):
    nc = bass.Bass("TRN2", target_bir_lowering=False)
    d = {}
    for name, arr in host_arrays.items():
        d[name] = nc.dram_tensor(name, list(arr.shape), F32, kind="ExternalInput").ap()
    d_y = nc.dram_tensor("y_fm", [D, T], F32, kind="ExternalOutput").ap()
    prog = Prog(nc)
    k = Ctx(nc, prog)
    k.xsp = nc.dram_tensor("xsp", [D, T], F32, kind="Internal").ap()
    load_basics_ops(k, d["consts_t"], d["params_t"], param_array.shape[1])
    emit_load_x(k, d["x_fm"])
    for emit_fn, meta in slots:
        emit_fn(k, {key: (d[v] if isinstance(v, str) and v in d else v)
                    for key, v in meta.items()})
    emit_store_x(k, d_y, ch="gpsimd")
    prog.build()
    k.close()
    return nc


def load_basics_ops(k, d_consts, d_params, n_param_cols):
    k.p.op("sync", lambda e: e.dma_start(out=k.consts.ap()[:, :], in_=d_consts),
           writes=[("consts",)], dma=True, tag="ld_consts")
    k.p.op("sync", lambda e, n=n_param_cols: e.dma_start(
        out=k.params_t.ap()[:, 0:n], in_=d_params[:, 0:n]),
        writes=[("params",)], dma=True, tag="ld_params")



def prep_l0_attn(host, pc, inp):
    wifzo = np.concatenate([inp["L0_attn_Wi"], inp["L0_attn_Wf"],
                            inp["L0_attn_Wz"], inp["L0_attn_Wo"]], axis=0)
    host.add("l0a_wifzo", pack_lhsT(wifzo.T, ow=64))
    host.add("l0a_wp", pack_lhsT(np.asarray(inp["L0_attn_Wp"]).T))
    meta = {
        "wifzo": "l0a_wifzo", "wp": "l0a_wp",
        "ln_w": pc.add_vecD(inp["L0_attn_ln_w"]),
        "ln_b": pc.add_vecD(inp["L0_attn_ln_b"]),
        "b_i": pc.add(inp["L0_attn_bi"]),
        "b_f": pc.add(inp["L0_attn_bf"]),
        "b_z": pc.add(inp["L0_attn_bz"]),
        "b_o": pc.add(inp["L0_attn_bo"]),
        "bp": pc.add_vecD(inp["L0_attn_bp"]),
    }
    return meta


SLOT_PREP = {"l0_attn": (prep_l0_attn, emit_l0_attn),
             "l0_ffn": (prep_l0_ffn, emit_l0_ffn),
             "l1_attn": (prep_l1_attn, emit_l1_attn),
             "l1_ffn": (prep_l1_ffn, emit_l1_ffn),
             "l2_attn": (prep_l2_attn, emit_l2_attn),
             "l2_ffn": (prep_l2_ffn, emit_l2_ffn),
             "l3_attn": (prep_l3_attn, emit_l3_attn),
             "l3_ffn": (prep_l3_ffn, emit_l3_ffn)}


def run_slots(slot_names, inputs, x_fm_percore):
    host = Host()
    pc = ParamCols()
    slots = []
    for s in slot_names:
        prep, emit = SLOT_PREP[s]
        meta = prep(host, pc, inputs)
        slots.append((emit, meta))
    param_array = pc.array()
    host.add("consts_t", make_consts())
    host.add("params_t", param_array)
    host.add("x_fm", x_fm_percore[0])
    nc = build_slot_program(slots, host.arrays, param_array)
    in_maps = []
    for core in range(8):
        m = dict(host.arrays)
        m["x_fm"] = x_fm_percore[core]
        in_maps.append(m)
    res = run_bass_kernel_spmd(nc, in_maps, core_ids=list(range(8)))
    return [r["y_fm"] for r in res.results]


ALL_SLOTS = ["l0_attn", "l0_ffn", "l1_attn", "l1_ffn", "l2_attn", "l2_ffn",
             "l3_attn", "l3_ffn"]


def kernel(**inputs):
    x = np.asarray(inputs["L0_attn_x"], np.float32)
    x_fm = [np.ascontiguousarray(x[b].T) for b in range(8)]
    outs = run_slots(ALL_SLOTS, inputs, x_fm)
    return np.stack([np.asarray(o).T for o in outs]).astype(np.float32)
